# version 20
# baseline (speedup 1.0000x reference)
import contextlib

import numpy as np
import ml_dtypes

import concourse.bass as bass
import concourse.mybir as mybir
from concourse.bass_utils import run_bass_kernel_spmd

F32 = mybir.dt.float32
BF16 = mybir.dt.bfloat16
I32 = mybir.dt.int32
AF = mybir.ActivationFunctionType
ALU = mybir.AluOpType

PI_LO = float(np.nextafter(np.float32(np.pi), np.float32(0)))
INV_2PI = float(np.float32(1.0 / (2.0 * np.pi)))
TWO_PI = float(np.float32(2.0 * np.pi))
MAGIC = float(np.float32(2.0 ** 23))
EPS_LN = 1e-5
EPS_RMS = float(np.finfo(np.float32).eps)
BIGNEG = -1e30


class Geom:
    def __init__(self, B=16, S=4096, D=1024, IRmul=4, ISmul=8, NC=8):
        self.B, self.S, self.D, self.NC = B, S, D, NC
        self.HD = 64
        self.H = D // self.HD
        self.PAIRS = self.H // 2
        self.FC = D // 128
        self.BL = B // NC
        self.TL = self.BL * S
        self.NT = self.TL // 128
        self.NBS = S // 128
        self.QW = 512
        self.NQ = S // self.QW
        self.E = 8
        self.IR = IRmul * D
        self.IS = ISmul * D
        self.T = B * S
        self.CAP = max(int(1.25 * self.T / self.E), 1)
        assert self.CAP % 128 == 0
        self.NCT = self.CAP // 128
        self.GS_S = min(2048, self.TL)
        self.GS_E = min(2048, self.CAP)
        self.ECH = self.GS_E
        self.NECH = self.CAP // self.ECH
        self.IQ_S = min(2048, self.IS)
        self.IQ_E = min(2048, self.IR)
        assert self.S % self.QW == 0 and self.D % 128 == 0
        assert self.TL % self.GS_S == 0 and self.CAP % self.GS_E == 0
        assert self.IS % self.IQ_S == 0 and self.IR % self.IQ_E == 0
        assert self.GS_S % 128 == 0 and self.GS_E % 128 == 0
        self.BUFA = max(self.FC * self.S,
                        (self.IQ_S // 128) * self.GS_S,
                        (self.IQ_E // 128) * self.GS_E)
        self.XSLAB = self.FC * max(self.GS_S, self.GS_E)


FULL = Geom()


class K:
    pass


class Sem:
    __slots__ = ("h", "n")

    def __init__(self, h):
        self.h = h
        self.n = 0

    def inc(self, inst, kk=1):
        inst.then_inc(self.h, kk)
        self.n += kk
        return self.n


class Stream:

    def __init__(self, k, name, pub):
        self.k = k
        self.s0 = Sem(k._es.enter_context(k.nc.semaphore(name + "0")))
        self.s1 = Sem(k._es.enter_context(k.nc.semaphore(name + "1")))
        self.pub = pub
        self.i = 0

    def batch(self, eng, emit_fn):
        ssem = self.s0 if self.i % 2 == 0 else self.s1
        n0 = ssem.n
        emit_fn(eng, ssem)
        if ssem.n == n0:
            return self.pub.n
        self.i += 1
        eng.wait_ge(ssem.h, ssem.n)
        eng.sem_inc(self.pub.h, 1)
        self.pub.n += 1
        return self.pub.n

    def flush(self, eng):
        return self.pub.n


def dma(eng, ssem, out, in_, **kw):
    ssem.n += 16
    return eng.dma_start(out=out, in_=in_, **kw).then_inc(ssem.h, 16)


def idma(eng, ssem, **kw):
    ssem.n += 16
    return eng.indirect_dma_start(**kw).then_inc(ssem.h, 16)


def widths(total, step):
    out = []
    c = 0
    while c < total:
        w = min(step, total - c)
        out.append((c, w))
        c += w
    return out


def sec(k, eng, fn):
    getattr(k.block, eng)(fn)


def bA(k, col, width, base=0, rows=128):
    return k.bufA[base:base + rows, col:col + width]


def marks(k):
    return dict(sP=k.sP.n, sV=k.sV.n, sA=k.sA.n, sG=k.sG.n, sY=k.sY.n)


def wait_marks(k, eng, m):
    for name, val in m.items():
        if val > 0:
            eng.wait_ge(getattr(k, name).h, val)


def dchunk(dram_ap, p=128):
    return dram_ap.rearrange("(c p) q -> p c q", p=p)


def schunk(sbuf_slice, q):
    return sbuf_slice.rearrange("p (c q) -> p c q", q=q)


def build(g: Geom):
    nc = bass.Bass("TRN2", target_bir_lowering=False)
    k = K()
    k.nc = nc
    k.g = g
    es = contextlib.ExitStack()
    D, S, FC, TL = g.D, g.S, g.FC, g.TL

    k.x_d = nc.dram_tensor("x", [TL, D], F32, kind="ExternalInput")
    k.wq_d = nc.dram_tensor("wq", [D, D], F32, kind="ExternalInput")
    k.wk_d = nc.dram_tensor("wk", [D, D], F32, kind="ExternalInput")
    k.wv_d = nc.dram_tensor("wv", [D, D], F32, kind="ExternalInput")
    k.wo_d = nc.dram_tensor("wo", [D, D], F32, kind="ExternalInput")
    k.g1_d = nc.dram_tensor("g1", [D, g.HD], F32, kind="ExternalInput")
    k.g2_d = nc.dram_tensor("g2", [g.HD, D], F32, kind="ExternalInput")
    k.gamma_d = nc.dram_tensor("gamma", [1, D], F32, kind="ExternalInput")
    k.beta_d = nc.dram_tensor("beta", [1, D], F32, kind="ExternalInput")
    k.theta_d = nc.dram_tensor("theta", [g.H * g.HD, 1], F32, kind="ExternalInput")
    k.rw_d = nc.dram_tensor("rw", [g.E, D], F32, kind="ExternalInput")
    k.rg_d = nc.dram_tensor("rg", [D, g.IR], F32, kind="ExternalInput")
    k.ru_d = nc.dram_tensor("ru", [D, g.IR], F32, kind="ExternalInput")
    k.rd_d = nc.dram_tensor("rd", [g.IR, D], F32, kind="ExternalInput")
    k.sg_d = nc.dram_tensor("sg", [D, g.IS], F32, kind="ExternalInput")
    k.su_d = nc.dram_tensor("su", [D, g.IS], F32, kind="ExternalInput")
    k.sd_d = nc.dram_tensor("sd", [g.IS, D], F32, kind="ExternalInput")
    k.ltcol_d = nc.dram_tensor("ltcol", [g.NC, 1], F32, kind="ExternalInput")
    k.myrow_d = nc.dram_tensor("myrow", [1, g.E], F32, kind="ExternalInput")
    k.ltrow_d = nc.dram_tensor("ltrow", [1, g.E], F32, kind="ExternalInput")
    k.cb_d = nc.dram_tensor("cb", [128, 1], F32, kind="ExternalInput")
    k.out_d = nc.dram_tensor("out", [TL, D], F32, kind="ExternalOutput")

    k.trig_d = nc.dram_tensor("trig", [g.PAIRS * 128, 2 * S], BF16)
    k.otT_d = nc.dram_tensor("otT", [g.BL * D, S], BF16)
    k.xmid_f_d = nc.dram_tensor("xmid_f", [TL, D], F32)
    k.xmid_b_d = nc.dram_tensor("xmid_b", [TL, D], BF16)
    k.xmT_d = nc.dram_tensor("xmT", [D, TL], BF16)
    k.sg_b_d = nc.dram_tensor("sg_b", [D, g.IS], BF16)
    k.su_b_d = nc.dram_tensor("su_b", [D, g.IS], BF16)
    k.sd_b_d = nc.dram_tensor("sd_b", [g.IS, D], BF16)
    k.rg_b_d = nc.dram_tensor("rg_b", [D, g.IR], BF16)
    k.ru_b_d = nc.dram_tensor("ru_b", [D, g.IR], BF16)
    k.rd_b_d = nc.dram_tensor("rd_b", [g.IR, D], BF16)
    k.xma_d = nc.dram_tensor("xma", [g.NC * TL, D], BF16)
    k.tid_d = nc.dram_tensor("tid", [2 * TL, 1], F32)
    k.tidall_d = nc.dram_tensor("tidall", [g.NC * 2 * TL, 1], F32)
    k.cnt_in_d = nc.dram_tensor("cnt_in", [1, g.E], F32)
    k.cnt_all_d = nc.dram_tensor("cnt_all", [g.NC, g.E], F32)
    k.eout_f_d = nc.dram_tensor("eout_f", [g.CAP, D], F32)
    k.eout_b_d = nc.dram_tensor("eout_b", [g.CAP, D], BF16)
    k.eall_d = nc.dram_tensor("eall", [g.NC * g.CAP, D], BF16)
    k.shr_d = nc.dram_tensor("shr", [TL, D], F32)

    sb = lambda name, shape, dt: es.enter_context(nc.sbuf_tensor(name, shape, dt))
    k.ident_bf = sb("ident_bf", [128, 128], BF16)
    k.ident_f = sb("ident_f", [128, 128], F32)
    k.tri_f = sb("tri_f", [128, 128], F32)
    k.ones_f = sb("ones_f", [128, 512], F32)
    k.ones_bf = sb("ones_bf", [128, 512], BF16)
    k.masks_bf = sb("masks_bf", [128, 4 * g.QW], BF16)
    k.iota8 = sb("iota8", [128, g.E], F32)
    k.e8mi = sb("e8mi", [128, g.E], F32)
    k.logits = sb("logits", [128, g.E * g.NT], F32)
    k.selbuf = sb("selbuf", [128, g.E * g.NT], F32)
    k.rankbuf = sb("rankbuf", [128, g.E * g.NT], F32)
    k.i1c = sb("i1c", [128, g.NT], F32)
    k.i2c = sb("i2c", [128, g.NT], F32)
    k.rw1c = sb("rw1c", [128, g.NT], F32)
    k.rw2c = sb("rw2c", [128, g.NT], F32)
    k.sidx1 = sb("sidx1", [128, g.NT], I32)
    k.sidx2 = sb("sidx2", [128, g.NT], I32)
    k.gidx1 = sb("gidx1", [128, g.NT], I32)
    k.gidx2 = sb("gidx2", [128, g.NT], I32)
    k.rwv1 = sb("rwv1", [128, g.NT], F32)
    k.rwv2 = sb("rwv2", [128, g.NT], F32)
    k.g1_sl = sb("g1_sl", [128, FC * g.HD], BF16)
    k.g2_sb = sb("g2_sb", [g.HD, D], BF16)
    k.rwT_sb = sb("rwT_sb", [128, FC * g.E], BF16)
    k.gam_b = sb("gam_b", [128, D], F32)
    k.bet_b = sb("bet_b", [128, D], F32)
    k.theta_c = sb("theta_c", [128, 1], F32)
    k.tw0 = sb("tw0", [128, D], F32)
    k.tw1 = sb("tw1", [128, D], F32)
    k.tw2 = sb("tw2", [128, D], F32)
    k.gth0 = sb("gth0", [128, D], BF16)
    k.gth1 = sb("gth1", [128, D], BF16)
    k.st0 = sb("st0", [128, 1], F32)
    k.st1 = sb("st1", [128, 1], F32)
    k.st2 = sb("st2", [128, 1], F32)
    k.st3 = sb("st3", [128, 1], F32)
    k.st4 = sb("st4", [128, 1], F32)
    k.epsln_c = sb("epsln_c", [128, 1], F32)
    k.epsrms_c = sb("epsrms_c", [128, 1], F32)
    k.r8a = sb("r8a", [128, g.E], F32)
    k.r8b = sb("r8b", [128, g.E], F32)
    k.r8c = sb("r8c", [128, g.E], F32)
    k.r8d = sb("r8d", [128, g.E], F32)
    k.cnt_sb = sb("cnt_sb", [g.E, max(g.NT, g.E)], F32)
    k.cntT_sb = sb("cntT_sb", [max(g.NT, g.E), g.E], F32)
    k.acnt_sb = sb("acnt_sb", [g.NC, g.E], F32)
    k.colA = sb("colA", [g.NC, 1], F32)
    k.colB = sb("colB", [g.NC, 1], F32)
    k.colC = sb("colC", [g.NC, 1], F32)
    k.colD = sb("colD", [g.NC, 1], F32)
    k.colE = sb("colE", [g.NC, 1], F32)
    k.brow = sb("brow", [128, 8 * g.E], F32)
    k.ricmp = sb("ricmp", [128, g.NCT], F32)
    k.icmpf = sb("icmpf", [128, g.NCT], F32)
    k.icmpw = sb("icmpw", [128, g.NCT], F32)
    k.icmp = sb("icmp", [128, g.NCT], I32)
    k.ftmp = sb("ftmp", [128, 2 * 512], F32)
    k.tabA = sb("tabA", [128, S], BF16)
    k.tabB = sb("tabB", [128, S], BF16)
    k.cbcol = sb("cbcol", [128, 1], F32)
    k.tidcol = sb("tidcol", [128, 8], F32)
    k.tidv = sb("tidv", [128, 2], F32)
    k.tidvi = sb("tidvi", [128, 2], I32)

    k.psBIG1 = es.enter_context(nc.psum_tensor("psBIG1", [128, 1024], F32))
    k.psBIG2 = es.enter_context(nc.psum_tensor("psBIG2", [128, 1024], F32))
    k.psMED1 = es.enter_context(nc.psum_tensor("psMED1", [128, 512], F32))
    k.psMED2 = es.enter_context(nc.psum_tensor("psMED2", [128, 512], F32))
    k.psSM1 = es.enter_context(nc.psum_tensor("psSM1", [128, 128], BF16))
    k.psSM2 = es.enter_context(nc.psum_tensor("psSM2", [128, 128], BF16))

    k.sP = Sem(es.enter_context(nc.semaphore("sP")))
    k.sV = Sem(es.enter_context(nc.semaphore("sV")))
    k.sA = Sem(es.enter_context(nc.semaphore("sA")))
    k.sG = Sem(es.enter_context(nc.semaphore("sG")))
    k.sY = Sem(es.enter_context(nc.semaphore("sY")))
    k._es = es
    k.stS = Stream(k, "qs", k.sY)
    k.stG = Stream(k, "qg", k.sG)
    k.sPC = Sem(es.enter_context(nc.semaphore("sPC")))
    k.ccCnt = Sem(es.enter_context(nc.semaphore("ccCnt")))
    k.ccXg = Sem(es.enter_context(nc.semaphore("ccXg")))
    k.ccEo = Sem(es.enter_context(nc.semaphore("ccEo")))

    k.block = es.enter_context(nc.Block())

    with es:
        emit_setup(k)
        with nc.sbuf_tensor("trig_f", [128, 3 * S], F32) as trig_f, \
             nc.sbuf_tensor("tabGen", [128, 2 * S], BF16) as tabGen:
            k.trig_f = trig_f
            k.tabGen = tabGen
            emit_tables(k)
        k.bufA = es.enter_context(nc.sbuf_tensor("bufA", [128, g.BUFA], BF16))
        with contextlib.ExitStack() as es_a:
            asb = lambda name, shape, dt: es_a.enter_context(nc.sbuf_tensor(name, shape, dt))
            k.qlrA = asb("qlrA", [128, S], BF16)
            k.qlrB = asb("qlrB", [128, S], BF16)
            k.klrA = asb("klrA", [128, S], BF16)
            k.klrB = asb("klrB", [128, S], BF16)
            k.vpr = asb("vpr", [128, S], BF16)
            k.wv_sl = asb("wv_sl", [128, FC * 128], BF16)
            k.stbf = asb("stbf", [128, 2 * g.QW], BF16)
            k.ftqkA = asb("ftqkA", [128, 512], BF16)
            k.ftqkB = asb("ftqkB", [128, 512], BF16)
            k.gxg1 = asb("gxg1", [g.HD, S], BF16)
            k.wo_sb = asb("wo_sb", [128, FC * D], BF16)
            k.otld = asb("otld", [128, 2 * FC * 128], BF16)
            k.o_ev = k.otld
            k.S_f32 = [asb("sf0", [128, g.HD], F32), asb("sf1", [128, g.HD], F32)]
            k.S_bf = [asb("sb0", [128, g.HD], BF16), asb("sb1", [128, g.HD], BF16)]
            k.xres0 = asb("xres0", [128, D], BF16)
            k.ast = asb("ast", [128, 16], F32)
            for b in range(g.BL):
                emit_a1(k, b)
                emit_gxg1(k, b)
                for hp in range(g.PAIRS):
                    emit_pair(k, b, hp)
                emit_a4(k, b)
                emit_ag_xmid(k, b)
        with contextlib.ExitStack() as es_f:
            fsb = lambda name, shape, dt: es_f.enter_context(nc.sbuf_tensor(name, shape, dt))
            k.rsc = fsb("rsc", [128, 160], F32)
            emit_routing(k)
            emit_scatter(k)
            emit_ag_xg(k)
            k.xslab = fsb("xslab", [128, g.XSLAB], BF16)
            k.gth2 = fsb("gth2", [128, D], BF16)
            k.gth3 = fsb("gth3", [128, D], BF16)
            k.wg_sl = fsb("wg_sl", [128, 2 * FC * 128], BF16)
            k.wu_sl = fsb("wu_sl", [128, 2 * FC * 128], BF16)
            k.wd_sl = fsb("wd_sl", [128, 16 * D], BF16)
            k.gidx_jobs = list(range(g.NT))
            emit_glu(k, "shared")
            if k.gidx_jobs:
                emit_gidx(k, list(k.gidx_jobs))
                k.gidx_jobs = []
            emit_glu(k, "expert")
        emit_phase_c(k)
        emit_final(k)
    return nc



def emit_setup(k):
    nc, g = k.nc, k.g
    D, FC = g.D, g.FC

    def pool(gp):
        k.sG.inc(gp.memset(k.ones_f[:, :], 1.0))
        k.sG.inc(gp.memset(k.ones_bf[:, :], 1.0))
        gp.wait_ge(k.sG.h, k.sG.n)
        k.sG.inc(gp.affine_select(k.ident_bf[:, :], k.ones_bf[:, 0:128],
                                  [[1, 128]], ALU.is_equal, 0.0,
                                  base=0, channel_multiplier=-1))
        k.sG.inc(gp.affine_select(k.ident_f[:, :], k.ones_f[:, 0:128],
                                  [[1, 128]], ALU.is_equal, 0.0,
                                  base=0, channel_multiplier=-1))
        k.sG.inc(gp.affine_select(k.tri_f[:, :], k.ones_f[:, 0:128],
                                  [[1, 128]], ALU.is_gt, 0.0,
                                  base=0, channel_multiplier=-1))
        for d in range(4):
            k.sG.inc(gp.affine_select(k.masks_bf[:, d * g.QW:(d + 1) * g.QW],
                                      k.ones_bf[:, 0:g.QW],
                                      [[1, g.QW]], ALU.is_ge, 0.0,
                                      base=-128 * d, channel_multiplier=-1))
        k.sG.inc(gp.iota(k.iota8[:, :], [[1, g.E]], base=0, channel_multiplier=0,
                         allow_small_or_imprecise_dtypes=True))
        k.sG.inc(gp.memset(k.epsln_c[:, :], EPS_LN))
        k.sG.inc(gp.memset(k.epsrms_c[:, :], EPS_RMS))
        k.m_small_w = k.stG.batch(gp, lambda e, ss: [
            dma(e, ss, out=schunk(k.g1_sl[:, :], g.HD), in_=dchunk(k.g1_d.ap())),
            dma(e, ss, out=k.g2_sb[:, :], in_=k.g2_d[:, :]),
            dma(e, ss, out=k.cbcol[:, :], in_=k.cb_d[:, :]),
            dma(e, ss, out=k.gth0[0:g.E, 0:D], in_=k.rw_d[:, :])])

    sec(k, "gpsimd", pool)
    k.m_consts = k.sG.n

    k.precast_jobs = []
    for src, dst in [(k.sg_d, k.sg_b_d), (k.su_d, k.su_b_d), (k.sd_d, k.sd_b_d),
                     (k.rg_d, k.rg_b_d), (k.ru_d, k.ru_b_d), (k.rd_d, k.rd_b_d)]:
        for r0, rw_ in widths(src.shape[0], 512):
            k.precast_jobs.append((src, dst, r0, rw_))

    def sy(s):
        k.m_setup_rows = k.stS.batch(s, lambda e, ss: [
            dma(e, ss, out=k.tw1[0:1, 0:D], in_=k.gamma_d[:, :]),
            dma(e, ss, out=k.brow[0:1, 0:g.E], in_=k.myrow_d[:, :]),
            dma(e, ss, out=k.brow[0:1, g.E:2 * g.E], in_=k.ltrow_d[:, :])])

    sec(k, "sync", sy)

    def pe1(t):
        t.wait_ge(k.sY.h, k.m_setup_rows)
        t.wait_ge(k.sG.h, 2)
        for h0, w in widths(D, 512):
            mm = t.matmul(k.psBIG1[:, h0:h0 + w], k.ones_f[0:1, 0:128],
                          k.tw1[0:1, h0:h0 + w], start=True, stop=True)
        k.sP.inc(mm)

    sec(k, "tensor", pe1)
    m_gb = k.sP.n

    def dv1(v):
        v.wait_ge(k.sP.h, m_gb)
        k.sV.inc(v.tensor_copy(k.gam_b[:, :], k.psBIG1[:, 0:D]))

    sec(k, "vector", dv1)
    m_gamv = k.sV.n

    def sy2(s):
        s.wait_ge(k.sV.h, m_gamv)
        k.m_beta = k.stS.batch(s, lambda e, ss: [
            dma(e, ss, out=k.tw1[0:1, 0:D], in_=k.beta_d[:, :])])

    sec(k, "sync", sy2)

    def pe2(t):
        t.wait_ge(k.sY.h, k.m_beta)
        for h0, w in widths(D, 512):
            mm = t.matmul(k.psBIG1[:, h0:h0 + w], k.ones_f[0:1, 0:128],
                          k.tw1[0:1, h0:h0 + w], start=True, stop=True)
        k.sP.inc(mm)

    sec(k, "tensor", pe2)
    m_bb = k.sP.n

    def dv2(v):
        v.wait_ge(k.sP.h, m_bb)
        k.sV.inc(v.tensor_copy(k.bet_b[:, :], k.psBIG1[:, 0:D]))
        v.wait_ge(k.sG.h, k.m_consts)
        v.wait_ge(k.sV.h, k.sV.n)
        k.sV.inc(v.tensor_scalar(k.e8mi[:, :], k.iota8[:, :], -1.0, float(g.E),
                                 ALU.mult, ALU.add))

    sec(k, "vector", dv2)

    def pool2(gp):
        gp.wait_ge(k.sP.h, m_bb)
        k.m_wo = k.stG.batch(gp, lambda e, ss: [
            dma(e, ss, out=schunk(k.wo_sb[:, :], g.D), in_=dchunk(k.wo_d.ap()))])

    k.pool2 = pool2

    p0 = k.sP.n
    v0 = k.sV.n

    def pe3(t):
        t.wait_ge(k.sG.h, k.m_small_w)
        t.wait_ge(k.sG.h, 3)
        for c in range(FC):
            if c >= 2:
                t.wait_ge(k.sV.h, v0 + (c - 1))
            ps = k.psSM1 if c % 2 == 0 else k.psSM2
            mm = t.transpose(ps[:, 0:g.E], k.gth0[0:g.E, c * 128:(c + 1) * 128],
                             k.ident_bf[0:g.E, 0:g.E])
            k.sP.inc(mm)

    def dv3(v):
        for c in range(FC):
            ps = k.psSM1 if c % 2 == 0 else k.psSM2
            v.wait_ge(k.sP.h, p0 + c + 1)
            k.sV.inc(v.tensor_copy(k.rwT_sb[:, c * g.E:(c + 1) * g.E], ps[:, 0:g.E]))

    sec(k, "tensor", pe3)
    sec(k, "vector", dv3)
    k.m_setup = marks(k)



def emit_tables(k):
    nc, g = k.nc, k.g
    S = g.S

    for hp in range(g.PAIRS):
        m_store_prev = k.sY.n

        def sy(s, hp=hp):
            if hp > 0:
                s.wait_ge(k.sV.h, k.m_tab_thused)
            k.m_th = k.stS.batch(s, lambda e, ss: [
                dma(e, ss, out=k.theta_c[:, :],
                    in_=k.theta_d[hp * 128:(hp + 1) * 128, :])])

        sec(k, "sync", sy)
        m_th = None

        def pool(gp, hp=hp):
            if hp > 0:
                gp.wait_ge(k.sV.h, k.m_tab_angused)
            k.sG.inc(gp.iota(k.trig_f[:, 0:S], [[1, S]], base=0, channel_multiplier=0,
                             allow_small_or_imprecise_dtypes=True))

        sec(k, "gpsimd", pool)
        m_io = k.sG.n
        a0 = k.sA.n

        def dv(v, hp=hp, m_io=m_io, a0=a0):
            A = k.trig_f[:, 0:S]
            U = k.trig_f[:, S:2 * S]
            KK = k.trig_f[:, 2 * S:3 * S]

            def step(inst):
                k.sV.inc(inst)
                v.wait_ge(k.sV.h, k.sV.n)
            v.wait_ge(k.sG.h, m_io)
            v.wait_ge(k.sY.h, k.m_th)
            if a0 > 0:
                v.wait_ge(k.sA.h, a0)
            step(v.tensor_scalar(A, A, k.theta_c[:, :], None, ALU.mult))
            k.m_tab_thused = k.sV.n
            step(v.tensor_scalar(U, A, INV_2PI, MAGIC, ALU.mult, ALU.add))
            step(v.tensor_scalar(KK, U, -MAGIC, None, ALU.add))
            step(v.scalar_tensor_tensor(U, KK, -TWO_PI, A, ALU.mult, ALU.add))
            step(v.tensor_scalar(U, U, PI_LO, None, ALU.min))
            step(v.tensor_scalar(U, U, -PI_LO, None, ALU.max))
            k.m_tab_vsin = k.sV.n
            v.wait_ge(k.sA.h, a0 + 1)
            step(v.tensor_scalar(U, A, INV_2PI, 0.25, ALU.mult, ALU.add))
            step(v.tensor_scalar(U, U, MAGIC, None, ALU.add))
            step(v.tensor_scalar(KK, U, -MAGIC, -0.25, ALU.add, ALU.add))
            step(v.scalar_tensor_tensor(U, KK, -TWO_PI, A, ALU.mult, ALU.add))
            k.m_tab_angused = k.sV.n
            step(v.tensor_scalar(U, U, PI_LO, None, ALU.min))
            step(v.tensor_scalar(U, U, -PI_LO, None, ALU.max))
            k.m_tab_vcos = k.sV.n

        def ac(s, hp=hp, m_store_prev=m_store_prev):
            s.wait_ge(k.sV.h, k.m_tab_vsin)
            if hp > 0:
                s.wait_ge(k.sY.h, m_store_prev)
            k.sA.inc(s.activation(k.tabGen[:, S:2 * S], k.trig_f[:, S:2 * S], AF.Sin))
            s.wait_ge(k.sV.h, k.m_tab_vcos)
            k.sA.inc(s.activation(k.tabGen[:, 0:S], k.trig_f[:, S:2 * S], AF.Sin))

        sec(k, "vector", dv)
        sec(k, "scalar", ac)
        m_sin = k.sA.n

        def sy2(s, hp=hp, m_sin=m_sin):
            s.wait_ge(k.sA.h, m_sin)
            k.stS.batch(s, lambda e, ss: [
                dma(e, ss, out=k.trig_d[hp * 128:(hp + 1) * 128, :], in_=k.tabGen[:, :])])

        sec(k, "sync", sy2)
    k.m_tables_sy = k.sY.n
    k.m_tables_sA = k.sA.n
    k.m_tables_sV = k.sV.n



def emit_a1(k, b):
    nc, g = k.nc, k.g
    S, FC = g.S, g.FC
    if b == 0:
        sec(k, "gpsimd", k.pool2)
        guard = dict(sP=k.m_setup["sP"], sV=k.m_setup["sV"])
    else:
        guard = dict(k.m_seq_end)

    d_base = k.sG.n
    p_base = k.sP.n
    v_base = k.sV.n
    k.m_a1_ld = {}

    def pool(gp, b=b, guard=guard):
        wait_marks(k, gp, guard)
        for nb in range(g.NBS):
            if nb >= 2:
                gp.wait_ge(k.sP.h, p_base + (nb - 1) * FC)
            dst = k.gth0 if nb % 2 == 0 else k.gth1
            k.m_a1_ld[nb] = k.stG.batch(gp, lambda e, ss, dst=dst, nb=nb: [
                dma(e, ss, out=dst[:, :],
                    in_=k.x_d[b * S + nb * 128: b * S + (nb + 1) * 128, :])])

    def pe(t, b=b):
        t.wait_ge(k.sG.h, 3)
        if b == 0:
            t.wait_ge(k.sA.h, k.m_tables_sA)
        for nb in range(g.NBS):
            t.wait_ge(k.sG.h, k.m_a1_ld[nb])
            src = k.gth0 if nb % 2 == 0 else k.gth1
            for c in range(FC):
                idx = nb * FC + c
                ps = k.psSM1 if idx % 2 == 0 else k.psSM2
                if idx >= 2:
                    t.wait_ge(k.sV.h, v_base + idx - 1)
                mm = t.transpose(ps[:, :], src[:, c * 128:(c + 1) * 128], k.ident_bf[:, :])
                k.sP.inc(mm)

    def dv(v, b=b):
        if b == 0:
            v.wait_ge(k.sA.h, k.m_tables_sA)
            v.wait_ge(k.sY.h, k.m_tables_sy)
        for nb in range(g.NBS):
            for c in range(FC):
                idx = nb * FC + c
                ps = k.psSM1 if idx % 2 == 0 else k.psSM2
                v.wait_ge(k.sP.h, p_base + idx + 1)
                k.sV.inc(v.tensor_copy(bA(k, c * S + nb * 128, 128), ps[:, :]))

    sec(k, "gpsimd", pool)
    sec(k, "tensor", pe)
    sec(k, "vector", dv)
    k.m_xbt = marks(k)



def emit_gxg1(k, b):
    nc, g = k.nc, k.g
    S, FC, HD = g.S, g.FC, g.HD
    p_base = k.sP.n
    v_base = k.sV.n
    tiles = widths(S, 512)

    def pe(t, b=b):
        t.wait_ge(k.sV.h, k.m_xbt["sV"])
        t.wait_ge(k.sG.h, k.m_small_w)
        for i, (c0, w) in enumerate(tiles):
            if i >= 2:
                t.wait_ge(k.sV.h, v_base + i - 1)
            ps = k.psMED1 if i % 2 == 0 else k.psMED2
            for c in range(FC):
                mm = t.matmul(ps[0:HD, 0:w], k.g1_sl[:, c * HD:(c + 1) * HD],
                              bA(k, c * S + c0, w), start=(c == 0), stop=(c == FC - 1))
            k.sP.inc(mm)

    def dv(v, b=b):
        for i, (c0, w) in enumerate(tiles):
            ps = k.psMED1 if i % 2 == 0 else k.psMED2
            v.wait_ge(k.sP.h, p_base + i + 1)
            k.sV.inc(v.tensor_copy(k.gxg1[0:HD, c0:c0 + w], ps[0:HD, 0:w]))

    sec(k, "tensor", pe)
    sec(k, "vector", dv)
    k.m_gxg1 = marks(k)



def emit_pair(k, b, hp):
    nc, g = k.nc, k.g
    S, FC, QW = g.S, g.FC, g.QW
    ntiles = S // 512
    m_prev = getattr(k, "m_pair_cons", None)

    def pool(gp, b=b, hp=hp):
        if m_prev is not None:
            wait_marks(k, gp, dict(sP=m_prev["sP"]))
        else:
            wait_marks(k, gp, dict(sP=k.m_xbt["sP"]))
        def em(e, ss):
            for w_d, dst in [(k.wq_d, k.gth0), (k.wk_d, k.gth1),
                             (k.wv_d, k.wv_sl)]:
                dsl = dst[:, 0:FC * 128] if dst is k.wv_sl else dst[:, :]
                dma(e, ss, out=schunk(dsl, 128),
                    in_=dchunk(w_d.ap()[:, hp * 128:(hp + 1) * 128]))
        k.m_pair_w = k.stG.batch(gp, em)
        for src, dst, r0, rw_ in k.precast_jobs[:4]:
            dma(gp, k.sPC, out=dst[r0:r0 + rw_, :], in_=src[r0:r0 + rw_, :])
        del k.precast_jobs[:4]

    sec(k, "gpsimd", pool)

    def syt(s, b=b, hp=hp):
        if m_prev is not None:
            wait_marks(k, s, dict(sV=m_prev["sV"]))
        def em(e, ss):
            for a, tab in [(0, k.tabA), (1, k.tabB)]:
                r = hp * 128 + a * 64
                dma(e, ss, out=tab[0:64, 0:S], in_=k.trig_d[r:r + 64, 0:S])
                dma(e, ss, out=tab[64:128, 0:S], in_=k.trig_d[r:r + 64, S:2 * S])
        k.m_pair_t = k.stS.batch(s, em)

    sec(k, "sync", syt)

    p_base = k.sP.n
    a_base = k.sA.n
    v_base = k.sV.n
    nproj_v = 6 * (2 * ntiles)

    def pe(t, b=b, hp=hp):
        t.wait_ge(k.sG.h, k.m_pair_w)
        t.wait_ge(k.sV.h, k.m_xbt["sV"])
        if m_prev is not None:
            t.wait_ge(k.sV.h, m_prev["sV"])
        else:
            t.wait_ge(k.sV.h, k.m_gxg1["sV"])
        for j in range(2 * ntiles):
            iqk, nt = divmod(j, ntiles)
            wsl = k.gth0 if iqk == 0 else k.gth1
            if j >= 2:
                t.wait_ge(k.sA.h, a_base + j - 1)
            ps = k.psMED1 if j % 2 == 0 else k.psMED2
            for c in range(FC):
                mm = t.matmul(ps[:, 0:512], wsl[:, c * 128:(c + 1) * 128],
                              bA(k, c * S + nt * 512, 512),
                              start=(c == 0), stop=(c == FC - 1))
            k.sP.inc(mm)
        for mt in range(S // 128):
            reg = [k.psBIG1[:, 0:128], k.psBIG1[:, 512:640],
                   k.psBIG2[:, 0:128], k.psBIG2[:, 512:640]][mt % 4]
            if mt >= 4:
                t.wait_ge(k.sV.h, v_base + nproj_v + (mt - 4) + 1)
            for c in range(FC):
                mm = t.matmul(reg, bA(k, c * S + mt * 128, 128),
                              k.wv_sl[:, c * 128:(c + 1) * 128],
                              start=(c == 0), stop=(c == FC - 1))
            k.sP.inc(mm)

    def ac(s, b=b, hp=hp):
        for j in range(2 * ntiles):
            ps = k.psMED1 if j % 2 == 0 else k.psMED2
            ft = k.ftmp[:, 0:512] if j % 2 == 0 else k.ftmp[:, 512:1024]
            s.wait_ge(k.sP.h, p_base + j + 1)
            if j >= 2:
                s.wait_ge(k.sV.h, v_base + 6 * (j - 1))
            k.sA.inc(s.activation(ft, ps[:, 0:512], AF.Silu))

    def dv(v, b=b, hp=hp):
        v.wait_ge(k.sY.h, k.m_pair_t)
        if m_prev is not None:
            v.wait_ge(k.sP.h, m_prev["sP"])
        for j in range(2 * ntiles):
            iqk, nt = divmod(j, ntiles)
            ft = k.ftmp[:, 0:512] if j % 2 == 0 else k.ftmp[:, 512:1024]
            dstA = k.qlrA if iqk == 0 else k.klrA
            dstB = k.qlrB if iqk == 0 else k.klrB
            v.wait_ge(k.sA.h, a_base + j + 1)
            v.wait_ge(k.sV.h, k.sV.n)
            sl = slice(nt * 512, (nt + 1) * 512)
            fA = k.ftqkA[:, :]
            fB = k.ftqkB[:, :]
            k.sV.inc(v.tensor_copy(fA[0:64, :], ft[0:64, :]))
            k.sV.inc(v.tensor_copy(fA[64:128, :], ft[0:64, :]))
            k.sV.inc(v.tensor_copy(fB[0:64, :], ft[64:128, :]))
            k.sV.inc(v.tensor_copy(fB[64:128, :], ft[64:128, :]))
            v.wait_ge(k.sV.h, k.sV.n)
            k.sV.inc(v.tensor_mul(dstA[:, sl], fA, k.tabA[:, sl]))
            k.sV.inc(v.tensor_mul(dstB[:, sl], fB, k.tabB[:, sl]))
        for mt in range(S // 128):
            reg = [k.psBIG1[:, 0:128], k.psBIG1[:, 512:640],
                   k.psBIG2[:, 0:128], k.psBIG2[:, 512:640]][mt % 4]
            v.wait_ge(k.sP.h, p_base + 2 * ntiles + mt + 1)
            k.sV.inc(v.tensor_copy(k.vpr[:, mt * 128:(mt + 1) * 128], reg))

    sec(k, "tensor", pe)
    sec(k, "scalar", ac)
    sec(k, "vector", dv)
    m_proj = marks(k)

    k.m_prev_head_v = m_proj["sV"]
    emit_heads(k, b, hp)

    k.m_pair_cons = marks(k)


def emit_heads(k, b, hp):
    nc, g = k.nc, k.g
    S, QW, NQ = g.S, g.QW, g.NQ
    NB = 2 * NQ
    prev_v = k.m_prev_head_v
    p_start = k.sP.n
    a_start = k.sA.n
    v0 = k.sV.n
    if not hasattr(k, "oev_store_mark"):
        k.oev_store_mark = {}

    def M_MASK(q, d):
        return v0 + 11 * q + 1 + d

    def M_KT(q, d):
        return v0 + 11 * q + 5 + d

    def M_OEV(q):
        return v0 + 11 * q + 9

    def M_SADD(q):
        return v0 + 11 * q + 10

    def M_SBF(q):
        return v0 + 11 * q + 11

    pE, pT, pAV, pS, pO = {}, {}, {}, {}, {}
    mOEV = {}

    def opsum_of(q):
        nt, a = q // 2, q % 2
        bank = k.psBIG1 if nt % 2 == 0 else k.psBIG2
        return bank[0:64, a * QW:(a + 1) * QW]

    for q in range(NB):
        nt, a = q // 2, q % 2
        qlr = k.qlrA if a == 0 else k.qlrB
        klr = k.klrA if a == 0 else k.klrB

        def pe(t, q=q, nt=nt, a=a, qlr=qlr, klr=klr):
            qsl = qlr[:, nt * QW:(nt + 1) * QW]
            opsum = opsum_of(q)
            if q == 0:
                t.wait_ge(k.sV.h, prev_v)
            if nt > 0:
                t.wait_ge(k.sV.h, M_SBF(q - 2))
                if nt >= 2:
                    t.wait_ge(k.sV.h, M_OEV(q - 4))
                mm = t.matmul(opsum, k.S_bf[a][:, :], qsl, start=True, stop=False)
                k.sP.inc(mm)
            for d in range(4):
                mb = 4 * nt + d
                psE = k.psMED1 if d % 2 == 0 else k.psMED2
                if d == 0 and q >= 1:
                    t.wait_ge(k.sV.h, M_MASK(q - 1, 2))
                if d == 1 and q >= 1:
                    t.wait_ge(k.sV.h, M_MASK(q - 1, 3))
                    t.wait_ge(k.sV.h, M_SADD(q - 1))
                if d >= 2:
                    t.wait_ge(k.sV.h, M_MASK(q, d - 2))
                mm = t.matmul(psE[:, 0:QW], klr[:, mb * 128:(mb + 1) * 128],
                              qsl, start=True, stop=True)
                k.sP.inc(mm)
                pE[(q, d)] = k.sP.n
            for d in range(4):
                mb = 4 * nt + d
                stb = k.stbf[:, 0:QW] if d % 2 == 0 else k.stbf[:, QW:2 * QW]
                t.wait_ge(k.sV.h, M_MASK(q, d))
                mm = t.matmul(opsum,
                              k.vpr[:, mb * 128 + a * 64: mb * 128 + a * 64 + 64],
                              stb, start=(nt == 0 and d == 0), stop=(d == 3))
                k.sP.inc(mm)
                pAV[(q, d)] = k.sP.n
            pO[q] = k.sP.n
            for d in range(4):
                mb = 4 * nt + d
                psT = k.psSM1 if d % 2 == 0 else k.psSM2
                if d < 2:
                    if q >= 1:
                        t.wait_ge(k.sV.h, M_KT(q - 1, d + 2))
                else:
                    t.wait_ge(k.sV.h, M_KT(q, d - 2))
                mm = t.transpose(psT[:, :], klr[:, mb * 128:(mb + 1) * 128],
                                 k.ident_bf[:, :])
                k.sP.inc(mm)
                pT[(q, d)] = k.sP.n
                if d >= 1:
                    emit_supd(t, q, nt, a, d - 1)
            emit_supd(t, q, nt, a, 3)

        def emit_supd(t, q, nt, a, d):
            mb = 4 * nt + d
            kts = k.ftqkA[:, 0:128] if d % 2 == 0 else k.ftqkA[:, 128:256]
            if d == 0:
                t.wait_ge(k.sV.h, M_MASK(q, 3))
            t.wait_ge(k.sV.h, M_KT(q, d))
            mm = t.matmul(k.psMED2[:, 0:64], kts,
                          k.vpr[:, mb * 128 + a * 64: mb * 128 + a * 64 + 64],
                          start=(d == 0), stop=(d == 3))
            k.sP.inc(mm)
            pS[(q, d)] = k.sP.n

        def dv(v, q=q, nt=nt, a=a):
            if q == 0:
                v.wait_ge(k.sP.h, p_start)
                v.wait_ge(k.sA.h, a_start)
            for d in range(4):
                psE = k.psMED1 if d % 2 == 0 else k.psMED2
                stb = k.stbf[:, 0:QW] if d % 2 == 0 else k.stbf[:, QW:2 * QW]
                v.wait_ge(k.sP.h, pE[(q, d)])
                if d < 2:
                    if q >= 1:
                        v.wait_ge(k.sP.h, pAV[(q - 1, d + 2)])
                else:
                    v.wait_ge(k.sP.h, pAV[(q, d - 2)])
                k.sV.inc(v.tensor_mul(stb, psE[:, 0:QW],
                                      k.masks_bf[:, d * QW:(d + 1) * QW]))
            for d in range(4):
                psT = k.psSM1 if d % 2 == 0 else k.psSM2
                kts = k.ftqkA[:, 0:128] if d % 2 == 0 else k.ftqkA[:, 128:256]
                v.wait_ge(k.sP.h, pT[(q, d)])
                if d < 2:
                    if q >= 1:
                        v.wait_ge(k.sP.h, pS[(q - 1, d + 2)])
                else:
                    v.wait_ge(k.sP.h, pS[(q, d - 2)])
                k.sV.inc(v.tensor_copy(kts, psT[:, :]))
            slot = a * 2 + nt % 2
            oev = k.o_ev[0:64, slot * QW:(slot + 1) * QW]
            v.wait_ge(k.sP.h, pO[q])
            prevst = k.oev_store_mark.get(slot)
            if prevst:
                v.wait_ge(k.sY.h, prevst)
            k.sV.inc(v.tensor_copy(oev, opsum_of(q)))
            mOEV[q] = k.sV.n
            v.wait_ge(k.sP.h, pS[(q, 3)])
            if nt == 0:
                k.sV.inc(v.tensor_copy(k.S_f32[a][:, :], k.psMED2[:, 0:64]))
            else:
                k.sV.inc(v.tensor_add(k.S_f32[a][:, :], k.S_f32[a][:, :],
                                      k.psMED2[:, 0:64]))
            v.wait_ge(k.sV.h, k.sV.n)
            k.sV.inc(v.tensor_copy(k.S_bf[a][:, :], k.S_f32[a][:, :]))

        def syo(s, q=q, nt=nt, a=a):
            slot = a * 2 + nt % 2
            oev = k.o_ev[0:64, slot * QW:(slot + 1) * QW]
            h = 2 * hp + a
            s.wait_ge(k.sV.h, mOEV[q])
            k.oev_store_mark[slot] = k.stS.batch(s, lambda e, ss: [
                dma(e, ss, out=k.otT_d[b * g.D + h * 64: b * g.D + h * 64 + 64,
                                       nt * QW:(nt + 1) * QW], in_=oev)])

        sec(k, "tensor", pe)
        sec(k, "vector", dv)
        sec(k, "sync", syo)
        assert k.sV.n == v0 + 11 * (q + 1), (k.sV.n, v0, q)
    k.m_prev_head_v = k.sV.n



def emit_a4(k, b):
    nc, g = k.nc, k.g
    S, FC, D, HD = k.g.S, k.g.FC, k.g.D, k.g.HD
    NBS = g.NBS
    m_pair_v = k.m_pair_cons["sV"]
    m_pair_p = k.m_pair_cons["sP"]
    acc = [k.tw0, k.tw1, k.tw2]
    xres = [k.xres0, k.qlrA[:, 0:D]]
    asg = k.ftmp[:, 0:D]
    dumm = k.stbf[:, 0:D]
    ast = k.ast

    def c_s0(y): return ast[:, y % 3: y % 3 + 1]
    def c_s1(y): return ast[:, 3 + y % 3: 4 + y % 3]
    def c_s2(y): return ast[:, 6 + y % 3: 7 + y % 3]
    def c_s3(y): return ast[:, 9 + y % 3: 10 + y % 3]
    def c_rr(y): return ast[:, 12 + y % 2: 13 + y % 2]

    mLD, mXR, mST = {}, {}, {}
    mPW, mPG, mTR, mLG = {}, {}, {}, {}
    mSIG, mSQ1, mSQ2, mSQ3, mQ1, mQ2, mQ3 = {}, {}, {}, {}, {}, {}, {}
    mV6, mV5, mV7, mRC, mSC, mEV, mST1, mST2, mRM = {}, {}, {}, {}, {}, {}, {}, {}, {}

    for r in range(NBS + 7):
        tT, tP, tL = r - 5, r - 2, r - 6

        def pe(t_, r=r, tT=tT, tP=tP, tL=tL, b=b):
            if 0 <= tT < NBS:
                src = k.gth0 if tT % 2 == 0 else k.gth1
                ar_base = k.sA.n
                t_.wait_ge(k.sV.h, mV7[tT])
                for c in range(FC):
                    ps = k.psSM1 if c % 2 == 0 else k.psSM2
                    if c >= 2:
                        t_.wait_ge(k.sA.h, ar_base + (c - 2) + 1)
                    elif tT >= 1:
                        t_.wait_ge(k.sA.h, mEV[tT - 1])
                    mm = t_.transpose(ps[:, :], src[:, c * 128:(c + 1) * 128],
                                      k.ident_bf[:, :])
                    k.sP.inc(mm)
                mTR[tT] = k.sP.n
            if 0 <= tP < NBS:
                half = tP % 2
                t_.wait_ge(k.sY.h, mLD[tP])
                t_.wait_ge(k.sG.h, k.m_wo)
                if tP == 0:
                    t_.wait_ge(k.sV.h, m_pair_v)
                    wait_marks(k, t_, dict(sV=k.m_gxg1["sV"]))
                else:
                    t_.wait_ge(k.sA.h, mSIG[tP - 1])
                    t_.wait_ge(k.sV.h, mV6[tP - 1])
                for h0, w in widths(D, 512):
                    mm = t_.matmul(k.psBIG2[:, h0:h0 + w],
                                   k.gxg1[0:HD, tP * 128:(tP + 1) * 128],
                                   k.g2_sb[0:HD, h0:h0 + w], start=True, stop=True)
                k.sP.inc(mm)
                mPG[tP] = k.sP.n
                for h0, w in widths(D, 512):
                    for c in range(FC):
                        mm = t_.matmul(k.psBIG1[:, h0:h0 + w],
                                       k.otld[:, half * FC * 128 + c * 128:
                                              half * FC * 128 + (c + 1) * 128],
                                       k.wo_sb[:, c * D + h0: c * D + h0 + w],
                                       start=(c == 0), stop=(c == FC - 1))
                k.sP.inc(mm)
                mPW[tP] = k.sP.n
            if 0 <= tL < NBS:
                t_.wait_ge(k.sA.h, mEV[tL])
                if tL >= 2:
                    t_.wait_ge(k.sV.h, mSC[tL - 2])
                ps = k.psMED1 if tL % 2 == 0 else k.psMED2
                for c in range(FC):
                    mm = t_.matmul(ps[:, 0:g.E],
                                   bA(k, c * S + tL * 128, 128),
                                   k.rwT_sb[:, c * g.E:(c + 1) * g.E],
                                   start=(c == 0), stop=(c == FC - 1))
                k.sP.inc(mm)
                mLG[tL] = k.sP.n

        sec(k, "tensor", pe)

        def sa(s, r=r, tT=tT, tP=tP):
            if 0 <= tT < NBS:
                for c in range(FC):
                    ps = k.psSM1 if c % 2 == 0 else k.psSM2
                    s.wait_ge(k.sP.h, mTR[tT] - FC + c + 1)
                    k.sA.inc(s.activation(bA(k, c * S + tT * 128, 128),
                                          ps[:, :], AF.Copy))
                mEV[tT] = k.sA.n
            if 0 <= tP < NBS:
                s.wait_ge(k.sP.h, mPG[tP])
                if tP >= 1:
                    s.wait_ge(k.sV.h, mV6[tP - 1])
                k.sA.inc(s.activation(asg, k.psBIG2[:, 0:D], AF.Sigmoid))
                mSIG[tP] = k.sA.n

        sec(k, "scalar", sa)

        def gst(gp, r=r):
            def step(inst):
                k.sG.inc(inst)
                gp.wait_ge(k.sG.h, k.sG.n)
            y = r - 3
            if 0 <= y < NBS:
                gp.wait_ge(k.sA.h, mQ1[y])
                gp.wait_ge(k.sV.h, mV6[y] + 1)
                step(gp.tensor_scalar(c_s0(y), c_s0(y), 1.0 / D, None, ALU.mult))
                step(gp.tensor_scalar(c_s1(y), c_s1(y), 1.0 / D, None, ALU.mult))
                step(gp.tensor_mul(c_s2(y), c_s0(y), c_s0(y)))
                step(gp.tensor_sub(c_s1(y), c_s1(y), c_s2(y)))
                mST1[y] = k.sG.n
            y = r - 4
            if 0 <= y < NBS:
                gp.wait_ge(k.sA.h, mQ2[y])
                gp.wait_ge(k.sV.h, mV5[y] + 1)
                step(gp.tensor_scalar(c_s0(y), c_s0(y), 1.0 / D, None, ALU.mult))
                step(gp.tensor_scalar(c_s1(y), c_s1(y), 1.0 / D, None, ALU.mult))
                step(gp.tensor_mul(c_s2(y), c_s0(y), c_s0(y)))
                step(gp.tensor_sub(c_s1(y), c_s1(y), c_s2(y)))
                mST2[y] = k.sG.n
            y = r - 5
            if 0 <= y < NBS:
                gp.wait_ge(k.sA.h, mQ3[y])
                step(gp.tensor_scalar(c_s1(y), c_s1(y), 1.0 / D, None, ALU.mult))
                mRM[y] = k.sG.n

        sec(k, "gpsimd", gst)

        def sb_(s, r=r):
            y = r - 3
            if 0 <= y < NBS:
                s.wait_ge(k.sG.h, mST1[y])
                k.sA.inc(s.activation(c_s2(y), c_s1(y), AF.Sqrt, bias=k.epsln_c[:, :]))
                mSQ1[y] = k.sA.n
            y = r - 4
            if 0 <= y < NBS:
                s.wait_ge(k.sG.h, mST2[y])
                k.sA.inc(s.activation(c_s2(y), c_s1(y), AF.Sqrt, bias=k.epsln_c[:, :]))
                mSQ2[y] = k.sA.n
            y = r - 5
            if 0 <= y < NBS:
                s.wait_ge(k.sG.h, mRM[y])
                k.sA.inc(s.activation(c_s2(y), c_s1(y), AF.Sqrt, bias=k.epsrms_c[:, :]))
                mSQ3[y] = k.sA.n

        sec(k, "scalar", sb_)

        def vb(v, r=r, b=b):
            def step(inst):
                k.sV.inc(inst)
                v.wait_ge(k.sV.h, k.sV.n)
            y = r - 3
            if 0 <= y < NBS:
                a = acc[y % 3][:, :]
                v.wait_ge(k.sA.h, mSQ1[y])
                v.wait_ge(k.sG.h, mXR[y])
                step(v.reciprocal(c_s2(y), c_s2(y)))
                step(v.tensor_scalar(c_s3(y), c_s0(y), -1.0, None, ALU.mult))
                step(v.tensor_scalar(a, a, c_s3(y), c_s2(y), ALU.add, ALU.mult))
                step(v.tensor_mul(a, a, k.gam_b[:, :]))
                step(v.tensor_add(a, a, k.bet_b[:, :]))
                step(v.tensor_add(a, a, xres[y % 2][:, :]))
                mV5[y] = k.sV.n
                step(v.reduce_sum(c_s0(y), a, mybir.AxisListType.X))
            y = r - 4
            if 0 <= y < NBS:
                a = acc[y % 3][:, :]
                v.wait_ge(k.sA.h, mSQ2[y])
                if y >= 2:
                    v.wait_ge(k.sP.h, mTR[y - 2])
                    v.wait_ge(k.sY.h, mST[y - 2])
                else:
                    v.wait_ge(k.sP.h, m_pair_p)
                step(v.reciprocal(c_s2(y), c_s2(y)))
                step(v.tensor_scalar(c_s3(y), c_s0(y), -1.0, None, ALU.mult))
                step(v.tensor_scalar(a, a, c_s3(y), c_s2(y), ALU.add, ALU.mult))
                dst = k.gth0 if y % 2 == 0 else k.gth1
                step(v.tensor_copy(dst[:, :], a))
                mV7[y] = k.sV.n
            y = r - 5
            if 0 <= y < NBS:
                v.wait_ge(k.sA.h, mSQ3[y])
                if y >= 2:
                    v.wait_ge(k.sV.h, mSC[y - 2])
                step(v.reciprocal(c_rr(y), c_s2(y)))
                mRC[y] = k.sV.n
            y = r - 6
            if 0 <= y < NBS:
                t = b * NBS + y
                ps = k.psMED1 if y % 2 == 0 else k.psMED2
                v.wait_ge(k.sP.h, mLG[y])
                k.sV.inc(v.tensor_scalar(k.logits[:, t * g.E:(t + 1) * g.E],
                                         ps[:, 0:g.E], c_rr(y), None, ALU.mult))
                mSC[y] = k.sV.n
            y = r - 2
            if 0 <= y < NBS:
                a = acc[y % 3][:, :]
                v.wait_ge(k.sA.h, mSIG[y])
                v.wait_ge(k.sP.h, mPW[y])
                if y >= 3:
                    v.wait_ge(k.sY.h, mST[y - 3])
                step(v.tensor_mul(a, k.psBIG1[:, 0:D], asg))
                mV6[y] = k.sV.n
                step(v.reduce_sum(c_s0(y), a, mybir.AxisListType.X))

        sec(k, "vector", vb)

        def sc_(s, r=r):
            y = r - 3
            if 0 <= y < NBS:
                s.wait_ge(k.sV.h, mV5[y])
                k.sA.inc(s.activation(dumm, acc[y % 3][:, :], AF.Square,
                                      accum_out=c_s1(y)))
                mQ2[y] = k.sA.n
            y = r - 4
            if 0 <= y < NBS:
                s.wait_ge(k.sV.h, mV7[y])
                k.sA.inc(s.activation(dumm, acc[y % 3][:, :], AF.Square,
                                      accum_out=c_s1(y)))
                mQ3[y] = k.sA.n
            y = r - 2
            if 0 <= y < NBS:
                s.wait_ge(k.sV.h, mV6[y] + 1)
                if y >= 3:
                    s.wait_ge(k.sG.h, mRM[y - 3])
                k.sA.inc(s.activation(dumm, acc[y % 3][:, :], AF.Square,
                                      accum_out=c_s1(y)))
                mQ1[y] = k.sA.n

        sec(k, "scalar", sc_)

        def sy(s, r=r, b=b):
            y = r
            if 0 <= y < NBS:
                half = y % 2
                if y >= 2:
                    s.wait_ge(k.sP.h, mPW[y - 2])
                mLD[y] = k.stS.batch(s, lambda e, ss, y=y, half=half: [
                    dma(e, ss,
                        out=schunk(k.otld[:, half * FC * 128:(half + 1) * FC * 128],
                                   128),
                        in_=dchunk(k.otT_d.ap()[b * D:(b + 1) * D,
                                                y * 128:(y + 1) * 128]))])
            y = r - 4
            if 0 <= y < NBS:
                t = b * NBS + y
                src = k.gth0 if y % 2 == 0 else k.gth1
                s.wait_ge(k.sV.h, mV7[y])
                mST[y] = k.stS.batch(s, lambda e, ss, t=t, y=y, src=src: [
                    dma(e, ss, out=k.xmid_f_d[t * 128:(t + 1) * 128, :],
                        in_=acc[y % 3][:, :]),
                    dma(e, ss, out=k.xmid_b_d[t * 128:(t + 1) * 128, :],
                        in_=src[:, :])])

        sec(k, "sync", sy)

        def gx(gp, r=r, b=b):
            y = r - 2
            if 0 <= y < NBS:
                if y <= 1:
                    gp.wait_ge(k.sP.h, m_pair_p)
                else:
                    gp.wait_ge(k.sV.h, mV5[y - 2])
                mXR[y] = k.stG.batch(gp, lambda e, ss, y=y: [
                    dma(e, ss, out=xres[y % 2][:, :],
                        in_=k.x_d[b * S + y * 128:(b * S + (y + 1) * 128), :])])

        sec(k, "gpsimd", gx)

    m_last_v = k.sV.n
    m_last_a = k.sA.n

    def syf(s, b=b):
        s.wait_ge(k.sV.h, m_last_v)
        s.wait_ge(k.sA.h, m_last_a)
        def em(e, ss):
            for c in range(FC):
                dma(e, ss, out=k.xmT_d[c * 128:(c + 1) * 128, b * S:(b + 1) * S],
                    in_=bA(k, c * S, S))
        k.stS.batch(s, em)

    sec(k, "sync", syf)
    k.m_seq_end = dict(sY=k.sY.n, sV=k.sV.n, sP=k.sP.n, sA=k.sA.n, sG=k.sG.n)



def emit_routing(k):
    nc, g = k.nc, k.g
    NT, E = g.NT, g.E

    def RS(t):
        b0 = (t % 4) * 40
        r = k.rsc
        return {"a": r[:, b0:b0 + 8], "b": r[:, b0 + 8:b0 + 16],
                "c": r[:, b0 + 16:b0 + 24], "d": r[:, b0 + 24:b0 + 32],
                "s0": r[:, b0 + 32:b0 + 33], "s1": r[:, b0 + 33:b0 + 34],
                "s2": r[:, b0 + 34:b0 + 35], "s3": r[:, b0 + 35:b0 + 36],
                "s4": r[:, b0 + 36:b0 + 37]}

    def ilv(v, tiles, oplists, lastm, xwait=None):
        nops = len(oplists[tiles[0]])
        for opi in range(nops):
            for t in tiles:
                if lastm.get(t):
                    v.wait_ge(k.sV.h, lastm[t])
                elif xwait is not None:
                    xwait(t)
                k.sV.inc(oplists[t][opi]())
                lastm[t] = k.sV.n

    k.m_cnt_v = {}
    mV1, mEXP, mV2, mPC = {}, {}, {}, {}
    X_ = mybir.AxisListType.X
    for base in range(0, NT, 4):
        tiles = list(range(base, min(base + 4, NT)))

        def dvA(v, tiles=tiles, base=base):
            if base == 0:
                v.wait_ge(k.sY.h, k.m_seq_end["sY"])
            lastm = {}

            def ops(t):
                R = RS(t)
                L = k.logits[:, t * E:(t + 1) * E]
                return [
                    lambda: v.reduce_max(R["s0"], L, X_),
                    lambda: v.tensor_scalar(R["a"], L, R["s0"], None, ALU.is_equal),
                    lambda: v.tensor_mul(R["b"], R["a"], k.e8mi[:, :]),
                    lambda: v.tensor_scalar(R["b"], R["b"], -1.0, float(E),
                                            ALU.mult, ALU.add),
                    lambda: v.tensor_reduce(R["s1"], R["b"], X_, ALU.min),
                    lambda: v.tensor_scalar(R["a"], k.iota8[:, :], R["s1"],
                                            None, ALU.is_equal),
                    lambda: v.scalar_tensor_tensor(R["c"], R["a"], BIGNEG, L,
                                                   ALU.mult, ALU.add),
                    lambda: v.reduce_max(R["s2"], R["c"], X_),
                    lambda: v.tensor_scalar(R["d"], R["c"], R["s2"], None,
                                            ALU.is_equal),
                    lambda: v.tensor_mul(R["d"], R["d"], k.e8mi[:, :]),
                    lambda: v.tensor_scalar(R["d"], R["d"], -1.0, float(E),
                                            ALU.mult, ALU.add),
                    lambda: v.tensor_reduce(R["s3"], R["d"], X_, ALU.min),
                    lambda: v.tensor_scalar(R["b"], k.iota8[:, :], R["s3"],
                                            None, ALU.is_equal),
                    lambda: v.tensor_add(k.selbuf[:, t * E:(t + 1) * E],
                                         R["a"], R["b"]),
                    lambda: v.tensor_sub(R["s2"], R["s2"], R["s0"]),
                    lambda: v.tensor_copy(k.i1c[:, t:t + 1], R["s1"]),
                    lambda: v.tensor_copy(k.i2c[:, t:t + 1], R["s3"]),
                ]
            ilv(v, tiles, {t: ops(t) for t in tiles}, lastm)
            for t in tiles:
                mV1[t] = lastm[t]

        sec(k, "vector", dvA)

        def acA(s, tiles=tiles):
            for t in tiles:
                s.wait_ge(k.sV.h, mV1[t])
                R = RS(t)
                k.sA.inc(s.activation(R["s2"], R["s2"], AF.Exp))
                mEXP[t] = k.sA.n

        sec(k, "scalar", acA)

        def dvB(v, tiles=tiles):
            lastm = {}

            def ops(t):
                R = RS(t)
                return [
                    lambda: v.tensor_scalar(R["s2"], R["s2"], 1.0, None, ALU.add),
                    lambda: v.reciprocal(R["s2"], R["s2"]),
                    lambda: v.tensor_copy(k.rw1c[:, t:t + 1], R["s2"]),
                    lambda: v.tensor_scalar(R["s2"], R["s2"], -1.0, 1.0,
                                            ALU.mult, ALU.add),
                    lambda: v.tensor_copy(k.rw2c[:, t:t + 1], R["s2"]),
                ]
            ilv(v, tiles, {t: ops(t) for t in tiles}, lastm,
                xwait=lambda t: v.wait_ge(k.sA.h, mEXP[t]))
            for t in tiles:
                mV2[t] = lastm[t]

        sec(k, "vector", dvB)

        def cps(t):
            ps = k.psMED1 if t % 2 == 0 else k.psMED2
            c0 = 128 * ((t % 4) // 2)
            return ps[0:E, c0:c0 + 1]

        def peA(t_, tiles=tiles):
            for t in tiles:
                t_.wait_ge(k.sV.h, mV1[t])
                if t >= 4:
                    t_.wait_ge(k.sV.h, k.m_cnt_v[t - 4])
                mm = t_.matmul(cps(t),
                               k.selbuf[:, t * E:(t + 1) * E], k.ones_f[:, 0:1],
                               start=True, stop=True)
                k.sP.inc(mm)
                mPC[t] = k.sP.n

        sec(k, "tensor", peA)

        def dvc(v, tiles=tiles):
            for t in tiles:
                v.wait_ge(k.sP.h, mPC[t])
                k.sV.inc(v.tensor_copy(k.cnt_sb[0:E, t:t + 1], cps(t)))
                k.m_cnt_v[t] = k.sV.n

        sec(k, "vector", dvc)

    m_c1 = k.sV.n

    def pe4(t_):
        t_.wait_ge(k.sV.h, m_c1)
        mm = t_.matmul(k.psMED1[0:NT, 0:E], k.cnt_sb[0:E, 0:NT],
                       k.ident_f[0:E, 0:E], start=True, stop=True)
        k.sP.inc(mm)

    sec(k, "tensor", pe4)
    m_c2 = k.sP.n

    def dv11(v):
        v.wait_ge(k.sP.h, m_c2)
        k.sV.inc(v.tensor_copy(k.cntT_sb[0:NT, 0:E], k.psMED1[0:NT, 0:E]))

    sec(k, "vector", dv11)
    m_c3 = k.sV.n

    def pe5(t_):
        t_.wait_ge(k.sV.h, m_c3)
        mm = t_.matmul(k.psMED1[0:E, 0:1], k.cntT_sb[0:NT, 0:E],
                       k.ones_f[0:NT, 0:1], start=True, stop=True)
        k.sP.inc(mm)

    sec(k, "tensor", pe5)
    m_c4 = k.sP.n

    def dv12(v):
        v.wait_ge(k.sP.h, m_c4)
        k.sV.inc(v.tensor_copy(k.colA[0:E, :], k.psMED1[0:E, 0:1]))

    sec(k, "vector", dv12)
    m_c5 = k.sV.n

    def pe6(t_):
        t_.wait_ge(k.sV.h, m_c5)
        mm = t_.matmul(k.psMED1[0:E, 0:1], k.tri_f[0:E, 0:E], k.colA[0:E, :],
                       start=True, stop=True)
        k.sP.inc(mm)
        mm = t_.matmul(k.psMED2[0:1, 0:E], k.colA[0:E, :], k.ident_f[0:E, 0:E],
                       start=True, stop=True)
        k.sP.inc(mm)

    sec(k, "tensor", pe6)
    m_c6 = k.sP.n

    def dv13(v):
        v.wait_ge(k.sP.h, m_c6)
        k.sV.inc(v.tensor_copy(k.colB[0:E, :], k.psMED1[0:E, 0:1]))
        v.wait_ge(k.sV.h, k.sV.n)
        k.sV.inc(v.tensor_copy(k.tw1[0:1, 0:E], k.psMED2[0:1, 0:E]))

    sec(k, "vector", dv13)
    m_c7 = k.sV.n

    def sy3(s):
        s.wait_ge(k.sV.h, m_c7)
        k.m_cin = k.stS.batch(s, lambda e, ss: [
            dma(e, ss, out=k.cnt_in_d[:, :], in_=k.tw1[0:1, 0:E])])

    sec(k, "sync", sy3)

    def poolcc(gp):
        gp.wait_ge(k.sY.h, k.m_cin)
        cc = gp.collective_compute(
            "AllGather", ALU.bypass,
            replica_groups=[list(range(g.NC))],
            ins=[k.cnt_in_d.ap().opt()], outs=[k.cnt_all_d.ap().opt()])
        k.ccCnt.inc(cc)
        gp.wait_ge(k.ccCnt.h, 1)
        k.m_acnt = k.stG.batch(gp, lambda e, ss: [
            dma(e, ss, out=k.acnt_sb[:, :], in_=k.cnt_all_d[:, :]),
            dma(e, ss, out=k.colD[:, :], in_=k.ltcol_d[:, :])])

    sec(k, "gpsimd", poolcc)
    m_acnt = None

    def pe7(t_):
        t_.wait_ge(k.sV.h, m_c7)
        mm = t_.matmul(k.psMED1[0:1, 0:E], k.colB[0:E, :], k.ident_f[0:E, 0:E],
                       start=True, stop=True)
        k.sP.inc(mm)

    sec(k, "tensor", pe7)
    m_c8 = k.sP.n

    def dv14(v):
        v.wait_ge(k.sP.h, m_c8)
        k.sV.inc(v.tensor_copy(k.tw1[0:1, 64:64 + E], k.psMED1[0:1, 0:E]))

    sec(k, "vector", dv14)
    m_c9 = k.sV.n

    def pe8(t_):
        t_.wait_ge(k.sV.h, m_c9)
        mm = t_.matmul(k.psMED1[:, 0:E], k.ones_f[0:1, 0:128],
                       k.tw1[0:1, 64:64 + E], start=True, stop=True)
        k.sP.inc(mm)

    sec(k, "tensor", pe8)
    m_c10 = k.sP.n

    def dv15(v):
        v.wait_ge(k.sP.h, m_c10)
        k.sV.inc(v.tensor_copy(k.brow[:, 24:24 + E], k.psMED1[:, 0:E]))

    sec(k, "vector", dv15)
    m_slocb = k.sV.n

    k.m_rank_v = {}
    for t in range(NT):
        def pe9(t_, t=t):
            if t == 0:
                t_.wait_ge(k.sV.h, m_slocb)
            if t >= 1:
                t_.wait_ge(k.sV.h, k.m_rank_v[t - 1])
            mm = t_.matmul(k.psMED2[:, 0:E], k.tri_f[:, :],
                           k.selbuf[:, t * E:(t + 1) * E],
                           start=True, stop=(t == 0))
            if t > 0:
                mm = t_.matmul(k.psMED2[:, 0:E], k.ones_f[0:t, 0:128],
                               k.cntT_sb[0:t, 0:E], start=False, stop=True)
            k.sP.inc(mm)

        sec(k, "tensor", pe9)
        m_rk = k.sP.n

        def dv16(v, t=t):
            v.wait_ge(k.sP.h, m_rk)
            k.sV.inc(v.tensor_copy(k.rankbuf[:, t * E:(t + 1) * E], k.psMED2[:, 0:E]))

        sec(k, "vector", dv16)
        k.m_rank_v[t] = k.sV.n

    def pool2(gp):
        gp.wait_ge(k.sG.h, k.m_acnt)
        k.sG.inc(gp.iota(k.colC[0:g.NC, :], [[1, 1]], base=0, channel_multiplier=1,
                         allow_small_or_imprecise_dtypes=True))

    sec(k, "gpsimd", pool2)
    m_iotc = k.sG.n

    def pe10(t_):
        t_.wait_ge(k.sG.h, k.m_acnt)
        t_.wait_ge(k.sV.h, k.m_rank_v[NT - 1])
        mm = t_.matmul(k.psMED1[0:g.NC, 0:E], k.ones_f[0:1, 0:g.NC],
                       k.brow[0:1, 0:E], start=True, stop=True)
        k.sP.inc(mm)
        mm = t_.matmul(k.psMED2[0:g.NC, 0:E], k.ones_f[0:1, 0:g.NC],
                       k.brow[0:1, E:2 * E], start=True, stop=True)
        k.sP.inc(mm)
        mm = t_.matmul(k.psMED1[0:1, 256:256 + E], k.colD[0:g.NC, :],
                       k.acnt_sb[:, :], start=True, stop=True)
        k.sP.inc(mm)

    sec(k, "tensor", pe10)
    m_bc1 = k.sP.n

    def dv17(v):
        def step(inst):
            k.sV.inc(inst)
            v.wait_ge(k.sV.h, k.sV.n)
        v.wait_ge(k.sP.h, m_bc1)
        step(v.tensor_mul(k.brow[0:g.NC, 32:32 + E], k.acnt_sb[:, :], k.psMED1[0:g.NC, 0:E]))
        step(v.reduce_sum(k.colE[0:g.NC, :], k.brow[0:g.NC, 32:32 + E], mybir.AxisListType.X))
        step(v.tensor_mul(k.brow[0:g.NC, 32:32 + E], k.acnt_sb[:, :], k.psMED2[0:g.NC, 0:E]))
        step(v.tensor_copy(k.colB[0:g.NC, :], k.colE[0:g.NC, :]))
        step(v.reduce_sum(k.colA[0:g.NC, :], k.brow[0:g.NC, 32:32 + E], mybir.AxisListType.X))
        step(v.tensor_copy(k.tw1[0:1, 16:16 + E], k.psMED1[0:1, 256:256 + E]))

    sec(k, "vector", dv17)
    m_bc2 = k.sV.n

    def pe11(t_):
        t_.wait_ge(k.sV.h, m_bc2)
        mm = t_.matmul(k.psMED2[0:g.NC, 0:1], k.tri_f[0:g.NC, 0:g.NC], k.colE[0:g.NC, :],
                       start=True, stop=True)
        k.sP.inc(mm)
        mm = t_.matmul(k.psMED1[:, 256:256 + E], k.ones_f[0:1, 0:128],
                       k.tw1[0:1, 16:16 + E], start=True, stop=True)
        k.sP.inc(mm)

    sec(k, "tensor", pe11)
    m_bc3 = k.sP.n

    def dv18(v):
        def step(inst):
            k.sV.inc(inst)
            v.wait_ge(k.sV.h, k.sV.n)
        v.wait_ge(k.sP.h, m_bc3)
        step(v.tensor_copy(k.colD[0:g.NC, :], k.psMED2[0:g.NC, 0:1]))
        step(v.tensor_copy(k.brow[:, 0:E], k.psMED1[:, 256:256 + E]))
        v.wait_ge(k.sG.h, m_iotc)
        step(v.tensor_scalar(k.colC[0:g.NC, :], k.colC[0:g.NC, :], float(2 * g.TL),
                             None, ALU.mult))
        step(v.tensor_add(k.colC[0:g.NC, :], k.colC[0:g.NC, :], k.colA[0:g.NC, :]))
        step(v.tensor_sub(k.colC[0:g.NC, :], k.colC[0:g.NC, :], k.colD[0:g.NC, :]))
        step(v.tensor_add(k.colE[0:g.NC, :], k.colD[0:g.NC, :], k.colB[0:g.NC, :]))

    sec(k, "vector", dv18)
    m_bc4 = k.sV.n

    def pe12(t_):
        t_.wait_ge(k.sV.h, m_bc4)
        mm = t_.matmul(k.psMED2[0:1, 0:g.NC], k.colC[0:g.NC, :], k.ident_f[0:g.NC, 0:g.NC],
                       start=True, stop=True)
        k.sP.inc(mm)
        mm = t_.matmul(k.psMED1[0:1, 256:256 + g.NC], k.colE[0:g.NC, :],
                       k.ident_f[0:g.NC, 0:g.NC], start=True, stop=True)
        k.sP.inc(mm)

    sec(k, "tensor", pe12)
    m_bc5 = k.sP.n

    def dv19(v):
        def step(inst):
            k.sV.inc(inst)
            v.wait_ge(k.sV.h, k.sV.n)
        v.wait_ge(k.sP.h, m_bc5)
        step(v.tensor_copy(k.tw1[0:1, 32:32 + g.NC], k.psMED2[0:1, 0:g.NC]))
        step(v.tensor_copy(k.tw1[0:1, 48:48 + g.NC], k.psMED1[0:1, 256:256 + g.NC]))

    sec(k, "vector", dv19)
    m_bc6 = k.sV.n

    def pe13(t_):
        t_.wait_ge(k.sV.h, m_bc6)
        mm = t_.matmul(k.psMED2[:, 0:g.NC], k.ones_f[0:1, 0:128],
                       k.tw1[0:1, 32:32 + g.NC], start=True, stop=True)
        k.sP.inc(mm)
        mm = t_.matmul(k.psMED1[:, 256:256 + g.NC], k.ones_f[0:1, 0:128],
                       k.tw1[0:1, 48:48 + g.NC], start=True, stop=True)
        k.sP.inc(mm)

    sec(k, "tensor", pe13)
    m_bc7 = k.sP.n

    def dv20(v):
        def step(inst):
            k.sV.inc(inst)
            v.wait_ge(k.sV.h, k.sV.n)
        v.wait_ge(k.sP.h, m_bc7)
        step(v.tensor_copy(k.brow[:, 8:8 + g.NC], k.psMED2[:, 0:g.NC]))
        step(v.tensor_copy(k.brow[:, 16:16 + g.NC], k.psMED1[:, 256:256 + g.NC]))
        step(v.tensor_sub(k.brow[:, 40:40 + g.NC - 1], k.brow[:, 9:9 + g.NC - 1],
                          k.brow[:, 8:8 + g.NC - 1]))

    sec(k, "vector", dv20)
    m_bcast = k.sV.n

    for base in range(0, NT, 4):
        tiles = list(range(base, min(base + 4, NT)))

        def dv21(v, tiles=tiles, base=base):
            if base == 0:
                v.wait_ge(k.sV.h, m_bcast)
            lastm = {}

            def ops(t):
                R = RS(t)
                out = []
                for ic, rwc, sx, gx, rv in [
                        (k.i1c, k.rw1c, k.sidx1, k.gidx1, k.rwv1),
                        (k.i2c, k.rw2c, k.sidx2, k.gidx2, k.rwv2)]:
                    out += [
                        lambda ic=ic: v.tensor_scalar(
                            R["a"], k.iota8[:, :],
                            ic[:, t:t + 1], None, ALU.is_equal),
                        lambda: v.tensor_mul(R["b"], R["a"],
                                             k.rankbuf[:, t * E:(t + 1) * E]),
                        lambda: v.reduce_sum(R["s0"], R["b"], mybir.AxisListType.X),
                        lambda: v.tensor_mul(R["b"], R["a"], k.brow[:, 24:24 + E]),
                        lambda: v.reduce_sum(R["s1"], R["b"], mybir.AxisListType.X),
                        lambda: v.tensor_add(R["s1"], R["s1"], R["s0"]),
                        lambda sx=sx: v.tensor_copy(sx[:, t:t + 1], R["s1"]),
                        lambda: v.tensor_mul(R["b"], R["a"], k.brow[:, 0:E]),
                        lambda: v.reduce_sum(R["s2"], R["b"], mybir.AxisListType.X),
                        lambda: v.tensor_add(R["s2"], R["s2"], R["s0"]),
                        lambda: v.tensor_scalar(R["s3"], R["s2"], float(g.CAP),
                                                None, ALU.is_lt),
                        lambda rwc=rwc: v.tensor_mul(R["s4"], rwc[:, t:t + 1],
                                                     R["s3"]),
                        lambda rv=rv: v.tensor_copy(rv[:, t:t + 1], R["s4"]),
                        lambda gx=gx: v.tensor_copy(gx[:, t:t + 1], R["s2"]),
                    ]
                return out
            ilv(v, tiles, {t: ops(t) for t in tiles}, lastm)

        sec(k, "vector", dv21)

    def pool3(gp):
        k.sG.inc(gp.iota(k.ricmp[:, :], [[128, g.NCT]], base=0, channel_multiplier=1,
                         allow_small_or_imprecise_dtypes=True))

    sec(k, "gpsimd", pool3)
    m_ric = k.sG.n

    def dv22(v):
        def step(inst):
            k.sV.inc(inst)
            v.wait_ge(k.sV.h, k.sV.n)
        v.wait_ge(k.sG.h, m_ric)
        v.wait_ge(k.sV.h, m_bcast)
        step(v.tensor_scalar(k.icmpf[:, :], k.ricmp[:, :], k.brow[:, 8:9], None, ALU.add))
        for c in range(1, g.NC):
            step(v.tensor_scalar(k.icmpw[:, :], k.ricmp[:, :],
                                 k.brow[:, 16 + c - 1:16 + c], None, ALU.is_ge))
            step(v.scalar_tensor_tensor(k.icmpf[:, :], k.icmpw[:, :],
                                        k.brow[:, 40 + c - 1:40 + c], k.icmpf[:, :],
                                        ALU.mult, ALU.add))
        step(v.tensor_scalar(k.icmpf[:, :], k.icmpf[:, :], float(g.NC * 2 * g.TL),
                             None, ALU.min))
        step(v.tensor_scalar(k.icmpf[:, :], k.icmpf[:, :], 0.0, None, ALU.max))
        step(v.tensor_copy(k.icmp[:, :], k.icmpf[:, :]))

    sec(k, "vector", dv22)
    k.m_routing = marks(k)



def emit_ag_xmid(k, b):
    g = k.g
    S = g.S

    def pool(gp, b=b):
        gp.wait_ge(k.sY.h, k.m_seq_end["sY"])
        cc = gp.collective_compute(
            "AllGather", ALU.bypass,
            replica_groups=[list(range(g.NC))],
            ins=[k.xmid_b_d.ap()[b * S:(b + 1) * S, :].opt()],
            outs=[k.xma_d.ap()[b * g.NC * S:(b + 1) * g.NC * S, :].opt()])
        k.ccXg.inc(cc)

    sec(k, "gpsimd", pool)


def emit_scatter(k):
    nc, g = k.nc, k.g
    S, NBS = g.S, g.NBS

    for base in range(0, g.NT, 4):
        tiles = list(range(base, min(base + 4, g.NT)))

        def pool(gp, tiles=tiles, base=base):
            if base == 0:
                wait_marks(k, gp, dict(sV=k.m_routing["sV"]))
            for t in tiles:
                b = t // NBS
                const = float(b * g.NC * S + (t % NBS) * 128)
                k.sG.inc(gp.tensor_scalar(k.tidcol[:, t % 8:t % 8 + 1],
                                          k.cbcol[:, :], const, None, ALU.add))
            gp.wait_ge(k.sG.h, k.sG.n)

            def em(e, ss):
                for t in tiles:
                    for sx in (k.sidx1, k.sidx2):
                        idma(e, ss,
                             out=k.tid_d[:, :],
                             out_offset=bass.IndirectOffsetOnAxis(
                                 ap=sx[:, t:t + 1], axis=0),
                             in_=k.tidcol[:, t % 8:t % 8 + 1],
                             in_offset=None)
            k.stG.batch(gp, em)

        sec(k, "gpsimd", pool)
    k.m_scatter_done = k.sG.n


def emit_ag_xg(k):
    g = k.g

    def pool(gp):
        k.rXgB = gp.alloc_register("rXgB")
        gp.reg_mov(k.rXgB, g.NC * 2 * g.TL - 1)
        k.rXmB = gp.alloc_register("rXmB")
        gp.reg_mov(k.rXmB, g.NC * g.TL - 1)
        k.rEoB = gp.alloc_register("rEoB")
        gp.reg_mov(k.rEoB, g.NC * g.CAP - 1)
        k.sG.inc(gp.memset(k.tidv[:, :], 0.0))
        cc = gp.collective_compute(
            "AllGather", ALU.bypass,
            replica_groups=[list(range(g.NC))],
            ins=[k.tid_d.ap().opt()], outs=[k.tidall_d.ap().opt()])
        k.ccXg.inc(cc)

    sec(k, "gpsimd", pool)


def emit_gidx(k, tiles=None):
    g = k.g
    NT, E = g.NT, g.E
    big = float(g.NC * g.CAP)
    if tiles is None:
        tiles = range(NT)

    for t in tiles:
        def gpass(gp, t=t):
            def step(inst):
                k.sG.inc(inst)
                gp.wait_ge(k.sG.h, k.sG.n)
            if t == 0:
                gp.wait_ge(k.sV.h, k.m_routing["sV"])
            for ic, gx in [(k.i1c, k.gidx1), (k.i2c, k.gidx2)]:
                step(gp.tensor_copy(k.st2[:, :], gx[:, t:t + 1]))
                step(gp.tensor_scalar(k.st3[:, :], k.st2[:, :], float(g.CAP),
                                      None, ALU.is_lt))
                step(gp.tensor_scalar(k.st0[:, :], k.st2[:, :], float(g.ECH),
                                      None, ALU.is_ge))
                for jj in range(2, g.NECH):
                    step(gp.tensor_scalar(k.st1[:, :], k.st2[:, :],
                                          float(jj * g.ECH), None, ALU.is_ge))
                    step(gp.tensor_add(k.st0[:, :], k.st0[:, :], k.st1[:, :]))
                step(gp.tensor_scalar(k.st0[:, :], k.st0[:, :],
                                      float((g.NC - 1) * g.ECH), None, ALU.mult))
                step(gp.tensor_add(k.st2[:, :], k.st2[:, :], k.st0[:, :]))
                step(gp.tensor_scalar(k.st0[:, :], ic[:, t:t + 1], float(g.ECH),
                                      None, ALU.mult))
                step(gp.tensor_add(k.st2[:, :], k.st2[:, :], k.st0[:, :]))
                step(gp.tensor_scalar(k.st0[:, :], k.st3[:, :], -big, big,
                                      ALU.mult, ALU.add))
                step(gp.tensor_add(k.st2[:, :], k.st2[:, :], k.st0[:, :]))
                step(gp.tensor_scalar(k.st2[:, :], k.st2[:, :], big, None, ALU.min))
                step(gp.tensor_copy(gx[:, t:t + 1], k.st2[:, :]))

        sec(k, "gpsimd", gpass)
    k.m_gidx = k.sG.n



def emit_glu(k, which):
    nc, g = k.nc, k.g
    D, FC = g.D, g.FC
    if which == "shared":
        rows, GS, HID, IQL = g.TL, g.GS_S, g.IS, g.IQ_S
        wg_d, wu_d, wd_d, out_d = k.sg_b_d, k.su_b_d, k.sd_b_d, k.shr_d
    else:
        rows, GS, HID, IQL = g.CAP, g.GS_E, g.IR, g.IQ_E
        wg_d, wu_d, wd_d, out_d = k.rg_b_d, k.ru_b_d, k.rd_b_d, k.eout_f_d
    ngroup = rows // GS
    nisub = IQL // 128
    nq = HID // IQL
    jw = widths(GS, 512)
    nsb = GS // 128
    dws = widths(D, 512)

    gi = dict(idx=0, pg={}, pu={}, am={}, vm={})
    k.m_glu_wused = {}
    k.m_glu_dn = {}
    k.m_glu_xpar = {}
    if not hasattr(k, "m_glu_last_st"):
        k.m_glu_last_st = 0
    m_prev_glu = getattr(k, "m_glu_done", None)

    def slab_sl(grp, col0, w):
        return k.xslab[:, col0:col0 + w]

    for grp in range(ngroup):
        r0 = grp * GS
        if which == "shared":
            def syx(s, grp=grp, r0=r0):
                if grp == 0:
                    assert not k.precast_jobs
                    s.wait_ge(k.sPC.h, k.sPC.n)
                else:
                    s.wait_ge(k.sP.h, k.m_glu_xused)
                def em(e, ss):
                    for c in range(FC):
                        dma(e, ss, out=k.xslab[:, c * GS:(c + 1) * GS],
                            in_=k.xmT_d[c * 128:(c + 1) * 128, r0:r0 + GS])
                k.m_glu_x = k.stS.batch(s, em)

            sec(k, "sync", syx)
        else:
            m_slab_free = (m_prev_glu["sP"] if grp == 0
                           else k.m_glu_xused)
            emit_expert_slab(k, grp, r0, GS, slab_sl, m_slab_free)
        m_x = marks(k)

        for q4 in range(nq):
            h0 = q4 * IQL

            def syd(s, grp=grp, q4=q4, h0=h0):
                if getattr(k, "m_glu_wdused", 0):
                    s.wait_ge(k.sP.h, k.m_glu_wdused)
                def em(e, ss):
                    for isub in range(nisub):
                        dma(e, ss, out=k.wd_sl[:, isub * D:(isub + 1) * D],
                            in_=wd_d[h0 + isub * 128: h0 + isub * 128 + 128, :])
                k.m_glu_wd = k.stS.batch(s, em)

            sec(k, "sync", syd)

            for isub in range(nisub):
                half = isub % 2

                def syw(s, grp=grp, q4=q4, isub=isub, h0=h0, half=half):
                    prev = k.m_glu_wused.get(half)
                    if prev:
                        s.wait_ge(k.sP.h, prev)
                    elif m_prev_glu:
                        s.wait_ge(k.sP.h, m_prev_glu["sP"])
                    def em(e, ss):
                        for w_d, dst in [(wg_d, k.wg_sl), (wu_d, k.wu_sl)]:
                            dma(e, ss,
                                out=schunk(dst[:, half * FC * 128:(half + 1) * FC * 128], 128),
                                in_=dchunk(w_d.ap()[:, h0 + isub * 128: h0 + isub * 128 + 128]))
                    k.m_glu_w = k.stS.batch(s, em)

                sec(k, "sync", syw)

                def pe(t_, grp=grp, q4=q4, isub=isub, half=half, m_x=m_x):
                    t_.wait_ge(k.sY.h, k.m_glu_w)
                    if isub == 0 and q4 == 0:
                        if which == "shared":
                            t_.wait_ge(k.sY.h, m_x["sY"])
                            if grp == 0:
                                t_.wait_ge(k.sV.h, k.m_seq_end["sV"])
                                t_.wait_ge(k.sA.h, k.m_seq_end["sA"])
                        else:
                            wait_marks(k, t_, dict(sV=m_x["sV"], sY=m_x["sY"],
                                                   sG=m_x["sG"]))
                            if grp == 0 and m_prev_glu:
                                t_.wait_ge(k.sA.h, m_prev_glu["sA"])
                    for j in range(len(jw)):
                        gi["am"][gi["idx"] + j] = k.sA.n + j + 1
                        gi["vm"][gi["idx"] + j] = k.sV.n + j + 1
                    for ji, (j0, w) in enumerate(jw):
                        idx = gi["idx"]
                        bank = k.psBIG1 if idx % 2 == 0 else k.psBIG2
                        if idx >= 2:
                            t_.wait_ge(k.sA.h, gi["am"][idx - 2])
                            t_.wait_ge(k.sV.h, gi["vm"][idx - 2])
                        for c in range(FC):
                            mm = t_.matmul(bank[:, 0:w],
                                           k.wg_sl[:, half * FC * 128 + c * 128:
                                                   half * FC * 128 + (c + 1) * 128],
                                           slab_sl(grp, c * GS + j0, w),
                                           start=(c == 0), stop=(c == FC - 1))
                        k.sP.inc(mm)
                        gi["pg"][idx] = k.sP.n
                        for c in range(FC):
                            mm = t_.matmul(bank[:, 512:512 + w],
                                           k.wu_sl[:, half * FC * 128 + c * 128:
                                                   half * FC * 128 + (c + 1) * 128],
                                           slab_sl(grp, c * GS + j0, w),
                                           start=(c == 0), stop=(c == FC - 1))
                        k.sP.inc(mm)
                        gi["pu"][idx] = k.sP.n
                        gi["idx"] += 1

                sec(k, "tensor", pe)
                k.m_glu_wused[half] = k.sP.n

                def ac(s, grp=grp, q4=q4, isub=isub):
                    n_j = len(jw)
                    base = gi["idx"] - n_j
                    for ji, (j0, w) in enumerate(jw):
                        idx = base + ji
                        bank = k.psBIG1 if idx % 2 == 0 else k.psBIG2
                        ft = k.ftmp[:, 0:512] if idx % 2 == 0 else k.ftmp[:, 512:1024]
                        s.wait_ge(k.sP.h, gi["pg"][idx])
                        if idx >= 2:
                            s.wait_ge(k.sV.h, gi["vm"][idx - 2])
                        k.sA.inc(s.activation(ft[:, 0:w], bank[:, 0:w], AF.Silu))
                        assert gi["am"][idx] == k.sA.n, (idx, gi["am"][idx], k.sA.n)

                sec(k, "scalar", ac)

                def dv(v, grp=grp, q4=q4, isub=isub):
                    n_j = len(jw)
                    base = gi["idx"] - n_j
                    for ji, (j0, w) in enumerate(jw):
                        idx = base + ji
                        bank = k.psBIG1 if idx % 2 == 0 else k.psBIG2
                        ft = k.ftmp[:, 0:512] if idx % 2 == 0 else k.ftmp[:, 512:1024]
                        v.wait_ge(k.sP.h, gi["pu"][idx])
                        v.wait_ge(k.sA.h, gi["am"][idx])
                        k.sV.inc(v.tensor_mul(bA(k, isub * GS + j0, w),
                                              ft[:, 0:w], bank[:, 512:512 + w]))
                        assert gi["vm"][idx] == k.sV.n, (idx, gi["vm"][idx], k.sV.n)

                sec(k, "vector", dv)

            m_h = marks(k)
            for sb_ in range(nsb):
                def pe2(t_, grp=grp, q4=q4, sb_=sb_, m_h=m_h):
                    if sb_ == 0:
                        t_.wait_ge(k.sY.h, k.m_glu_wd)
                        t_.wait_ge(k.sV.h, m_h["sV"])
                    if sb_ >= 1:
                        t_.wait_ge(k.sV.h, k.m_glu_dn[sb_ - 1])
                    for hi, (h0_, w) in enumerate(dws):
                        ps = k.psMED1 if hi == 0 else k.psMED2
                        for isub in range(nisub):
                            mm = t_.matmul(ps[:, 0:w],
                                           bA(k, isub * GS + sb_ * 128, 128),
                                           k.wd_sl[:, isub * D + h0_: isub * D + h0_ + w],
                                           start=(isub == 0),
                                           stop=(isub == nisub - 1))
                        k.sP.inc(mm)

                sec(k, "tensor", pe2)
                m_dn = k.sP.n

                def dv2(v, grp=grp, q4=q4, sb_=sb_, m_dn=m_dn):
                    v.wait_ge(k.sP.h, m_dn)
                    if k.m_glu_last_st:
                        v.wait_ge(k.sG.h, k.m_glu_last_st)
                    for hi, (h0_, w) in enumerate(dws):
                        ps = k.psMED1 if hi == 0 else k.psMED2
                        if hi > 0:
                            v.wait_ge(k.sV.h, k.sV.n)
                        k.sV.inc(v.tensor_copy(k.tw0[:, h0_:h0_ + w], ps[:, 0:w]))
                    k.m_glu_dn[sb_] = k.sV.n

                sec(k, "vector", dv2)
                m_ev = k.sV.n

                def pl2(gp, grp=grp, q4=q4, sb_=sb_, r0=r0, m_ev=m_ev):
                    gp.wait_ge(k.sV.h, m_ev)
                    rr = r0 + sb_ * 128
                    op = ALU.bypass if q4 == 0 else ALU.add
                    k.m_glu_last_st = k.stG.batch(gp, lambda e, ss: [
                        dma(e, ss, out=out_d[rr:rr + 128, :], in_=k.tw0[:, :],
                            accum_op=op)])

                sec(k, "gpsimd", pl2)
            k.m_glu_wdused = k.sP.n
            if which == "shared" and getattr(k, "gidx_jobs", None):
                emit_gidx(k, k.gidx_jobs[:4])
                del k.gidx_jobs[:4]
        k.m_glu_xused = k.sP.n
        if which == "expert":
            ch = grp
            m_st = k.m_glu_last_st

            def plag(gp, ch=ch, m_st=m_st):
                gp.wait_ge(k.sG.h, m_st)
                def em(e, ss):
                    for r0_, w_ in widths(g.ECH, 512):
                        rr_ = ch * g.ECH + r0_
                        dma(e, ss, out=k.eout_b_d[rr_:rr_ + w_, :],
                            in_=k.eout_f_d[rr_:rr_ + w_, :])
                k.stG.batch(gp, em)
                cc = gp.collective_compute(
                    "AllGather", ALU.bypass,
                    replica_groups=[list(range(g.NC))],
                    ins=[k.eout_b_d.ap()[ch * g.ECH:(ch + 1) * g.ECH, :].opt()],
                    outs=[k.eall_d.ap()[ch * g.NC * g.ECH:
                                        (ch + 1) * g.NC * g.ECH, :].opt()])
                k.ccEo.inc(cc)

            sec(k, "gpsimd", plag)
    k.m_glu_done = marks(k)
    if which == "shared":
        k.m_shared_done = marks(k)
    else:
        k.m_expert_done = marks(k)


def emit_expert_slab(k, grp, r0, GS, slab_sl, m_slab_free):
    nc, g = k.nc, k.g
    FC = g.FC
    ntile = GS // 128
    gths = [k.gth0, k.gth1, k.gth2, k.gth3]

    for tt in range(ntile):
        j = r0 // 128 + tt

        def pl(gp, tt=tt, j=j, grp=grp):
            if grp == 0 and tt == 0:
                gp.wait_ge(k.ccXg.h, 3)
                wait_marks(k, gp, dict(sV=k.m_routing["sV"],
                                       sP=k.m_seq_end["sP"],
                                       sY=k.m_seq_end["sY"]))
            prev = k.m_exp_tr.get(j - 4) if hasattr(k, "m_exp_tr") else None
            if prev:
                gp.wait_ge(k.sP.h, prev)
            dst = gths[j % 4]
            slot = tt % 2
            k.stG.batch(gp, lambda e, ss: [
                idma(e, ss,
                     out=k.tidv[:, slot:slot + 1],
                     out_offset=None,
                     in_=k.tidall_d[:, :],
                     in_offset=bass.IndirectOffsetOnAxis(ap=k.icmp[:, j:j + 1], axis=0),
                     bounds_check=k.rXgB,
                     oob_is_err=False)])
            k.sG.inc(gp.tensor_copy(k.tidvi[:, slot:slot + 1],
                                    k.tidv[:, slot:slot + 1]))
            gp.wait_ge(k.sG.h, k.sG.n)
            k.m_exp_g = k.stG.batch(gp, lambda e, ss: [
                idma(e, ss,
                     out=dst[:, :],
                     out_offset=None,
                     in_=k.xma_d[:, :],
                     in_offset=bass.IndirectOffsetOnAxis(ap=k.tidvi[:, slot:slot + 1],
                                                         axis=0),
                     bounds_check=k.rXmB,
                     oob_is_err=False)])

        sec(k, "gpsimd", pl)
        p0 = k.sP.n
        v0 = k.sV.n

        def pe(t_, tt=tt, j=j, p0=p0, v0=v0):
            t_.wait_ge(k.sG.h, k.m_exp_g)
            src = gths[j % 4]
            for c in range(FC):
                ps = k.psSM1 if c % 2 == 0 else k.psSM2
                if c >= 2:
                    t_.wait_ge(k.sV.h, v0 + c - 1)
                mm = t_.transpose(ps[:, :], src[:, c * 128:(c + 1) * 128], k.ident_bf[:, :])
                k.sP.inc(mm)
            k.m_exp_tr = getattr(k, "m_exp_tr", {})
            k.m_exp_tr[j] = k.sP.n

        def dv(v, tt=tt, j=j, p0=p0, GS=GS, grp=grp):
            if tt == 0:
                v.wait_ge(k.sP.h, m_slab_free)
            for c in range(FC):
                ps = k.psSM1 if c % 2 == 0 else k.psSM2
                v.wait_ge(k.sP.h, p0 + c + 1)
                k.sV.inc(v.tensor_copy(
                    slab_sl(grp, c * GS + tt * 128, 128), ps[:, :]))

        sec(k, "tensor", pe)
        sec(k, "vector", dv)



def emit_phase_c(k):
    nc, g = k.nc, k.g
    D = g.D
    NT = g.NT
    with contextlib.ExitStack() as es_c:
        csb = lambda name, shape, dt: es_c.enter_context(nc.sbuf_tensor(name, shape, dt))
        cshr = [csb("cshr%d" % j, [128, D], F32) for j in range(3)]
        cxm = [csb("cxm%d" % j, [128, D], F32) for j in range(3)]
        ce1 = [csb("ce1%d" % j, [128, D], BF16) for j in range(3)]
        ce2 = [csb("ce2%d" % j, [128, D], BF16) for j in range(3)]
        cacc = [csb("cacc%d" % j, [128, D], F32) for j in range(4)]
        cdum = [csb("cdum%d" % j, [128, D], F32) for j in range(2)]
        cst = csb("cst", [128, 16], F32)
        ml, mg, mcb, msq, msv, msx, mnm, mst = {}, {}, {}, {}, {}, {}, {}, {}

        for i in range(NT + 3):
            tl, tc, ts_, tn = i, i - 1, i - 2, i - 3
            if tl < NT:
                j3 = tl % 3

                def syl(s, t=tl, j3=j3):
                    if t == 0:
                        s.wait_ge(k.sG.h, k.m_expert_done["sG"])
                        s.wait_ge(k.sY.h, k.m_seq_end["sY"])
                    if t >= 3:
                        s.wait_ge(k.sV.h, mcb[t - 3])
                    ml[t] = k.stS.batch(s, lambda e, ss: [
                        dma(e, ss, out=cshr[j3][:, :],
                            in_=k.shr_d[t * 128:(t + 1) * 128, :]),
                        dma(e, ss, out=cxm[j3][:, :],
                            in_=k.xmid_f_d[t * 128:(t + 1) * 128, :])])

                sec(k, "sync", syl)

                def pl(gp, t=tl, j3=j3):
                    if t == 0:
                        gp.wait_ge(k.ccEo.h, k.g.NECH)
                        gp.wait_ge(k.sV.h, k.m_routing["sV"])
                    if t >= 3:
                        gp.wait_ge(k.sV.h, mcb[t - 3])

                    def em(e, ss):
                        for gx, dst in [(k.gidx1, ce1[j3]), (k.gidx2, ce2[j3])]:
                            idma(e, ss, out=dst[:, :], out_offset=None,
                                 in_=k.eall_d[:, :],
                                 in_offset=bass.IndirectOffsetOnAxis(
                                     ap=gx[:, t:t + 1], axis=0),
                                 bounds_check=k.rEoB, oob_is_err=False)
                    mg[t] = k.stG.batch(gp, em)

                sec(k, "gpsimd", pl)

            if 0 <= tc < NT:
                j3, j4 = tc % 3, tc % 4

                def dvc(v, t=tc, j3=j3, j4=j4):
                    def step(inst):
                        k.sV.inc(inst)
                        v.wait_ge(k.sV.h, k.sV.n)
                    v.wait_ge(k.sY.h, ml[t])
                    v.wait_ge(k.sG.h, mg[t])
                    if t >= 4:
                        v.wait_ge(k.sY.h, mst[t - 4])
                    step(v.tensor_add(cacc[j4][:, :], cshr[j3][:, :], cxm[j3][:, :]))
                    step(v.scalar_tensor_tensor(cacc[j4][:, :], ce1[j3][:, :],
                                                k.rwv1[:, t:t + 1], cacc[j4][:, :],
                                                ALU.mult, ALU.add))
                    step(v.scalar_tensor_tensor(cacc[j4][:, :], ce2[j3][:, :],
                                                k.rwv2[:, t:t + 1], cacc[j4][:, :],
                                                ALU.mult, ALU.add))
                    step(v.reduce_sum(cst[:, j4:j4 + 1], cacc[j4][:, :],
                                      mybir.AxisListType.X))
                    mcb[t] = k.sV.n

                sec(k, "vector", dvc)

                def acq(s, t=tc, j4=j4):
                    s.wait_ge(k.sV.h, mcb[t])
                    k.sA.inc(s.activation(cdum[t % 2][:, :], cacc[j4][:, :], AF.Square,
                                          accum_out=cst[:, 4 + j4:5 + j4]))
                    msq[t] = k.sA.n

                sec(k, "scalar", acq)

            if 0 <= ts_ < NT:
                j4 = ts_ % 4

                def dvs(v, t=ts_, j4=j4):
                    def step(inst):
                        k.sV.inc(inst)
                        v.wait_ge(k.sV.h, k.sV.n)
                    v.wait_ge(k.sA.h, msq[t])
                    step(v.tensor_scalar(cst[:, j4:j4 + 1], cst[:, j4:j4 + 1],
                                         1.0 / D, None, ALU.mult))
                    step(v.tensor_scalar(cst[:, 4 + j4:5 + j4], cst[:, 4 + j4:5 + j4],
                                         1.0 / D, None, ALU.mult))
                    step(v.tensor_mul(cst[:, 8 + j4:9 + j4], cst[:, j4:j4 + 1],
                                      cst[:, j4:j4 + 1]))
                    step(v.tensor_sub(cst[:, 4 + j4:5 + j4], cst[:, 4 + j4:5 + j4],
                                      cst[:, 8 + j4:9 + j4]))
                    msv[t] = k.sV.n

                sec(k, "vector", dvs)

                def acr(s, t=ts_, j4=j4):
                    s.wait_ge(k.sV.h, msv[t])
                    k.sA.inc(s.activation(cst[:, 8 + j4:9 + j4], cst[:, 4 + j4:5 + j4],
                                          AF.Sqrt, bias=k.epsln_c[:, :]))
                    msx[t] = k.sA.n

                sec(k, "scalar", acr)

            if 0 <= tn < NT:
                j4 = tn % 4

                def dvn(v, t=tn, j4=j4):
                    def step(inst):
                        k.sV.inc(inst)
                        v.wait_ge(k.sV.h, k.sV.n)
                    v.wait_ge(k.sA.h, msx[t])
                    step(v.reciprocal(cst[:, 8 + j4:9 + j4], cst[:, 8 + j4:9 + j4]))
                    step(v.tensor_scalar(cst[:, 12 + j4:13 + j4], cst[:, j4:j4 + 1],
                                         -1.0, None, ALU.mult))
                    step(v.tensor_scalar(cacc[j4][:, :], cacc[j4][:, :],
                                         cst[:, 12 + j4:13 + j4],
                                         cst[:, 8 + j4:9 + j4],
                                         ALU.add, ALU.mult))
                    mnm[t] = k.sV.n

                sec(k, "vector", dvn)

                def sy2(s, t=tn, j4=j4):
                    s.wait_ge(k.sV.h, mnm[t])
                    mst[t] = k.stS.batch(s, lambda e, ss: [
                        dma(e, ss, out=k.out_d[t * 128:(t + 1) * 128, :],
                            in_=cacc[j4][:, :])])

                sec(k, "sync", sy2)
    k.m_out_done = k.sY.n


def emit_final(k):
    def sy(s):
        s.wait_ge(k.sY.h, k.m_out_done)
        s.wait_ge(k.sG.h, k.sG.n)

    sec(k, "sync", sy)



def _shard_inputs(g, inputs):
    x = np.ascontiguousarray(inputs["L0_attn_x"], dtype=np.float32)
    D = g.D
    base = {
        "wq": np.ascontiguousarray(inputs["L0_attn_Wq"], dtype=np.float32),
        "wk": np.ascontiguousarray(inputs["L0_attn_Wk"], dtype=np.float32),
        "wv": np.ascontiguousarray(inputs["L0_attn_Wv"], dtype=np.float32),
        "wo": np.ascontiguousarray(inputs["L0_attn_Wo"], dtype=np.float32),
        "g1": np.ascontiguousarray(inputs["L0_attn_G1"], dtype=np.float32),
        "g2": np.ascontiguousarray(inputs["L0_attn_G2"], dtype=np.float32),
        "gamma": np.ascontiguousarray(inputs["L0_attn_gamma"], dtype=np.float32).reshape(1, D),
        "beta": np.ascontiguousarray(inputs["L0_attn_beta"], dtype=np.float32).reshape(1, D),
        "theta": np.ascontiguousarray(inputs["L0_attn_theta"], dtype=np.float32).reshape(g.H * g.HD, 1),
        "rw": np.ascontiguousarray(inputs["L0_ffn_router_w"], dtype=np.float32),
        "sg": np.ascontiguousarray(inputs["L0_ffn_sg"], dtype=np.float32),
        "su": np.ascontiguousarray(inputs["L0_ffn_su"], dtype=np.float32),
        "sd": np.ascontiguousarray(inputs["L0_ffn_sd"], dtype=np.float32),
    }
    rg, ru, rd = inputs["L0_ffn_rg"], inputs["L0_ffn_ru"], inputs["L0_ffn_rd"]
    in_maps = []
    for c in range(g.NC):
        m = dict(base)
        m["x"] = np.ascontiguousarray(x[c * g.BL:(c + 1) * g.BL].reshape(g.TL, D))
        m["rg"] = np.ascontiguousarray(rg[c], dtype=np.float32)
        m["ru"] = np.ascontiguousarray(ru[c], dtype=np.float32)
        m["rd"] = np.ascontiguousarray(rd[c], dtype=np.float32)
        m["ltcol"] = (np.arange(g.NC) < c).astype(np.float32).reshape(g.NC, 1)
        m["myrow"] = (np.arange(g.E) == c).astype(np.float32).reshape(1, g.E)
        m["ltrow"] = (np.arange(g.E) < c).astype(np.float32).reshape(1, g.E)
        m["cb"] = (c * g.S + np.arange(128)).astype(np.float32).reshape(128, 1)
        in_maps.append(m)
    return in_maps


def kernel(**inputs):
    g = FULL
    nc = build(g)
    in_maps = _shard_inputs(g, inputs)
    res = run_bass_kernel_spmd(nc, in_maps, core_ids=list(range(g.NC)))
    out = np.concatenate([res.results[c]["out"] for c in range(g.NC)], axis=0)
    return out.reshape(g.B, g.S, g.D).astype(np.float32)


# version 21 (final)
# speedup vs baseline: 1.0809x; 1.0809x over previous
import contextlib

import numpy as np
import ml_dtypes

import concourse.bass as bass
import concourse.mybir as mybir
from concourse.bass_utils import run_bass_kernel_spmd

F32 = mybir.dt.float32
BF16 = mybir.dt.bfloat16
I32 = mybir.dt.int32
AF = mybir.ActivationFunctionType
ALU = mybir.AluOpType

PI_LO = float(np.nextafter(np.float32(np.pi), np.float32(0)))
INV_2PI = float(np.float32(1.0 / (2.0 * np.pi)))
TWO_PI = float(np.float32(2.0 * np.pi))
MAGIC = float(np.float32(2.0 ** 23))
EPS_LN = 1e-5
EPS_RMS = float(np.finfo(np.float32).eps)
BIGNEG = -1e30


class Geom:
    def __init__(self, B=16, S=4096, D=1024, IRmul=4, ISmul=8, NC=8):
        self.B, self.S, self.D, self.NC = B, S, D, NC
        self.HD = 64
        self.H = D // self.HD
        self.PAIRS = self.H // 2
        self.FC = D // 128
        self.BL = B // NC
        self.TL = self.BL * S
        self.NT = self.TL // 128
        self.NBS = S // 128
        self.QW = 512
        self.NQ = S // self.QW
        self.E = 8
        self.IR = IRmul * D
        self.IS = ISmul * D
        self.T = B * S
        self.CAP = max(int(1.25 * self.T / self.E), 1)
        assert self.CAP % 128 == 0
        self.NCT = self.CAP // 128
        self.GS_S = min(2048, self.TL)
        self.GS_E = min(2048, self.CAP)
        self.ECH = self.GS_E
        self.NECH = self.CAP // self.ECH
        self.IQ_S = min(2048, self.IS)
        self.IQ_E = min(2048, self.IR)
        assert self.S % self.QW == 0 and self.D % 128 == 0
        assert self.TL % self.GS_S == 0 and self.CAP % self.GS_E == 0
        assert self.IS % self.IQ_S == 0 and self.IR % self.IQ_E == 0
        assert self.GS_S % 128 == 0 and self.GS_E % 128 == 0
        self.BUFA = max(self.FC * self.S,
                        (self.IQ_S // 128) * self.GS_S,
                        (self.IQ_E // 128) * self.GS_E)
        self.XSLAB = self.FC * max(self.GS_S, self.GS_E)


FULL = Geom()


class K:
    pass


class Sem:
    __slots__ = ("h", "n")

    def __init__(self, h):
        self.h = h
        self.n = 0

    def inc(self, inst, kk=1):
        inst.then_inc(self.h, kk)
        self.n += kk
        return self.n


class Stream:

    def __init__(self, k, name, pub):
        self.k = k
        self.s0 = Sem(k._es.enter_context(k.nc.semaphore(name + "0")))
        self.s1 = Sem(k._es.enter_context(k.nc.semaphore(name + "1")))
        self.pub = pub
        self.i = 0

    def batch(self, eng, emit_fn):
        ssem = self.s0 if self.i % 2 == 0 else self.s1
        n0 = ssem.n
        emit_fn(eng, ssem)
        if ssem.n == n0:
            return self.pub.n
        self.i += 1
        eng.wait_ge(ssem.h, ssem.n)
        eng.sem_inc(self.pub.h, 1)
        self.pub.n += 1
        return self.pub.n

    def flush(self, eng):
        return self.pub.n


def dma(eng, ssem, out, in_, **kw):
    ssem.n += 16
    return eng.dma_start(out=out, in_=in_, **kw).then_inc(ssem.h, 16)


def idma(eng, ssem, **kw):
    ssem.n += 16
    return eng.indirect_dma_start(**kw).then_inc(ssem.h, 16)


def widths(total, step):
    out = []
    c = 0
    while c < total:
        w = min(step, total - c)
        out.append((c, w))
        c += w
    return out


def sec(k, eng, fn):
    getattr(k.block, eng)(fn)


def bA(k, col, width, base=0, rows=128):
    return k.bufA[base:base + rows, col:col + width]


def marks(k):
    return dict(sP=k.sP.n, sV=k.sV.n, sA=k.sA.n, sG=k.sG.n, sY=k.sY.n)


def wait_marks(k, eng, m):
    for name, val in m.items():
        if val > 0:
            eng.wait_ge(getattr(k, name).h, val)


def dchunk(dram_ap, p=128):
    return dram_ap.rearrange("(c p) q -> p c q", p=p)


def schunk(sbuf_slice, q):
    return sbuf_slice.rearrange("p (c q) -> p c q", q=q)


def build(g: Geom):
    nc = bass.Bass("TRN2", target_bir_lowering=False)
    k = K()
    k.nc = nc
    k.g = g
    es = contextlib.ExitStack()
    D, S, FC, TL = g.D, g.S, g.FC, g.TL

    k.x_d = nc.dram_tensor("x", [TL, D], F32, kind="ExternalInput")
    k.wq_d = nc.dram_tensor("wq", [D, D], F32, kind="ExternalInput")
    k.wk_d = nc.dram_tensor("wk", [D, D], F32, kind="ExternalInput")
    k.wv_d = nc.dram_tensor("wv", [D, D], F32, kind="ExternalInput")
    k.wo_d = nc.dram_tensor("wo", [D, D], F32, kind="ExternalInput")
    k.g1_d = nc.dram_tensor("g1", [D, g.HD], F32, kind="ExternalInput")
    k.g2_d = nc.dram_tensor("g2", [g.HD, D], F32, kind="ExternalInput")
    k.gamma_d = nc.dram_tensor("gamma", [1, D], F32, kind="ExternalInput")
    k.beta_d = nc.dram_tensor("beta", [1, D], F32, kind="ExternalInput")
    k.theta_d = nc.dram_tensor("theta", [g.H * g.HD, 1], F32, kind="ExternalInput")
    k.rw_d = nc.dram_tensor("rw", [g.E, D], F32, kind="ExternalInput")
    k.rg_d = nc.dram_tensor("rg", [g.IR, D], F32, kind="ExternalInput")
    k.ru_d = nc.dram_tensor("ru", [g.IR, D], F32, kind="ExternalInput")
    k.rd_d = nc.dram_tensor("rd", [g.IR, D], F32, kind="ExternalInput")
    k.sg_d = nc.dram_tensor("sg", [g.IS, D], F32, kind="ExternalInput")
    k.su_d = nc.dram_tensor("su", [g.IS, D], F32, kind="ExternalInput")
    k.sd_d = nc.dram_tensor("sd", [g.IS, D], F32, kind="ExternalInput")
    k.ltcol_d = nc.dram_tensor("ltcol", [g.NC, 1], F32, kind="ExternalInput")
    k.myrow_d = nc.dram_tensor("myrow", [1, g.E], F32, kind="ExternalInput")
    k.ltrow_d = nc.dram_tensor("ltrow", [1, g.E], F32, kind="ExternalInput")
    k.cb_d = nc.dram_tensor("cb", [128, 1], F32, kind="ExternalInput")
    k.out_d = nc.dram_tensor("out", [TL, D], F32, kind="ExternalOutput")

    k.trig_d = nc.dram_tensor("trig", [g.PAIRS * 128, 2 * S], BF16)
    k.otT_d = nc.dram_tensor("otT", [g.BL * D, S], BF16)
    k.xmid_f_d = nc.dram_tensor("xmid_f", [TL, D], F32)
    k.xmid_b_d = nc.dram_tensor("xmid_b", [TL, D], BF16)
    k.xmT_d = nc.dram_tensor("xmT", [D, TL], BF16)
    k.sg_b_d = nc.dram_tensor("sg_b", [g.IS, D], BF16)
    k.su_b_d = nc.dram_tensor("su_b", [g.IS, D], BF16)
    k.sd_b_d = nc.dram_tensor("sd_b", [g.IS, D], BF16)
    k.rg_b_d = nc.dram_tensor("rg_b", [g.IR, D], BF16)
    k.ru_b_d = nc.dram_tensor("ru_b", [g.IR, D], BF16)
    k.rd_b_d = nc.dram_tensor("rd_b", [g.IR, D], BF16)
    k.xma_d = nc.dram_tensor("xma", [g.NC * TL, D], BF16)
    k.tid_d = nc.dram_tensor("tid", [2 * TL, 1], F32)
    k.tidall_d = nc.dram_tensor("tidall", [g.NC * 2 * TL, 1], F32)
    k.cnt_in_d = nc.dram_tensor("cnt_in", [1, g.E], F32)
    k.cnt_all_d = nc.dram_tensor("cnt_all", [g.NC, g.E], F32)
    k.eout_f_d = nc.dram_tensor("eout_f", [g.CAP, D], F32)
    k.eout_b_d = nc.dram_tensor("eout_b", [g.CAP, D], BF16)
    k.eall_d = nc.dram_tensor("eall", [g.NC * g.CAP, D], BF16)
    k.shr_d = nc.dram_tensor("shr", [TL, D], F32)

    sb = lambda name, shape, dt: es.enter_context(nc.sbuf_tensor(name, shape, dt))
    k.ident_bf = sb("ident_bf", [128, 128], BF16)
    k.ident_f = sb("ident_f", [128, 128], F32)
    k.tri_f = sb("tri_f", [128, 128], F32)
    k.ones_f = sb("ones_f", [128, 512], F32)
    k.ones_bf = sb("ones_bf", [128, 512], BF16)
    k.masks_bf = sb("masks_bf", [128, 4 * g.QW], BF16)
    k.iota8 = sb("iota8", [128, g.E], F32)
    k.e8mi = sb("e8mi", [128, g.E], F32)
    k.logits = sb("logits", [128, g.E * g.NT], F32)
    k.selbuf = sb("selbuf", [128, g.E * g.NT], F32)
    k.rankbuf = sb("rankbuf", [128, g.E * g.NT], F32)
    k.i1c = sb("i1c", [128, g.NT], F32)
    k.i2c = sb("i2c", [128, g.NT], F32)
    k.rw1c = sb("rw1c", [128, g.NT], F32)
    k.rw2c = sb("rw2c", [128, g.NT], F32)
    k.sidx1 = sb("sidx1", [128, g.NT], I32)
    k.sidx2 = sb("sidx2", [128, g.NT], I32)
    k.gidx1 = sb("gidx1", [128, g.NT], I32)
    k.gidx2 = sb("gidx2", [128, g.NT], I32)
    k.rwv1 = sb("rwv1", [128, g.NT], F32)
    k.rwv2 = sb("rwv2", [128, g.NT], F32)
    k.g1_sl = sb("g1_sl", [128, FC * g.HD], BF16)
    k.g2_sb = sb("g2_sb", [g.HD, D], BF16)
    k.rwT_sb = sb("rwT_sb", [128, FC * g.E], BF16)
    k.gam_b = sb("gam_b", [128, D], F32)
    k.bet_b = sb("bet_b", [128, D], F32)
    k.theta_c = sb("theta_c", [128, 1], F32)
    k.tw0 = sb("tw0", [128, D], F32)
    k.tw1 = sb("tw1", [128, D], F32)
    k.tw2 = sb("tw2", [128, D], F32)
    k.gth0 = sb("gth0", [128, D], BF16)
    k.gth1 = sb("gth1", [128, D], BF16)
    k.st0 = sb("st0", [128, 1], F32)
    k.st1 = sb("st1", [128, 1], F32)
    k.st2 = sb("st2", [128, 1], F32)
    k.st3 = sb("st3", [128, 1], F32)
    k.st4 = sb("st4", [128, 1], F32)
    k.epsln_c = sb("epsln_c", [128, 1], F32)
    k.epsrms_c = sb("epsrms_c", [128, 1], F32)
    k.r8a = sb("r8a", [128, g.E], F32)
    k.r8b = sb("r8b", [128, g.E], F32)
    k.r8c = sb("r8c", [128, g.E], F32)
    k.r8d = sb("r8d", [128, g.E], F32)
    k.cnt_sb = sb("cnt_sb", [g.E, max(g.NT, g.E)], F32)
    k.cntT_sb = sb("cntT_sb", [max(g.NT, g.E), g.E], F32)
    k.acnt_sb = sb("acnt_sb", [g.NC, g.E], F32)
    k.colA = sb("colA", [g.NC, 1], F32)
    k.colB = sb("colB", [g.NC, 1], F32)
    k.colC = sb("colC", [g.NC, 1], F32)
    k.colD = sb("colD", [g.NC, 1], F32)
    k.colE = sb("colE", [g.NC, 1], F32)
    k.brow = sb("brow", [128, 8 * g.E], F32)
    k.ricmp = sb("ricmp", [128, g.NCT], F32)
    k.icmpf = sb("icmpf", [128, g.NCT], F32)
    k.icmpw = sb("icmpw", [128, g.NCT], F32)
    k.icmp = sb("icmp", [128, g.NCT], I32)
    k.ftmp = sb("ftmp", [128, 2 * 512], F32)
    k.tabA = sb("tabA", [128, S], BF16)
    k.tabB = sb("tabB", [128, S], BF16)
    k.cbcol = sb("cbcol", [128, 1], F32)
    k.tidcol = sb("tidcol", [128, 8], F32)
    k.tidv = sb("tidv", [128, 2], F32)
    k.tidvi = sb("tidvi", [128, 2], I32)

    k.psBIG1 = es.enter_context(nc.psum_tensor("psBIG1", [128, 1024], F32))
    k.psBIG2 = es.enter_context(nc.psum_tensor("psBIG2", [128, 1024], F32))
    k.psMED1 = es.enter_context(nc.psum_tensor("psMED1", [128, 512], F32))
    k.psMED2 = es.enter_context(nc.psum_tensor("psMED2", [128, 512], F32))
    k.psSM1 = es.enter_context(nc.psum_tensor("psSM1", [128, 128], BF16))
    k.psSM2 = es.enter_context(nc.psum_tensor("psSM2", [128, 128], BF16))

    k.sP = Sem(es.enter_context(nc.semaphore("sP")))
    k.sV = Sem(es.enter_context(nc.semaphore("sV")))
    k.sA = Sem(es.enter_context(nc.semaphore("sA")))
    k.sG = Sem(es.enter_context(nc.semaphore("sG")))
    k.sY = Sem(es.enter_context(nc.semaphore("sY")))
    k._es = es
    k.stS = Stream(k, "qs", k.sY)
    k.stG = Stream(k, "qg", k.sG)
    k.sPC = Sem(es.enter_context(nc.semaphore("sPC")))
    k.ccCnt = Sem(es.enter_context(nc.semaphore("ccCnt")))
    k.ccXg = Sem(es.enter_context(nc.semaphore("ccXg")))
    k.ccEo = Sem(es.enter_context(nc.semaphore("ccEo")))

    k.block = es.enter_context(nc.Block())

    with es:
        emit_setup(k)
        with nc.sbuf_tensor("trig_f", [128, 3 * S], F32) as trig_f, \
             nc.sbuf_tensor("tabGen", [128, 2 * S], BF16) as tabGen:
            k.trig_f = trig_f
            k.tabGen = tabGen
            emit_tables(k)
        k.bufA = es.enter_context(nc.sbuf_tensor("bufA", [128, g.BUFA], BF16))
        with contextlib.ExitStack() as es_a:
            asb = lambda name, shape, dt: es_a.enter_context(nc.sbuf_tensor(name, shape, dt))
            k.qlrA = asb("qlrA", [128, S], BF16)
            k.qlrB = asb("qlrB", [128, S], BF16)
            k.klrA = asb("klrA", [128, S], BF16)
            k.klrB = asb("klrB", [128, S], BF16)
            k.vpr = asb("vpr", [128, S], BF16)
            k.wv_sl = asb("wv_sl", [128, FC * 128], BF16)
            k.stbf = asb("stbf", [128, 2 * g.QW], BF16)
            k.ftqkA = asb("ftqkA", [128, 512], BF16)
            k.ftqkB = asb("ftqkB", [128, 512], BF16)
            k.gxg1 = asb("gxg1", [g.HD, S], BF16)
            k.wo_sb = asb("wo_sb", [128, FC * D], BF16)
            k.otld = asb("otld", [128, 2 * FC * 128], BF16)
            k.o_ev = k.otld
            k.S_f32 = [asb("sf0", [128, g.HD], F32), asb("sf1", [128, g.HD], F32)]
            k.S_bf = [asb("sb0", [128, g.HD], BF16), asb("sb1", [128, g.HD], BF16)]
            k.xres0 = asb("xres0", [128, D], BF16)
            k.ast = asb("ast", [128, 16], F32)
            for b in range(g.BL):
                emit_a1(k, b)
                emit_gxg1(k, b)
                for hp in range(g.PAIRS):
                    emit_pair(k, b, hp)
                emit_a4(k, b)
                emit_ag_xmid(k, b)
        with contextlib.ExitStack() as es_f:
            fsb = lambda name, shape, dt: es_f.enter_context(nc.sbuf_tensor(name, shape, dt))
            k.rsc = fsb("rsc", [128, 160], F32)
            emit_routing(k)
            emit_scatter(k)
            emit_ag_xg(k)
            k.xslab = fsb("xslab", [128, g.XSLAB], BF16)
            k.gth2 = fsb("gth2", [128, D], BF16)
            k.gth3 = fsb("gth3", [128, D], BF16)
            k.wg_sl = fsb("wg_sl", [128, 2 * FC * 128], BF16)
            k.wu_sl = fsb("wu_sl", [128, 2 * FC * 128], BF16)
            k.wd_sl = fsb("wd_sl", [128, 16 * D], BF16)
            k.gidx_jobs = list(range(g.NT))
            emit_glu(k, "shared")
            if k.gidx_jobs:
                emit_gidx(k, list(k.gidx_jobs))
                k.gidx_jobs = []
            emit_glu(k, "expert")
        emit_phase_c(k)
        emit_final(k)
    return nc



def emit_setup(k):
    nc, g = k.nc, k.g
    D, FC = g.D, g.FC

    def pool(gp):
        k.sG.inc(gp.memset(k.ones_f[:, :], 1.0))
        k.sG.inc(gp.memset(k.ones_bf[:, :], 1.0))
        gp.wait_ge(k.sG.h, k.sG.n)
        k.sG.inc(gp.affine_select(k.ident_bf[:, :], k.ones_bf[:, 0:128],
                                  [[1, 128]], ALU.is_equal, 0.0,
                                  base=0, channel_multiplier=-1))
        k.sG.inc(gp.affine_select(k.ident_f[:, :], k.ones_f[:, 0:128],
                                  [[1, 128]], ALU.is_equal, 0.0,
                                  base=0, channel_multiplier=-1))
        k.sG.inc(gp.affine_select(k.tri_f[:, :], k.ones_f[:, 0:128],
                                  [[1, 128]], ALU.is_gt, 0.0,
                                  base=0, channel_multiplier=-1))
        for d in range(4):
            k.sG.inc(gp.affine_select(k.masks_bf[:, d * g.QW:(d + 1) * g.QW],
                                      k.ones_bf[:, 0:g.QW],
                                      [[1, g.QW]], ALU.is_ge, 0.0,
                                      base=-128 * d, channel_multiplier=-1))
        k.sG.inc(gp.iota(k.iota8[:, :], [[1, g.E]], base=0, channel_multiplier=0,
                         allow_small_or_imprecise_dtypes=True))
        k.sG.inc(gp.memset(k.epsln_c[:, :], EPS_LN))
        k.sG.inc(gp.memset(k.epsrms_c[:, :], EPS_RMS))
        k.m_small_w = k.stG.batch(gp, lambda e, ss: [
            dma(e, ss, out=schunk(k.g1_sl[:, :], g.HD), in_=dchunk(k.g1_d.ap())),
            dma(e, ss, out=k.g2_sb[:, :], in_=k.g2_d[:, :]),
            dma(e, ss, out=k.cbcol[:, :], in_=k.cb_d[:, :]),
            dma(e, ss, out=k.gth0[0:g.E, 0:D], in_=k.rw_d[:, :])])

    sec(k, "gpsimd", pool)
    k.m_consts = k.sG.n

    k.precast_jobs = []
    for src, dst in [(k.sg_d, k.sg_b_d), (k.su_d, k.su_b_d), (k.sd_d, k.sd_b_d),
                     (k.rg_d, k.rg_b_d), (k.ru_d, k.ru_b_d), (k.rd_d, k.rd_b_d)]:
        for r0, rw_ in widths(src.shape[0], 1024):
            k.precast_jobs.append((src, dst, r0, rw_))

    def sy(s):
        k.m_setup_rows = k.stS.batch(s, lambda e, ss: [
            dma(e, ss, out=k.tw1[0:1, 0:D], in_=k.gamma_d[:, :]),
            dma(e, ss, out=k.brow[0:1, 0:g.E], in_=k.myrow_d[:, :]),
            dma(e, ss, out=k.brow[0:1, g.E:2 * g.E], in_=k.ltrow_d[:, :])])

    sec(k, "sync", sy)

    def pe1(t):
        t.wait_ge(k.sY.h, k.m_setup_rows)
        t.wait_ge(k.sG.h, 2)
        for h0, w in widths(D, 512):
            mm = t.matmul(k.psBIG1[:, h0:h0 + w], k.ones_f[0:1, 0:128],
                          k.tw1[0:1, h0:h0 + w], start=True, stop=True)
        k.sP.inc(mm)

    sec(k, "tensor", pe1)
    m_gb = k.sP.n

    def dv1(v):
        v.wait_ge(k.sP.h, m_gb)
        k.sV.inc(v.tensor_copy(k.gam_b[:, :], k.psBIG1[:, 0:D]))

    sec(k, "vector", dv1)
    m_gamv = k.sV.n

    def sy2(s):
        s.wait_ge(k.sV.h, m_gamv)
        k.m_beta = k.stS.batch(s, lambda e, ss: [
            dma(e, ss, out=k.tw1[0:1, 0:D], in_=k.beta_d[:, :])])

    sec(k, "sync", sy2)

    def pe2(t):
        t.wait_ge(k.sY.h, k.m_beta)
        for h0, w in widths(D, 512):
            mm = t.matmul(k.psBIG1[:, h0:h0 + w], k.ones_f[0:1, 0:128],
                          k.tw1[0:1, h0:h0 + w], start=True, stop=True)
        k.sP.inc(mm)

    sec(k, "tensor", pe2)
    m_bb = k.sP.n

    def dv2(v):
        v.wait_ge(k.sP.h, m_bb)
        k.sV.inc(v.tensor_copy(k.bet_b[:, :], k.psBIG1[:, 0:D]))
        v.wait_ge(k.sG.h, k.m_consts)
        v.wait_ge(k.sV.h, k.sV.n)
        k.sV.inc(v.tensor_scalar(k.e8mi[:, :], k.iota8[:, :], -1.0, float(g.E),
                                 ALU.mult, ALU.add))

    sec(k, "vector", dv2)

    def pool2(gp):
        gp.wait_ge(k.sP.h, m_bb)
        k.m_wo = k.stG.batch(gp, lambda e, ss: [
            dma(e, ss, out=schunk(k.wo_sb[:, :], g.D), in_=dchunk(k.wo_d.ap()))])

    k.pool2 = pool2

    p0 = k.sP.n
    v0 = k.sV.n

    def pe3(t):
        t.wait_ge(k.sG.h, k.m_small_w)
        t.wait_ge(k.sG.h, 3)
        for c in range(FC):
            if c >= 2:
                t.wait_ge(k.sV.h, v0 + (c - 1))
            ps = k.psSM1 if c % 2 == 0 else k.psSM2
            mm = t.transpose(ps[:, 0:g.E], k.gth0[0:g.E, c * 128:(c + 1) * 128],
                             k.ident_bf[0:g.E, 0:g.E])
            k.sP.inc(mm)

    def dv3(v):
        for c in range(FC):
            ps = k.psSM1 if c % 2 == 0 else k.psSM2
            v.wait_ge(k.sP.h, p0 + c + 1)
            k.sV.inc(v.tensor_copy(k.rwT_sb[:, c * g.E:(c + 1) * g.E], ps[:, 0:g.E]))

    sec(k, "tensor", pe3)
    sec(k, "vector", dv3)
    k.m_setup = marks(k)



def emit_tables(k):
    nc, g = k.nc, k.g
    S = g.S

    for hp in range(g.PAIRS):
        m_store_prev = k.sY.n

        def sy(s, hp=hp):
            if hp > 0:
                s.wait_ge(k.sV.h, k.m_tab_thused)
            k.m_th = k.stS.batch(s, lambda e, ss: [
                dma(e, ss, out=k.theta_c[:, :],
                    in_=k.theta_d[hp * 128:(hp + 1) * 128, :])])

        sec(k, "sync", sy)
        m_th = None

        def pool(gp, hp=hp):
            if hp > 0:
                gp.wait_ge(k.sV.h, k.m_tab_angused)
            k.sG.inc(gp.iota(k.trig_f[:, 0:S], [[1, S]], base=0, channel_multiplier=0,
                             allow_small_or_imprecise_dtypes=True))

        sec(k, "gpsimd", pool)
        m_io = k.sG.n
        a0 = k.sA.n

        def dv(v, hp=hp, m_io=m_io, a0=a0):
            A = k.trig_f[:, 0:S]
            U = k.trig_f[:, S:2 * S]
            KK = k.trig_f[:, 2 * S:3 * S]

            def step(inst):
                k.sV.inc(inst)
                v.wait_ge(k.sV.h, k.sV.n)
            v.wait_ge(k.sG.h, m_io)
            v.wait_ge(k.sY.h, k.m_th)
            if a0 > 0:
                v.wait_ge(k.sA.h, a0)
            step(v.tensor_scalar(A, A, k.theta_c[:, :], None, ALU.mult))
            k.m_tab_thused = k.sV.n
            step(v.tensor_scalar(U, A, INV_2PI, MAGIC, ALU.mult, ALU.add))
            step(v.tensor_scalar(KK, U, -MAGIC, None, ALU.add))
            step(v.scalar_tensor_tensor(U, KK, -TWO_PI, A, ALU.mult, ALU.add))
            step(v.tensor_scalar(U, U, PI_LO, None, ALU.min))
            step(v.tensor_scalar(U, U, -PI_LO, None, ALU.max))
            k.m_tab_vsin = k.sV.n
            v.wait_ge(k.sA.h, a0 + 1)
            step(v.tensor_scalar(U, A, INV_2PI, 0.25, ALU.mult, ALU.add))
            step(v.tensor_scalar(U, U, MAGIC, None, ALU.add))
            step(v.tensor_scalar(KK, U, -MAGIC, -0.25, ALU.add, ALU.add))
            step(v.scalar_tensor_tensor(U, KK, -TWO_PI, A, ALU.mult, ALU.add))
            k.m_tab_angused = k.sV.n
            step(v.tensor_scalar(U, U, PI_LO, None, ALU.min))
            step(v.tensor_scalar(U, U, -PI_LO, None, ALU.max))
            k.m_tab_vcos = k.sV.n

        def ac(s, hp=hp, m_store_prev=m_store_prev):
            s.wait_ge(k.sV.h, k.m_tab_vsin)
            if hp > 0:
                s.wait_ge(k.sY.h, m_store_prev)
            k.sA.inc(s.activation(k.tabGen[:, S:2 * S], k.trig_f[:, S:2 * S], AF.Sin))
            s.wait_ge(k.sV.h, k.m_tab_vcos)
            k.sA.inc(s.activation(k.tabGen[:, 0:S], k.trig_f[:, S:2 * S], AF.Sin))

        sec(k, "vector", dv)
        sec(k, "scalar", ac)
        m_sin = k.sA.n

        def sy2(s, hp=hp, m_sin=m_sin):
            s.wait_ge(k.sA.h, m_sin)
            k.stS.batch(s, lambda e, ss: [
                dma(e, ss, out=k.trig_d[hp * 128:(hp + 1) * 128, :], in_=k.tabGen[:, :])])

        sec(k, "sync", sy2)
    k.m_tables_sy = k.sY.n
    k.m_tables_sA = k.sA.n
    k.m_tables_sV = k.sV.n



def emit_a1(k, b):
    nc, g = k.nc, k.g
    S, FC = g.S, g.FC
    if b == 0:
        sec(k, "gpsimd", k.pool2)
        guard = dict(sP=k.m_setup["sP"], sV=k.m_setup["sV"])
    else:
        guard = dict(k.m_seq_end)

    d_base = k.sG.n
    p_base = k.sP.n
    v_base = k.sV.n
    k.m_a1_ld = {}

    def pool(gp, b=b, guard=guard):
        wait_marks(k, gp, guard)
        for nb in range(g.NBS):
            if nb >= 2:
                gp.wait_ge(k.sP.h, p_base + (nb - 1) * FC)
            dst = k.gth0 if nb % 2 == 0 else k.gth1
            k.m_a1_ld[nb] = k.stG.batch(gp, lambda e, ss, dst=dst, nb=nb: [
                dma(e, ss, out=dst[:, :],
                    in_=k.x_d[b * S + nb * 128: b * S + (nb + 1) * 128, :])])

    def pe(t, b=b):
        t.wait_ge(k.sG.h, 3)
        if b == 0:
            t.wait_ge(k.sA.h, k.m_tables_sA)
        for nb in range(g.NBS):
            t.wait_ge(k.sG.h, k.m_a1_ld[nb])
            src = k.gth0 if nb % 2 == 0 else k.gth1
            for c in range(FC):
                idx = nb * FC + c
                ps = k.psSM1 if idx % 2 == 0 else k.psSM2
                if idx >= 2:
                    t.wait_ge(k.sV.h, v_base + idx - 1)
                mm = t.transpose(ps[:, :], src[:, c * 128:(c + 1) * 128], k.ident_bf[:, :])
                k.sP.inc(mm)

    def dv(v, b=b):
        if b == 0:
            v.wait_ge(k.sA.h, k.m_tables_sA)
            v.wait_ge(k.sY.h, k.m_tables_sy)
        for nb in range(g.NBS):
            for c in range(FC):
                idx = nb * FC + c
                ps = k.psSM1 if idx % 2 == 0 else k.psSM2
                v.wait_ge(k.sP.h, p_base + idx + 1)
                k.sV.inc(v.tensor_copy(bA(k, c * S + nb * 128, 128), ps[:, :]))

    sec(k, "gpsimd", pool)
    sec(k, "tensor", pe)
    sec(k, "vector", dv)
    k.m_xbt = marks(k)



def emit_gxg1(k, b):
    nc, g = k.nc, k.g
    S, FC, HD = g.S, g.FC, g.HD
    p_base = k.sP.n
    v_base = k.sV.n
    tiles = widths(S, 512)

    def pe(t, b=b):
        t.wait_ge(k.sV.h, k.m_xbt["sV"])
        t.wait_ge(k.sG.h, k.m_small_w)
        for i, (c0, w) in enumerate(tiles):
            if i >= 2:
                t.wait_ge(k.sV.h, v_base + i - 1)
            ps = k.psMED1 if i % 2 == 0 else k.psMED2
            for c in range(FC):
                mm = t.matmul(ps[0:HD, 0:w], k.g1_sl[:, c * HD:(c + 1) * HD],
                              bA(k, c * S + c0, w), start=(c == 0), stop=(c == FC - 1))
            k.sP.inc(mm)

    def dv(v, b=b):
        for i, (c0, w) in enumerate(tiles):
            ps = k.psMED1 if i % 2 == 0 else k.psMED2
            v.wait_ge(k.sP.h, p_base + i + 1)
            k.sV.inc(v.tensor_copy(k.gxg1[0:HD, c0:c0 + w], ps[0:HD, 0:w]))

    sec(k, "tensor", pe)
    sec(k, "vector", dv)
    k.m_gxg1 = marks(k)



def emit_pair(k, b, hp):
    nc, g = k.nc, k.g
    S, FC, QW = g.S, g.FC, g.QW
    ntiles = S // 512
    m_prev = getattr(k, "m_pair_cons", None)

    def pool(gp, b=b, hp=hp):
        if m_prev is not None:
            wait_marks(k, gp, dict(sP=m_prev["sP"]))
        else:
            wait_marks(k, gp, dict(sP=k.m_xbt["sP"]))
        def em(e, ss):
            for w_d, dst in [(k.wq_d, k.gth0), (k.wk_d, k.gth1),
                             (k.wv_d, k.wv_sl)]:
                dsl = dst[:, 0:FC * 128] if dst is k.wv_sl else dst[:, :]
                dma(e, ss, out=dsl,
                    in_=w_d[hp * 128:(hp + 1) * 128, :])
        k.m_pair_w = k.stG.batch(gp, em)
        for src, dst, r0, rw_ in k.precast_jobs[:4]:
            dma(gp, k.sPC, out=dst[r0:r0 + rw_, :], in_=src[r0:r0 + rw_, :])
        del k.precast_jobs[:4]

    sec(k, "gpsimd", pool)

    def syt(s, b=b, hp=hp):
        if m_prev is not None:
            wait_marks(k, s, dict(sV=m_prev["sV"]))
        def em(e, ss):
            for a, tab in [(0, k.tabA), (1, k.tabB)]:
                r = hp * 128 + a * 64
                dma(e, ss, out=tab[0:64, 0:S], in_=k.trig_d[r:r + 64, 0:S])
                dma(e, ss, out=tab[64:128, 0:S], in_=k.trig_d[r:r + 64, S:2 * S])
        k.m_pair_t = k.stS.batch(s, em)

    sec(k, "sync", syt)

    p_base = k.sP.n
    a_base = k.sA.n
    v_base = k.sV.n
    nproj_v = 6 * (2 * ntiles)

    def pe(t, b=b, hp=hp):
        t.wait_ge(k.sG.h, k.m_pair_w)
        t.wait_ge(k.sV.h, k.m_xbt["sV"])
        if m_prev is not None:
            t.wait_ge(k.sV.h, m_prev["sV"])
        else:
            t.wait_ge(k.sV.h, k.m_gxg1["sV"])
        for j in range(2 * ntiles):
            iqk, nt = divmod(j, ntiles)
            wsl = k.gth0 if iqk == 0 else k.gth1
            if j >= 2:
                t.wait_ge(k.sA.h, a_base + j - 1)
            ps = k.psMED1 if j % 2 == 0 else k.psMED2
            for c in range(FC):
                mm = t.matmul(ps[:, 0:512], wsl[:, c * 128:(c + 1) * 128],
                              bA(k, c * S + nt * 512, 512),
                              start=(c == 0), stop=(c == FC - 1))
            k.sP.inc(mm)
        for mt in range(S // 128):
            reg = [k.psBIG1[:, 0:128], k.psBIG1[:, 512:640],
                   k.psBIG2[:, 0:128], k.psBIG2[:, 512:640]][mt % 4]
            if mt >= 4:
                t.wait_ge(k.sV.h, v_base + nproj_v + (mt - 4) + 1)
            for c in range(FC):
                mm = t.matmul(reg, bA(k, c * S + mt * 128, 128),
                              k.wv_sl[:, c * 128:(c + 1) * 128],
                              start=(c == 0), stop=(c == FC - 1))
            k.sP.inc(mm)

    def ac(s, b=b, hp=hp):
        for j in range(2 * ntiles):
            ps = k.psMED1 if j % 2 == 0 else k.psMED2
            ft = k.ftmp[:, 0:512] if j % 2 == 0 else k.ftmp[:, 512:1024]
            s.wait_ge(k.sP.h, p_base + j + 1)
            if j >= 2:
                s.wait_ge(k.sV.h, v_base + 6 * (j - 1))
            k.sA.inc(s.activation(ft, ps[:, 0:512], AF.Silu))

    def dv(v, b=b, hp=hp):
        v.wait_ge(k.sY.h, k.m_pair_t)
        if m_prev is not None:
            v.wait_ge(k.sP.h, m_prev["sP"])
        for j in range(2 * ntiles):
            iqk, nt = divmod(j, ntiles)
            ft = k.ftmp[:, 0:512] if j % 2 == 0 else k.ftmp[:, 512:1024]
            dstA = k.qlrA if iqk == 0 else k.klrA
            dstB = k.qlrB if iqk == 0 else k.klrB
            v.wait_ge(k.sA.h, a_base + j + 1)
            v.wait_ge(k.sV.h, k.sV.n)
            sl = slice(nt * 512, (nt + 1) * 512)
            fA = k.ftqkA[:, :]
            fB = k.ftqkB[:, :]
            k.sV.inc(v.tensor_copy(fA[0:64, :], ft[0:64, :]))
            k.sV.inc(v.tensor_copy(fA[64:128, :], ft[0:64, :]))
            k.sV.inc(v.tensor_copy(fB[0:64, :], ft[64:128, :]))
            k.sV.inc(v.tensor_copy(fB[64:128, :], ft[64:128, :]))
            v.wait_ge(k.sV.h, k.sV.n)
            k.sV.inc(v.tensor_mul(dstA[:, sl], fA, k.tabA[:, sl]))
            k.sV.inc(v.tensor_mul(dstB[:, sl], fB, k.tabB[:, sl]))
        for mt in range(S // 128):
            reg = [k.psBIG1[:, 0:128], k.psBIG1[:, 512:640],
                   k.psBIG2[:, 0:128], k.psBIG2[:, 512:640]][mt % 4]
            v.wait_ge(k.sP.h, p_base + 2 * ntiles + mt + 1)
            k.sV.inc(v.tensor_copy(k.vpr[:, mt * 128:(mt + 1) * 128], reg))

    sec(k, "tensor", pe)
    sec(k, "scalar", ac)
    sec(k, "vector", dv)
    m_proj = marks(k)

    k.m_prev_head_v = m_proj["sV"]
    emit_heads(k, b, hp)

    k.m_pair_cons = marks(k)


def emit_heads(k, b, hp):
    nc, g = k.nc, k.g
    S, QW, NQ = g.S, g.QW, g.NQ
    NB = 2 * NQ
    prev_v = k.m_prev_head_v
    p_start = k.sP.n
    a_start = k.sA.n
    v0 = k.sV.n
    if not hasattr(k, "oev_store_mark"):
        k.oev_store_mark = {}

    def M_MASK(q, d):
        return v0 + 11 * q + 1 + d

    def M_KT(q, d):
        return v0 + 11 * q + 5 + d

    def M_OEV(q):
        return v0 + 11 * q + 9

    def M_SADD(q):
        return v0 + 11 * q + 10

    def M_SBF(q):
        return v0 + 11 * q + 11

    pE, pT, pAV, pS, pO = {}, {}, {}, {}, {}
    mOEV = {}

    def opsum_of(q):
        nt, a = q // 2, q % 2
        bank = k.psBIG1 if nt % 2 == 0 else k.psBIG2
        return bank[0:64, a * QW:(a + 1) * QW]

    for q in range(NB):
        nt, a = q // 2, q % 2
        qlr = k.qlrA if a == 0 else k.qlrB
        klr = k.klrA if a == 0 else k.klrB

        def pe(t, q=q, nt=nt, a=a, qlr=qlr, klr=klr):
            qsl = qlr[:, nt * QW:(nt + 1) * QW]
            opsum = opsum_of(q)
            if q == 0:
                t.wait_ge(k.sV.h, prev_v)
            if nt > 0:
                t.wait_ge(k.sV.h, M_SBF(q - 2))
                if nt >= 2:
                    t.wait_ge(k.sV.h, M_OEV(q - 4))
                mm = t.matmul(opsum, k.S_bf[a][:, :], qsl, start=True, stop=False)
                k.sP.inc(mm)
            for d in range(4):
                mb = 4 * nt + d
                psE = k.psMED1 if d % 2 == 0 else k.psMED2
                if d == 0 and q >= 1:
                    t.wait_ge(k.sV.h, M_MASK(q - 1, 2))
                if d == 1 and q >= 1:
                    t.wait_ge(k.sV.h, M_MASK(q - 1, 3))
                    t.wait_ge(k.sV.h, M_SADD(q - 1))
                if d >= 2:
                    t.wait_ge(k.sV.h, M_MASK(q, d - 2))
                mm = t.matmul(psE[:, 0:QW], klr[:, mb * 128:(mb + 1) * 128],
                              qsl, start=True, stop=True)
                k.sP.inc(mm)
                pE[(q, d)] = k.sP.n
            for d in range(4):
                mb = 4 * nt + d
                stb = k.stbf[:, 0:QW] if d % 2 == 0 else k.stbf[:, QW:2 * QW]
                t.wait_ge(k.sV.h, M_MASK(q, d))
                mm = t.matmul(opsum,
                              k.vpr[:, mb * 128 + a * 64: mb * 128 + a * 64 + 64],
                              stb, start=(nt == 0 and d == 0), stop=(d == 3))
                k.sP.inc(mm)
                pAV[(q, d)] = k.sP.n
            pO[q] = k.sP.n
            for d in range(4):
                mb = 4 * nt + d
                psT = k.psSM1 if d % 2 == 0 else k.psSM2
                if d < 2:
                    if q >= 1:
                        t.wait_ge(k.sV.h, M_KT(q - 1, d + 2))
                else:
                    t.wait_ge(k.sV.h, M_KT(q, d - 2))
                mm = t.transpose(psT[:, :], klr[:, mb * 128:(mb + 1) * 128],
                                 k.ident_bf[:, :])
                k.sP.inc(mm)
                pT[(q, d)] = k.sP.n
                if d >= 1:
                    emit_supd(t, q, nt, a, d - 1)
            emit_supd(t, q, nt, a, 3)

        def emit_supd(t, q, nt, a, d):
            mb = 4 * nt + d
            kts = k.ftqkA[:, 0:128] if d % 2 == 0 else k.ftqkA[:, 128:256]
            if d == 0:
                t.wait_ge(k.sV.h, M_MASK(q, 3))
            t.wait_ge(k.sV.h, M_KT(q, d))
            mm = t.matmul(k.psMED2[:, 0:64], kts,
                          k.vpr[:, mb * 128 + a * 64: mb * 128 + a * 64 + 64],
                          start=(d == 0), stop=(d == 3))
            k.sP.inc(mm)
            pS[(q, d)] = k.sP.n

        def dv(v, q=q, nt=nt, a=a):
            if q == 0:
                v.wait_ge(k.sP.h, p_start)
                v.wait_ge(k.sA.h, a_start)
            for d in range(4):
                psE = k.psMED1 if d % 2 == 0 else k.psMED2
                stb = k.stbf[:, 0:QW] if d % 2 == 0 else k.stbf[:, QW:2 * QW]
                v.wait_ge(k.sP.h, pE[(q, d)])
                if d < 2:
                    if q >= 1:
                        v.wait_ge(k.sP.h, pAV[(q - 1, d + 2)])
                else:
                    v.wait_ge(k.sP.h, pAV[(q, d - 2)])
                k.sV.inc(v.tensor_mul(stb, psE[:, 0:QW],
                                      k.masks_bf[:, d * QW:(d + 1) * QW]))
            for d in range(4):
                psT = k.psSM1 if d % 2 == 0 else k.psSM2
                kts = k.ftqkA[:, 0:128] if d % 2 == 0 else k.ftqkA[:, 128:256]
                v.wait_ge(k.sP.h, pT[(q, d)])
                if d < 2:
                    if q >= 1:
                        v.wait_ge(k.sP.h, pS[(q - 1, d + 2)])
                else:
                    v.wait_ge(k.sP.h, pS[(q, d - 2)])
                k.sV.inc(v.tensor_copy(kts, psT[:, :]))
            slot = a * 2 + nt % 2
            oev = k.o_ev[0:64, slot * QW:(slot + 1) * QW]
            v.wait_ge(k.sP.h, pO[q])
            prevst = k.oev_store_mark.get(slot)
            if prevst:
                v.wait_ge(k.sY.h, prevst)
            k.sV.inc(v.tensor_copy(oev, opsum_of(q)))
            mOEV[q] = k.sV.n
            v.wait_ge(k.sP.h, pS[(q, 3)])
            if nt == 0:
                k.sV.inc(v.tensor_copy(k.S_f32[a][:, :], k.psMED2[:, 0:64]))
            else:
                k.sV.inc(v.tensor_add(k.S_f32[a][:, :], k.S_f32[a][:, :],
                                      k.psMED2[:, 0:64]))
            v.wait_ge(k.sV.h, k.sV.n)
            k.sV.inc(v.tensor_copy(k.S_bf[a][:, :], k.S_f32[a][:, :]))

        def syo(s, q=q, nt=nt, a=a):
            slot = a * 2 + nt % 2
            oev = k.o_ev[0:64, slot * QW:(slot + 1) * QW]
            h = 2 * hp + a
            s.wait_ge(k.sV.h, mOEV[q])
            k.oev_store_mark[slot] = k.stS.batch(s, lambda e, ss: [
                dma(e, ss, out=k.otT_d[b * g.D + h * 64: b * g.D + h * 64 + 64,
                                       nt * QW:(nt + 1) * QW], in_=oev)])

        sec(k, "tensor", pe)
        sec(k, "vector", dv)
        sec(k, "sync", syo)
        assert k.sV.n == v0 + 11 * (q + 1), (k.sV.n, v0, q)
    k.m_prev_head_v = k.sV.n



def emit_a4(k, b):
    nc, g = k.nc, k.g
    S, FC, D, HD = k.g.S, k.g.FC, k.g.D, k.g.HD
    NBS = g.NBS
    m_pair_v = k.m_pair_cons["sV"]
    m_pair_p = k.m_pair_cons["sP"]
    acc = [k.tw0, k.tw1, k.tw2]
    xres = [k.xres0, k.qlrA[:, 0:D]]
    asg = k.ftmp[:, 0:D]
    dumm = k.stbf[:, 0:D]
    ast = k.ast

    def c_s0(y): return ast[:, y % 3: y % 3 + 1]
    def c_s1(y): return ast[:, 3 + y % 3: 4 + y % 3]
    def c_s2(y): return ast[:, 6 + y % 3: 7 + y % 3]
    def c_s3(y): return ast[:, 9 + y % 3: 10 + y % 3]
    def c_rr(y): return ast[:, 12 + y % 2: 13 + y % 2]

    mLD, mXR, mST = {}, {}, {}
    mPW, mPG, mTR, mLG = {}, {}, {}, {}
    mSIG, mSQ1, mSQ2, mSQ3, mQ1, mQ2, mQ3 = {}, {}, {}, {}, {}, {}, {}
    mV6, mV5, mV7, mRC, mSC, mEV, mST1, mST2, mRM = {}, {}, {}, {}, {}, {}, {}, {}, {}

    for r in range(NBS + 7):
        tT, tP, tL = r - 5, r - 2, r - 6

        def pe(t_, r=r, tT=tT, tP=tP, tL=tL, b=b):
            if 0 <= tT < NBS:
                src = k.gth0 if tT % 2 == 0 else k.gth1
                vr_base = k.sV.n
                t_.wait_ge(k.sV.h, mV7[tT])
                for c in range(FC):
                    ps = k.psSM1 if c % 2 == 0 else k.psSM2
                    if c >= 2:
                        t_.wait_ge(k.sV.h, vr_base + (c - 2) + 1)
                    elif tT >= 1:
                        t_.wait_ge(k.sV.h, mEV[tT - 1])
                    mm = t_.transpose(ps[:, :], src[:, c * 128:(c + 1) * 128],
                                      k.ident_bf[:, :])
                    k.sP.inc(mm)
                mTR[tT] = k.sP.n
            if 0 <= tP < NBS:
                half = tP % 2
                t_.wait_ge(k.sY.h, mLD[tP])
                t_.wait_ge(k.sG.h, k.m_wo)
                if tP == 0:
                    t_.wait_ge(k.sV.h, m_pair_v)
                    wait_marks(k, t_, dict(sV=k.m_gxg1["sV"]))
                else:
                    t_.wait_ge(k.sA.h, mSIG[tP - 1])
                    t_.wait_ge(k.sV.h, mV6[tP - 1])
                for h0, w in widths(D, 512):
                    mm = t_.matmul(k.psBIG2[:, h0:h0 + w],
                                   k.gxg1[0:HD, tP * 128:(tP + 1) * 128],
                                   k.g2_sb[0:HD, h0:h0 + w], start=True, stop=True)
                k.sP.inc(mm)
                mPG[tP] = k.sP.n
                for h0, w in widths(D, 512):
                    for c in range(FC):
                        mm = t_.matmul(k.psBIG1[:, h0:h0 + w],
                                       k.otld[:, half * FC * 128 + c * 128:
                                              half * FC * 128 + (c + 1) * 128],
                                       k.wo_sb[:, c * D + h0: c * D + h0 + w],
                                       start=(c == 0), stop=(c == FC - 1))
                k.sP.inc(mm)
                mPW[tP] = k.sP.n
            if 0 <= tL < NBS:
                t_.wait_ge(k.sV.h, mEV[tL])
                if tL >= 2:
                    t_.wait_ge(k.sV.h, mSC[tL - 2])
                ps = k.psMED1 if tL % 2 == 0 else k.psMED2
                for c in range(FC):
                    mm = t_.matmul(ps[:, 0:g.E],
                                   bA(k, c * S + tL * 128, 128),
                                   k.rwT_sb[:, c * g.E:(c + 1) * g.E],
                                   start=(c == 0), stop=(c == FC - 1))
                k.sP.inc(mm)
                mLG[tL] = k.sP.n

        sec(k, "tensor", pe)

        if 0 <= tP < NBS:
            def sa(s, tP=tP):
                s.wait_ge(k.sP.h, mPG[tP])
                if tP >= 1:
                    s.wait_ge(k.sV.h, mV6[tP - 1])
                k.sA.inc(s.activation(asg, k.psBIG2[:, 0:D], AF.Sigmoid))
                mSIG[tP] = k.sA.n

            sec(k, "scalar", sa)

        def va(v, r=r, tT=tT):
            def step(inst):
                k.sV.inc(inst)
                v.wait_ge(k.sV.h, k.sV.n)
            if 0 <= tT < NBS:
                for c in range(FC):
                    ps = k.psSM1 if c % 2 == 0 else k.psSM2
                    v.wait_ge(k.sP.h, mTR[tT] - FC + c + 1)
                    k.sV.inc(v.tensor_copy(bA(k, c * S + tT * 128, 128), ps[:, :]))
                mEV[tT] = k.sV.n
            y = r - 3
            if 0 <= y < NBS:
                v.wait_ge(k.sA.h, mQ1[y])
                step(v.tensor_scalar(c_s0(y), c_s0(y), 1.0 / D, None, ALU.mult))
                step(v.tensor_scalar(c_s1(y), c_s1(y), 1.0 / D, None, ALU.mult))
                step(v.tensor_mul(c_s2(y), c_s0(y), c_s0(y)))
                step(v.tensor_sub(c_s1(y), c_s1(y), c_s2(y)))
                mST1[y] = k.sV.n
            y = r - 4
            if 0 <= y < NBS:
                v.wait_ge(k.sA.h, mQ2[y])
                step(v.tensor_scalar(c_s0(y), c_s0(y), 1.0 / D, None, ALU.mult))
                step(v.tensor_scalar(c_s1(y), c_s1(y), 1.0 / D, None, ALU.mult))
                step(v.tensor_mul(c_s2(y), c_s0(y), c_s0(y)))
                step(v.tensor_sub(c_s1(y), c_s1(y), c_s2(y)))
                mST2[y] = k.sV.n
            y = r - 5
            if 0 <= y < NBS:
                v.wait_ge(k.sA.h, mQ3[y])
                step(v.tensor_scalar(c_s1(y), c_s1(y), 1.0 / D, None, ALU.mult))
                mRM[y] = k.sV.n

        sec(k, "vector", va)

        def sb_(s, r=r):
            y = r - 3
            if 0 <= y < NBS:
                s.wait_ge(k.sV.h, mST1[y])
                k.sA.inc(s.activation(c_s2(y), c_s1(y), AF.Sqrt, bias=k.epsln_c[:, :]))
                mSQ1[y] = k.sA.n
            y = r - 4
            if 0 <= y < NBS:
                s.wait_ge(k.sV.h, mST2[y])
                k.sA.inc(s.activation(c_s2(y), c_s1(y), AF.Sqrt, bias=k.epsln_c[:, :]))
                mSQ2[y] = k.sA.n
            y = r - 5
            if 0 <= y < NBS:
                s.wait_ge(k.sV.h, mRM[y])
                k.sA.inc(s.activation(c_s2(y), c_s1(y), AF.Sqrt, bias=k.epsrms_c[:, :]))
                mSQ3[y] = k.sA.n

        sec(k, "scalar", sb_)

        def vb(v, r=r, b=b):
            def step(inst):
                k.sV.inc(inst)
                v.wait_ge(k.sV.h, k.sV.n)
            y = r - 3
            if 0 <= y < NBS:
                a = acc[y % 3][:, :]
                v.wait_ge(k.sA.h, mSQ1[y])
                v.wait_ge(k.sG.h, mXR[y])
                step(v.reciprocal(c_s2(y), c_s2(y)))
                step(v.tensor_scalar(c_s3(y), c_s0(y), -1.0, None, ALU.mult))
                step(v.tensor_scalar(a, a, c_s3(y), c_s2(y), ALU.add, ALU.mult))
                step(v.tensor_mul(a, a, k.gam_b[:, :]))
                step(v.tensor_add(a, a, k.bet_b[:, :]))
                step(v.tensor_add(a, a, xres[y % 2][:, :]))
                mV5[y] = k.sV.n
                step(v.reduce_sum(c_s0(y), a, mybir.AxisListType.X))
            y = r - 4
            if 0 <= y < NBS:
                a = acc[y % 3][:, :]
                v.wait_ge(k.sA.h, mSQ2[y])
                if y >= 2:
                    v.wait_ge(k.sP.h, mTR[y - 2])
                    v.wait_ge(k.sY.h, mST[y - 2])
                else:
                    v.wait_ge(k.sP.h, m_pair_p)
                step(v.reciprocal(c_s2(y), c_s2(y)))
                step(v.tensor_scalar(c_s3(y), c_s0(y), -1.0, None, ALU.mult))
                step(v.tensor_scalar(a, a, c_s3(y), c_s2(y), ALU.add, ALU.mult))
                dst = k.gth0 if y % 2 == 0 else k.gth1
                step(v.tensor_copy(dst[:, :], a))
                mV7[y] = k.sV.n
            y = r - 5
            if 0 <= y < NBS:
                v.wait_ge(k.sA.h, mSQ3[y])
                if y >= 2:
                    v.wait_ge(k.sV.h, mSC[y - 2])
                step(v.reciprocal(c_rr(y), c_s2(y)))
                mRC[y] = k.sV.n
            y = r - 6
            if 0 <= y < NBS:
                t = b * NBS + y
                ps = k.psMED1 if y % 2 == 0 else k.psMED2
                v.wait_ge(k.sP.h, mLG[y])
                k.sV.inc(v.tensor_scalar(k.logits[:, t * g.E:(t + 1) * g.E],
                                         ps[:, 0:g.E], c_rr(y), None, ALU.mult))
                mSC[y] = k.sV.n
            y = r - 2
            if 0 <= y < NBS:
                a = acc[y % 3][:, :]
                v.wait_ge(k.sA.h, mSIG[y])
                v.wait_ge(k.sP.h, mPW[y])
                if y >= 3:
                    v.wait_ge(k.sY.h, mST[y - 3])
                step(v.tensor_mul(a, k.psBIG1[:, 0:D], asg))
                mV6[y] = k.sV.n
                step(v.reduce_sum(c_s0(y), a, mybir.AxisListType.X))

        sec(k, "vector", vb)

        def sc_(s, r=r):
            y = r - 3
            if 0 <= y < NBS:
                s.wait_ge(k.sV.h, mV5[y])
                k.sA.inc(s.activation(dumm, acc[y % 3][:, :], AF.Square,
                                      accum_out=c_s1(y)))
                mQ2[y] = k.sA.n
            y = r - 4
            if 0 <= y < NBS:
                s.wait_ge(k.sV.h, mV7[y])
                k.sA.inc(s.activation(dumm, acc[y % 3][:, :], AF.Square,
                                      accum_out=c_s1(y)))
                mQ3[y] = k.sA.n
            y = r - 2
            if 0 <= y < NBS:
                s.wait_ge(k.sV.h, mV6[y] + 1)
                k.sA.inc(s.activation(dumm, acc[y % 3][:, :], AF.Square,
                                      accum_out=c_s1(y)))
                mQ1[y] = k.sA.n

        sec(k, "scalar", sc_)

        def sy(s, r=r, b=b):
            y = r
            if 0 <= y < NBS:
                half = y % 2
                if y >= 2:
                    s.wait_ge(k.sP.h, mPW[y - 2])
                mLD[y] = k.stS.batch(s, lambda e, ss, y=y, half=half: [
                    dma(e, ss,
                        out=schunk(k.otld[:, half * FC * 128:(half + 1) * FC * 128],
                                   128),
                        in_=dchunk(k.otT_d.ap()[b * D:(b + 1) * D,
                                                y * 128:(y + 1) * 128]))])
            y = r - 4
            if 0 <= y < NBS:
                t = b * NBS + y
                src = k.gth0 if y % 2 == 0 else k.gth1
                s.wait_ge(k.sV.h, mV7[y])
                mST[y] = k.stS.batch(s, lambda e, ss, t=t, y=y, src=src: [
                    dma(e, ss, out=k.xmid_f_d[t * 128:(t + 1) * 128, :],
                        in_=acc[y % 3][:, :]),
                    dma(e, ss, out=k.xmid_b_d[t * 128:(t + 1) * 128, :],
                        in_=src[:, :])])

        sec(k, "sync", sy)

        def gx(gp, r=r, b=b):
            y = r - 2
            if 0 <= y < NBS:
                if y <= 1:
                    gp.wait_ge(k.sP.h, m_pair_p)
                else:
                    gp.wait_ge(k.sV.h, mV5[y - 2])
                mXR[y] = k.stG.batch(gp, lambda e, ss, y=y: [
                    dma(e, ss, out=xres[y % 2][:, :],
                        in_=k.x_d[b * S + y * 128:(b * S + (y + 1) * 128), :])])

        sec(k, "gpsimd", gx)

    m_last_v = k.sV.n

    def syf(s, b=b):
        s.wait_ge(k.sV.h, m_last_v)
        def em(e, ss):
            for c in range(FC):
                dma(e, ss, out=k.xmT_d[c * 128:(c + 1) * 128, b * S:(b + 1) * S],
                    in_=bA(k, c * S, S))
        k.stS.batch(s, em)

    sec(k, "sync", syf)
    k.m_seq_end = dict(sY=k.sY.n, sV=k.sV.n, sP=k.sP.n, sA=k.sA.n, sG=k.sG.n)



def emit_routing(k):
    nc, g = k.nc, k.g
    NT, E = g.NT, g.E

    def RS(t):
        b0 = (t % 4) * 40
        r = k.rsc
        return {"a": r[:, b0:b0 + 8], "b": r[:, b0 + 8:b0 + 16],
                "c": r[:, b0 + 16:b0 + 24], "d": r[:, b0 + 24:b0 + 32],
                "s0": r[:, b0 + 32:b0 + 33], "s1": r[:, b0 + 33:b0 + 34],
                "s2": r[:, b0 + 34:b0 + 35], "s3": r[:, b0 + 35:b0 + 36],
                "s4": r[:, b0 + 36:b0 + 37]}

    def ilv(v, tiles, oplists, lastm, xwait=None):
        nops = len(oplists[tiles[0]])
        for opi in range(nops):
            for t in tiles:
                if lastm.get(t):
                    v.wait_ge(k.sV.h, lastm[t])
                elif xwait is not None:
                    xwait(t)
                k.sV.inc(oplists[t][opi]())
                lastm[t] = k.sV.n

    k.m_cnt_v = {}
    mV1, mEXP, mV2, mPC = {}, {}, {}, {}
    X_ = mybir.AxisListType.X
    for base in range(0, NT, 4):
        tiles = list(range(base, min(base + 4, NT)))

        def dvA(v, tiles=tiles, base=base):
            if base == 0:
                v.wait_ge(k.sY.h, k.m_seq_end["sY"])
            lastm = {}

            def ops(t):
                R = RS(t)
                L = k.logits[:, t * E:(t + 1) * E]
                return [
                    lambda: v.reduce_max(R["s0"], L, X_),
                    lambda: v.tensor_scalar(R["a"], L, R["s0"], None, ALU.is_equal),
                    lambda: v.tensor_mul(R["b"], R["a"], k.e8mi[:, :]),
                    lambda: v.tensor_scalar(R["b"], R["b"], -1.0, float(E),
                                            ALU.mult, ALU.add),
                    lambda: v.tensor_reduce(R["s1"], R["b"], X_, ALU.min),
                    lambda: v.tensor_scalar(R["a"], k.iota8[:, :], R["s1"],
                                            None, ALU.is_equal),
                    lambda: v.scalar_tensor_tensor(R["c"], R["a"], BIGNEG, L,
                                                   ALU.mult, ALU.add),
                    lambda: v.reduce_max(R["s2"], R["c"], X_),
                    lambda: v.tensor_scalar(R["d"], R["c"], R["s2"], None,
                                            ALU.is_equal),
                    lambda: v.tensor_mul(R["d"], R["d"], k.e8mi[:, :]),
                    lambda: v.tensor_scalar(R["d"], R["d"], -1.0, float(E),
                                            ALU.mult, ALU.add),
                    lambda: v.tensor_reduce(R["s3"], R["d"], X_, ALU.min),
                    lambda: v.tensor_scalar(R["b"], k.iota8[:, :], R["s3"],
                                            None, ALU.is_equal),
                    lambda: v.tensor_add(k.selbuf[:, t * E:(t + 1) * E],
                                         R["a"], R["b"]),
                    lambda: v.tensor_sub(R["s2"], R["s2"], R["s0"]),
                    lambda: v.tensor_copy(k.i1c[:, t:t + 1], R["s1"]),
                    lambda: v.tensor_copy(k.i2c[:, t:t + 1], R["s3"]),
                ]
            ilv(v, tiles, {t: ops(t) for t in tiles}, lastm)
            for t in tiles:
                mV1[t] = lastm[t]

        sec(k, "vector", dvA)

        def acA(s, tiles=tiles):
            for t in tiles:
                s.wait_ge(k.sV.h, mV1[t])
                R = RS(t)
                k.sA.inc(s.activation(R["s2"], R["s2"], AF.Exp))
                mEXP[t] = k.sA.n

        sec(k, "scalar", acA)

        def dvB(v, tiles=tiles):
            lastm = {}

            def ops(t):
                R = RS(t)
                return [
                    lambda: v.tensor_scalar(R["s2"], R["s2"], 1.0, None, ALU.add),
                    lambda: v.reciprocal(R["s2"], R["s2"]),
                    lambda: v.tensor_copy(k.rw1c[:, t:t + 1], R["s2"]),
                    lambda: v.tensor_scalar(R["s2"], R["s2"], -1.0, 1.0,
                                            ALU.mult, ALU.add),
                    lambda: v.tensor_copy(k.rw2c[:, t:t + 1], R["s2"]),
                ]
            ilv(v, tiles, {t: ops(t) for t in tiles}, lastm,
                xwait=lambda t: v.wait_ge(k.sA.h, mEXP[t]))
            for t in tiles:
                mV2[t] = lastm[t]

        sec(k, "vector", dvB)

        def cps(t):
            ps = k.psMED1 if t % 2 == 0 else k.psMED2
            c0 = 128 * ((t % 4) // 2)
            return ps[0:E, c0:c0 + 1]

        def peA(t_, tiles=tiles):
            for t in tiles:
                t_.wait_ge(k.sV.h, mV1[t])
                if t >= 4:
                    t_.wait_ge(k.sV.h, k.m_cnt_v[t - 4])
                mm = t_.matmul(cps(t),
                               k.selbuf[:, t * E:(t + 1) * E], k.ones_f[:, 0:1],
                               start=True, stop=True)
                k.sP.inc(mm)
                mPC[t] = k.sP.n

        sec(k, "tensor", peA)

        def dvc(v, tiles=tiles):
            for t in tiles:
                v.wait_ge(k.sP.h, mPC[t])
                k.sV.inc(v.tensor_copy(k.cnt_sb[0:E, t:t + 1], cps(t)))
                k.m_cnt_v[t] = k.sV.n

        sec(k, "vector", dvc)

    m_c1 = k.sV.n

    def pe4(t_):
        t_.wait_ge(k.sV.h, m_c1)
        mm = t_.matmul(k.psMED1[0:NT, 0:E], k.cnt_sb[0:E, 0:NT],
                       k.ident_f[0:E, 0:E], start=True, stop=True)
        k.sP.inc(mm)

    sec(k, "tensor", pe4)
    m_c2 = k.sP.n

    def dv11(v):
        v.wait_ge(k.sP.h, m_c2)
        k.sV.inc(v.tensor_copy(k.cntT_sb[0:NT, 0:E], k.psMED1[0:NT, 0:E]))

    sec(k, "vector", dv11)
    m_c3 = k.sV.n

    def pe5(t_):
        t_.wait_ge(k.sV.h, m_c3)
        mm = t_.matmul(k.psMED1[0:E, 0:1], k.cntT_sb[0:NT, 0:E],
                       k.ones_f[0:NT, 0:1], start=True, stop=True)
        k.sP.inc(mm)

    sec(k, "tensor", pe5)
    m_c4 = k.sP.n

    def dv12(v):
        v.wait_ge(k.sP.h, m_c4)
        k.sV.inc(v.tensor_copy(k.colA[0:E, :], k.psMED1[0:E, 0:1]))

    sec(k, "vector", dv12)
    m_c5 = k.sV.n

    def pe6(t_):
        t_.wait_ge(k.sV.h, m_c5)
        mm = t_.matmul(k.psMED1[0:E, 0:1], k.tri_f[0:E, 0:E], k.colA[0:E, :],
                       start=True, stop=True)
        k.sP.inc(mm)
        mm = t_.matmul(k.psMED2[0:1, 0:E], k.colA[0:E, :], k.ident_f[0:E, 0:E],
                       start=True, stop=True)
        k.sP.inc(mm)

    sec(k, "tensor", pe6)
    m_c6 = k.sP.n

    def dv13(v):
        v.wait_ge(k.sP.h, m_c6)
        k.sV.inc(v.tensor_copy(k.colB[0:E, :], k.psMED1[0:E, 0:1]))
        v.wait_ge(k.sV.h, k.sV.n)
        k.sV.inc(v.tensor_copy(k.tw1[0:1, 0:E], k.psMED2[0:1, 0:E]))

    sec(k, "vector", dv13)
    m_c7 = k.sV.n

    def sy3(s):
        s.wait_ge(k.sV.h, m_c7)
        k.m_cin = k.stS.batch(s, lambda e, ss: [
            dma(e, ss, out=k.cnt_in_d[:, :], in_=k.tw1[0:1, 0:E])])

    sec(k, "sync", sy3)

    def poolcc(gp):
        gp.wait_ge(k.sY.h, k.m_cin)
        cc = gp.collective_compute(
            "AllGather", ALU.bypass,
            replica_groups=[list(range(g.NC))],
            ins=[k.cnt_in_d.ap().opt()], outs=[k.cnt_all_d.ap().opt()])
        k.ccCnt.inc(cc)
        gp.wait_ge(k.ccCnt.h, 1)
        k.m_acnt = k.stG.batch(gp, lambda e, ss: [
            dma(e, ss, out=k.acnt_sb[:, :], in_=k.cnt_all_d[:, :]),
            dma(e, ss, out=k.colD[:, :], in_=k.ltcol_d[:, :])])

    sec(k, "gpsimd", poolcc)
    m_acnt = None

    def pe7(t_):
        t_.wait_ge(k.sV.h, m_c7)
        mm = t_.matmul(k.psMED1[0:1, 0:E], k.colB[0:E, :], k.ident_f[0:E, 0:E],
                       start=True, stop=True)
        k.sP.inc(mm)

    sec(k, "tensor", pe7)
    m_c8 = k.sP.n

    def dv14(v):
        v.wait_ge(k.sP.h, m_c8)
        k.sV.inc(v.tensor_copy(k.tw1[0:1, 64:64 + E], k.psMED1[0:1, 0:E]))

    sec(k, "vector", dv14)
    m_c9 = k.sV.n

    def pe8(t_):
        t_.wait_ge(k.sV.h, m_c9)
        mm = t_.matmul(k.psMED1[:, 0:E], k.ones_f[0:1, 0:128],
                       k.tw1[0:1, 64:64 + E], start=True, stop=True)
        k.sP.inc(mm)

    sec(k, "tensor", pe8)
    m_c10 = k.sP.n

    def dv15(v):
        v.wait_ge(k.sP.h, m_c10)
        k.sV.inc(v.tensor_copy(k.brow[:, 24:24 + E], k.psMED1[:, 0:E]))

    sec(k, "vector", dv15)
    m_slocb = k.sV.n

    k.m_rank_v = {}
    for t in range(NT):
        def pe9(t_, t=t):
            if t == 0:
                t_.wait_ge(k.sV.h, m_slocb)
            if t >= 1:
                t_.wait_ge(k.sV.h, k.m_rank_v[t - 1])
            mm = t_.matmul(k.psMED2[:, 0:E], k.tri_f[:, :],
                           k.selbuf[:, t * E:(t + 1) * E],
                           start=True, stop=(t == 0))
            if t > 0:
                mm = t_.matmul(k.psMED2[:, 0:E], k.ones_f[0:t, 0:128],
                               k.cntT_sb[0:t, 0:E], start=False, stop=True)
            k.sP.inc(mm)

        sec(k, "tensor", pe9)
        m_rk = k.sP.n

        def dv16(v, t=t):
            v.wait_ge(k.sP.h, m_rk)
            k.sV.inc(v.tensor_copy(k.rankbuf[:, t * E:(t + 1) * E], k.psMED2[:, 0:E]))

        sec(k, "vector", dv16)
        k.m_rank_v[t] = k.sV.n

    def pool2(gp):
        gp.wait_ge(k.sG.h, k.m_acnt)
        k.sG.inc(gp.iota(k.colC[0:g.NC, :], [[1, 1]], base=0, channel_multiplier=1,
                         allow_small_or_imprecise_dtypes=True))

    sec(k, "gpsimd", pool2)
    m_iotc = k.sG.n

    def pe10(t_):
        t_.wait_ge(k.sG.h, k.m_acnt)
        t_.wait_ge(k.sV.h, k.m_rank_v[NT - 1])
        mm = t_.matmul(k.psMED1[0:g.NC, 0:E], k.ones_f[0:1, 0:g.NC],
                       k.brow[0:1, 0:E], start=True, stop=True)
        k.sP.inc(mm)
        mm = t_.matmul(k.psMED2[0:g.NC, 0:E], k.ones_f[0:1, 0:g.NC],
                       k.brow[0:1, E:2 * E], start=True, stop=True)
        k.sP.inc(mm)
        mm = t_.matmul(k.psMED1[0:1, 256:256 + E], k.colD[0:g.NC, :],
                       k.acnt_sb[:, :], start=True, stop=True)
        k.sP.inc(mm)

    sec(k, "tensor", pe10)
    m_bc1 = k.sP.n

    def dv17(v):
        def step(inst):
            k.sV.inc(inst)
            v.wait_ge(k.sV.h, k.sV.n)
        v.wait_ge(k.sP.h, m_bc1)
        step(v.tensor_mul(k.brow[0:g.NC, 32:32 + E], k.acnt_sb[:, :], k.psMED1[0:g.NC, 0:E]))
        step(v.reduce_sum(k.colE[0:g.NC, :], k.brow[0:g.NC, 32:32 + E], mybir.AxisListType.X))
        step(v.tensor_mul(k.brow[0:g.NC, 32:32 + E], k.acnt_sb[:, :], k.psMED2[0:g.NC, 0:E]))
        step(v.tensor_copy(k.colB[0:g.NC, :], k.colE[0:g.NC, :]))
        step(v.reduce_sum(k.colA[0:g.NC, :], k.brow[0:g.NC, 32:32 + E], mybir.AxisListType.X))
        step(v.tensor_copy(k.tw1[0:1, 16:16 + E], k.psMED1[0:1, 256:256 + E]))

    sec(k, "vector", dv17)
    m_bc2 = k.sV.n

    def pe11(t_):
        t_.wait_ge(k.sV.h, m_bc2)
        mm = t_.matmul(k.psMED2[0:g.NC, 0:1], k.tri_f[0:g.NC, 0:g.NC], k.colE[0:g.NC, :],
                       start=True, stop=True)
        k.sP.inc(mm)
        mm = t_.matmul(k.psMED1[:, 256:256 + E], k.ones_f[0:1, 0:128],
                       k.tw1[0:1, 16:16 + E], start=True, stop=True)
        k.sP.inc(mm)

    sec(k, "tensor", pe11)
    m_bc3 = k.sP.n

    def dv18(v):
        def step(inst):
            k.sV.inc(inst)
            v.wait_ge(k.sV.h, k.sV.n)
        v.wait_ge(k.sP.h, m_bc3)
        step(v.tensor_copy(k.colD[0:g.NC, :], k.psMED2[0:g.NC, 0:1]))
        step(v.tensor_copy(k.brow[:, 0:E], k.psMED1[:, 256:256 + E]))
        v.wait_ge(k.sG.h, m_iotc)
        step(v.tensor_scalar(k.colC[0:g.NC, :], k.colC[0:g.NC, :], float(2 * g.TL),
                             None, ALU.mult))
        step(v.tensor_add(k.colC[0:g.NC, :], k.colC[0:g.NC, :], k.colA[0:g.NC, :]))
        step(v.tensor_sub(k.colC[0:g.NC, :], k.colC[0:g.NC, :], k.colD[0:g.NC, :]))
        step(v.tensor_add(k.colE[0:g.NC, :], k.colD[0:g.NC, :], k.colB[0:g.NC, :]))

    sec(k, "vector", dv18)
    m_bc4 = k.sV.n

    def pe12(t_):
        t_.wait_ge(k.sV.h, m_bc4)
        mm = t_.matmul(k.psMED2[0:1, 0:g.NC], k.colC[0:g.NC, :], k.ident_f[0:g.NC, 0:g.NC],
                       start=True, stop=True)
        k.sP.inc(mm)
        mm = t_.matmul(k.psMED1[0:1, 256:256 + g.NC], k.colE[0:g.NC, :],
                       k.ident_f[0:g.NC, 0:g.NC], start=True, stop=True)
        k.sP.inc(mm)

    sec(k, "tensor", pe12)
    m_bc5 = k.sP.n

    def dv19(v):
        def step(inst):
            k.sV.inc(inst)
            v.wait_ge(k.sV.h, k.sV.n)
        v.wait_ge(k.sP.h, m_bc5)
        step(v.tensor_copy(k.tw1[0:1, 32:32 + g.NC], k.psMED2[0:1, 0:g.NC]))
        step(v.tensor_copy(k.tw1[0:1, 48:48 + g.NC], k.psMED1[0:1, 256:256 + g.NC]))

    sec(k, "vector", dv19)
    m_bc6 = k.sV.n

    def pe13(t_):
        t_.wait_ge(k.sV.h, m_bc6)
        mm = t_.matmul(k.psMED2[:, 0:g.NC], k.ones_f[0:1, 0:128],
                       k.tw1[0:1, 32:32 + g.NC], start=True, stop=True)
        k.sP.inc(mm)
        mm = t_.matmul(k.psMED1[:, 256:256 + g.NC], k.ones_f[0:1, 0:128],
                       k.tw1[0:1, 48:48 + g.NC], start=True, stop=True)
        k.sP.inc(mm)

    sec(k, "tensor", pe13)
    m_bc7 = k.sP.n

    def dv20(v):
        def step(inst):
            k.sV.inc(inst)
            v.wait_ge(k.sV.h, k.sV.n)
        v.wait_ge(k.sP.h, m_bc7)
        step(v.tensor_copy(k.brow[:, 8:8 + g.NC], k.psMED2[:, 0:g.NC]))
        step(v.tensor_copy(k.brow[:, 16:16 + g.NC], k.psMED1[:, 256:256 + g.NC]))
        step(v.tensor_sub(k.brow[:, 40:40 + g.NC - 1], k.brow[:, 9:9 + g.NC - 1],
                          k.brow[:, 8:8 + g.NC - 1]))

    sec(k, "vector", dv20)
    m_bcast = k.sV.n

    for base in range(0, NT, 4):
        tiles = list(range(base, min(base + 4, NT)))

        def dv21(v, tiles=tiles, base=base):
            if base == 0:
                v.wait_ge(k.sV.h, m_bcast)
            lastm = {}

            def ops(t):
                R = RS(t)
                out = []
                for ic, rwc, sx, gx, rv in [
                        (k.i1c, k.rw1c, k.sidx1, k.gidx1, k.rwv1),
                        (k.i2c, k.rw2c, k.sidx2, k.gidx2, k.rwv2)]:
                    out += [
                        lambda ic=ic: v.tensor_scalar(
                            R["a"], k.iota8[:, :],
                            ic[:, t:t + 1], None, ALU.is_equal),
                        lambda: v.tensor_mul(R["b"], R["a"],
                                             k.rankbuf[:, t * E:(t + 1) * E]),
                        lambda: v.reduce_sum(R["s0"], R["b"], mybir.AxisListType.X),
                        lambda: v.tensor_mul(R["b"], R["a"], k.brow[:, 24:24 + E]),
                        lambda: v.reduce_sum(R["s1"], R["b"], mybir.AxisListType.X),
                        lambda: v.tensor_add(R["s1"], R["s1"], R["s0"]),
                        lambda sx=sx: v.tensor_copy(sx[:, t:t + 1], R["s1"]),
                        lambda: v.tensor_mul(R["b"], R["a"], k.brow[:, 0:E]),
                        lambda: v.reduce_sum(R["s2"], R["b"], mybir.AxisListType.X),
                        lambda: v.tensor_add(R["s2"], R["s2"], R["s0"]),
                        lambda: v.tensor_scalar(R["s3"], R["s2"], float(g.CAP),
                                                None, ALU.is_lt),
                        lambda rwc=rwc: v.tensor_mul(R["s4"], rwc[:, t:t + 1],
                                                     R["s3"]),
                        lambda rv=rv: v.tensor_copy(rv[:, t:t + 1], R["s4"]),
                        lambda gx=gx: v.tensor_copy(gx[:, t:t + 1], R["s2"]),
                    ]
                return out
            ilv(v, tiles, {t: ops(t) for t in tiles}, lastm)

        sec(k, "vector", dv21)

    def pool3(gp):
        k.sG.inc(gp.iota(k.ricmp[:, :], [[128, g.NCT]], base=0, channel_multiplier=1,
                         allow_small_or_imprecise_dtypes=True))

    sec(k, "gpsimd", pool3)
    m_ric = k.sG.n

    def dv22(v):
        def step(inst):
            k.sV.inc(inst)
            v.wait_ge(k.sV.h, k.sV.n)
        v.wait_ge(k.sG.h, m_ric)
        v.wait_ge(k.sV.h, m_bcast)
        step(v.tensor_scalar(k.icmpf[:, :], k.ricmp[:, :], k.brow[:, 8:9], None, ALU.add))
        for c in range(1, g.NC):
            step(v.tensor_scalar(k.icmpw[:, :], k.ricmp[:, :],
                                 k.brow[:, 16 + c - 1:16 + c], None, ALU.is_ge))
            step(v.scalar_tensor_tensor(k.icmpf[:, :], k.icmpw[:, :],
                                        k.brow[:, 40 + c - 1:40 + c], k.icmpf[:, :],
                                        ALU.mult, ALU.add))
        step(v.tensor_scalar(k.icmpf[:, :], k.icmpf[:, :], float(g.NC * 2 * g.TL),
                             None, ALU.min))
        step(v.tensor_scalar(k.icmpf[:, :], k.icmpf[:, :], 0.0, None, ALU.max))
        step(v.tensor_copy(k.icmp[:, :], k.icmpf[:, :]))

    sec(k, "vector", dv22)
    k.m_routing = marks(k)



def emit_ag_xmid(k, b):
    g = k.g
    S = g.S

    def pool(gp, b=b):
        gp.wait_ge(k.sY.h, k.m_seq_end["sY"])
        cc = gp.collective_compute(
            "AllGather", ALU.bypass,
            replica_groups=[list(range(g.NC))],
            ins=[k.xmid_b_d.ap()[b * S:(b + 1) * S, :].opt()],
            outs=[k.xma_d.ap()[b * g.NC * S:(b + 1) * g.NC * S, :].opt()])
        k.ccXg.inc(cc)

    sec(k, "gpsimd", pool)


def emit_scatter(k):
    nc, g = k.nc, k.g
    S, NBS = g.S, g.NBS

    for base in range(0, g.NT, 4):
        tiles = list(range(base, min(base + 4, g.NT)))

        def pool(gp, tiles=tiles, base=base):
            if base == 0:
                wait_marks(k, gp, dict(sV=k.m_routing["sV"]))
            for t in tiles:
                b = t // NBS
                const = float(b * g.NC * S + (t % NBS) * 128)
                k.sG.inc(gp.tensor_scalar(k.tidcol[:, t % 8:t % 8 + 1],
                                          k.cbcol[:, :], const, None, ALU.add))
            gp.wait_ge(k.sG.h, k.sG.n)

            def em(e, ss):
                for t in tiles:
                    for sx in (k.sidx1, k.sidx2):
                        idma(e, ss,
                             out=k.tid_d[:, :],
                             out_offset=bass.IndirectOffsetOnAxis(
                                 ap=sx[:, t:t + 1], axis=0),
                             in_=k.tidcol[:, t % 8:t % 8 + 1],
                             in_offset=None)
            k.stG.batch(gp, em)

        sec(k, "gpsimd", pool)
    k.m_scatter_done = k.sG.n


def emit_ag_xg(k):
    g = k.g

    def pool(gp):
        k.rXgB = gp.alloc_register("rXgB")
        gp.reg_mov(k.rXgB, g.NC * 2 * g.TL - 1)
        k.rXmB = gp.alloc_register("rXmB")
        gp.reg_mov(k.rXmB, g.NC * g.TL - 1)
        k.rEoB = gp.alloc_register("rEoB")
        gp.reg_mov(k.rEoB, g.NC * g.CAP - 1)
        k.sG.inc(gp.memset(k.tidv[:, :], 0.0))
        cc = gp.collective_compute(
            "AllGather", ALU.bypass,
            replica_groups=[list(range(g.NC))],
            ins=[k.tid_d.ap().opt()], outs=[k.tidall_d.ap().opt()])
        k.ccXg.inc(cc)

    sec(k, "gpsimd", pool)


def emit_gidx(k, tiles=None):
    g = k.g
    NT, E = g.NT, g.E
    big = float(g.NC * g.CAP)
    if tiles is None:
        tiles = range(NT)

    for t in tiles:
        def gpass(gp, t=t):
            def step(inst):
                k.sG.inc(inst)
                gp.wait_ge(k.sG.h, k.sG.n)
            if t == 0:
                gp.wait_ge(k.sV.h, k.m_routing["sV"])
            for ic, gx in [(k.i1c, k.gidx1), (k.i2c, k.gidx2)]:
                step(gp.tensor_copy(k.st2[:, :], gx[:, t:t + 1]))
                step(gp.tensor_scalar(k.st3[:, :], k.st2[:, :], float(g.CAP),
                                      None, ALU.is_lt))
                step(gp.tensor_scalar(k.st0[:, :], k.st2[:, :], float(g.ECH),
                                      None, ALU.is_ge))
                for jj in range(2, g.NECH):
                    step(gp.tensor_scalar(k.st1[:, :], k.st2[:, :],
                                          float(jj * g.ECH), None, ALU.is_ge))
                    step(gp.tensor_add(k.st0[:, :], k.st0[:, :], k.st1[:, :]))
                step(gp.tensor_scalar(k.st0[:, :], k.st0[:, :],
                                      float((g.NC - 1) * g.ECH), None, ALU.mult))
                step(gp.tensor_add(k.st2[:, :], k.st2[:, :], k.st0[:, :]))
                step(gp.tensor_scalar(k.st0[:, :], ic[:, t:t + 1], float(g.ECH),
                                      None, ALU.mult))
                step(gp.tensor_add(k.st2[:, :], k.st2[:, :], k.st0[:, :]))
                step(gp.tensor_scalar(k.st0[:, :], k.st3[:, :], -big, big,
                                      ALU.mult, ALU.add))
                step(gp.tensor_add(k.st2[:, :], k.st2[:, :], k.st0[:, :]))
                step(gp.tensor_scalar(k.st2[:, :], k.st2[:, :], big, None, ALU.min))
                step(gp.tensor_copy(gx[:, t:t + 1], k.st2[:, :]))

        sec(k, "gpsimd", gpass)
    k.m_gidx = k.sG.n



def emit_glu(k, which):
    nc, g = k.nc, k.g
    D, FC = g.D, g.FC
    if which == "shared":
        rows, GS, HID, IQL = g.TL, g.GS_S, g.IS, g.IQ_S
        wg_d, wu_d, wd_d, out_d = k.sg_b_d, k.su_b_d, k.sd_b_d, k.shr_d
    else:
        rows, GS, HID, IQL = g.CAP, g.GS_E, g.IR, g.IQ_E
        wg_d, wu_d, wd_d, out_d = k.rg_b_d, k.ru_b_d, k.rd_b_d, k.eout_f_d
    ngroup = rows // GS
    nisub = IQL // 128
    nq = HID // IQL
    jw = widths(GS, 512)
    nsb = GS // 128
    dws = widths(D, 512)

    gi = dict(idx=0, pg={}, pu={}, am={}, vm={})
    k.m_glu_wused = {}
    k.m_glu_dn = {}
    k.m_glu_xpar = {}
    if not hasattr(k, "m_glu_last_st"):
        k.m_glu_last_st = 0
    m_prev_glu = getattr(k, "m_glu_done", None)

    def slab_sl(grp, col0, w):
        return k.xslab[:, col0:col0 + w]

    for grp in range(ngroup):
        r0 = grp * GS
        if which == "shared":
            def syx(s, grp=grp, r0=r0):
                if grp == 0:
                    assert not k.precast_jobs
                    s.wait_ge(k.sPC.h, k.sPC.n)
                else:
                    s.wait_ge(k.sP.h, k.m_glu_xused)
                def em(e, ss):
                    for c in range(FC):
                        dma(e, ss, out=k.xslab[:, c * GS:(c + 1) * GS],
                            in_=k.xmT_d[c * 128:(c + 1) * 128, r0:r0 + GS])
                k.m_glu_x = k.stS.batch(s, em)

            sec(k, "sync", syx)
        else:
            m_slab_free = (m_prev_glu["sP"] if grp == 0
                           else k.m_glu_xused)
            emit_expert_slab(k, grp, r0, GS, slab_sl, m_slab_free)
        m_x = marks(k)

        for q4 in range(nq):
            h0 = q4 * IQL

            def syd(s, grp=grp, q4=q4, h0=h0):
                if getattr(k, "m_glu_wdused", 0):
                    s.wait_ge(k.sP.h, k.m_glu_wdused)
                def em(e, ss):
                    for isub in range(nisub):
                        dma(e, ss, out=k.wd_sl[:, isub * D:(isub + 1) * D],
                            in_=wd_d[h0 + isub * 128: h0 + isub * 128 + 128, :])
                k.m_glu_wd = k.stS.batch(s, em)

            sec(k, "sync", syd)

            for isub in range(nisub):
                half = isub % 2

                def syw(s, grp=grp, q4=q4, isub=isub, h0=h0, half=half):
                    prev = k.m_glu_wused.get(half)
                    if prev:
                        s.wait_ge(k.sP.h, prev)
                    elif m_prev_glu:
                        s.wait_ge(k.sP.h, m_prev_glu["sP"])
                    def em(e, ss):
                        for w_d, dst in [(wg_d, k.wg_sl), (wu_d, k.wu_sl)]:
                            dma(e, ss,
                                out=dst[:, half * FC * 128:(half + 1) * FC * 128],
                                in_=w_d[h0 + isub * 128: h0 + isub * 128 + 128, :])
                    k.m_glu_w = k.stS.batch(s, em)

                sec(k, "sync", syw)

                def pe(t_, grp=grp, q4=q4, isub=isub, half=half, m_x=m_x):
                    t_.wait_ge(k.sY.h, k.m_glu_w)
                    if isub == 0 and q4 == 0:
                        if which == "shared":
                            t_.wait_ge(k.sY.h, m_x["sY"])
                            if grp == 0:
                                t_.wait_ge(k.sV.h, k.m_seq_end["sV"])
                                t_.wait_ge(k.sA.h, k.m_seq_end["sA"])
                        else:
                            wait_marks(k, t_, dict(sV=m_x["sV"], sY=m_x["sY"],
                                                   sG=m_x["sG"]))
                            if grp == 0 and m_prev_glu:
                                t_.wait_ge(k.sA.h, m_prev_glu["sA"])
                    for j in range(len(jw)):
                        gi["am"][gi["idx"] + j] = k.sA.n + j + 1
                        gi["vm"][gi["idx"] + j] = k.sV.n + j + 1
                    for ji, (j0, w) in enumerate(jw):
                        idx = gi["idx"]
                        bank = k.psBIG1 if idx % 2 == 0 else k.psBIG2
                        if idx >= 2:
                            t_.wait_ge(k.sA.h, gi["am"][idx - 2])
                            t_.wait_ge(k.sV.h, gi["vm"][idx - 2])
                        for c in range(FC):
                            mm = t_.matmul(bank[:, 0:w],
                                           k.wg_sl[:, half * FC * 128 + c * 128:
                                                   half * FC * 128 + (c + 1) * 128],
                                           slab_sl(grp, c * GS + j0, w),
                                           start=(c == 0), stop=(c == FC - 1))
                        k.sP.inc(mm)
                        gi["pg"][idx] = k.sP.n
                        for c in range(FC):
                            mm = t_.matmul(bank[:, 512:512 + w],
                                           k.wu_sl[:, half * FC * 128 + c * 128:
                                                   half * FC * 128 + (c + 1) * 128],
                                           slab_sl(grp, c * GS + j0, w),
                                           start=(c == 0), stop=(c == FC - 1))
                        k.sP.inc(mm)
                        gi["pu"][idx] = k.sP.n
                        gi["idx"] += 1

                sec(k, "tensor", pe)
                k.m_glu_wused[half] = k.sP.n

                def ac(s, grp=grp, q4=q4, isub=isub):
                    n_j = len(jw)
                    base = gi["idx"] - n_j
                    for ji, (j0, w) in enumerate(jw):
                        idx = base + ji
                        bank = k.psBIG1 if idx % 2 == 0 else k.psBIG2
                        ft = k.ftmp[:, 0:512] if idx % 2 == 0 else k.ftmp[:, 512:1024]
                        s.wait_ge(k.sP.h, gi["pg"][idx])
                        if idx >= 2:
                            s.wait_ge(k.sV.h, gi["vm"][idx - 2])
                        k.sA.inc(s.activation(ft[:, 0:w], bank[:, 0:w], AF.Silu))
                        assert gi["am"][idx] == k.sA.n, (idx, gi["am"][idx], k.sA.n)

                sec(k, "scalar", ac)

                def dv(v, grp=grp, q4=q4, isub=isub):
                    n_j = len(jw)
                    base = gi["idx"] - n_j
                    for ji, (j0, w) in enumerate(jw):
                        idx = base + ji
                        bank = k.psBIG1 if idx % 2 == 0 else k.psBIG2
                        ft = k.ftmp[:, 0:512] if idx % 2 == 0 else k.ftmp[:, 512:1024]
                        v.wait_ge(k.sP.h, gi["pu"][idx])
                        v.wait_ge(k.sA.h, gi["am"][idx])
                        k.sV.inc(v.tensor_mul(bA(k, isub * GS + j0, w),
                                              ft[:, 0:w], bank[:, 512:512 + w]))
                        assert gi["vm"][idx] == k.sV.n, (idx, gi["vm"][idx], k.sV.n)

                sec(k, "vector", dv)

            m_h = marks(k)
            for sb_ in range(nsb):
                def pe2(t_, grp=grp, q4=q4, sb_=sb_, m_h=m_h):
                    if sb_ == 0:
                        t_.wait_ge(k.sY.h, k.m_glu_wd)
                        t_.wait_ge(k.sV.h, m_h["sV"])
                    if sb_ >= 1:
                        t_.wait_ge(k.sV.h, k.m_glu_dn[sb_ - 1])
                    for hi, (h0_, w) in enumerate(dws):
                        ps = k.psMED1 if hi == 0 else k.psMED2
                        for isub in range(nisub):
                            mm = t_.matmul(ps[:, 0:w],
                                           bA(k, isub * GS + sb_ * 128, 128),
                                           k.wd_sl[:, isub * D + h0_: isub * D + h0_ + w],
                                           start=(isub == 0),
                                           stop=(isub == nisub - 1))
                        k.sP.inc(mm)

                sec(k, "tensor", pe2)
                m_dn = k.sP.n

                def dv2(v, grp=grp, q4=q4, sb_=sb_, m_dn=m_dn):
                    v.wait_ge(k.sP.h, m_dn)
                    if k.m_glu_last_st:
                        v.wait_ge(k.sG.h, k.m_glu_last_st)
                    for hi, (h0_, w) in enumerate(dws):
                        ps = k.psMED1 if hi == 0 else k.psMED2
                        if hi > 0:
                            v.wait_ge(k.sV.h, k.sV.n)
                        k.sV.inc(v.tensor_copy(k.tw0[:, h0_:h0_ + w], ps[:, 0:w]))
                    k.m_glu_dn[sb_] = k.sV.n

                sec(k, "vector", dv2)
                m_ev = k.sV.n

                def pl2(gp, grp=grp, q4=q4, sb_=sb_, r0=r0, m_ev=m_ev):
                    gp.wait_ge(k.sV.h, m_ev)
                    rr = r0 + sb_ * 128
                    op = ALU.bypass if q4 == 0 else ALU.add
                    k.m_glu_last_st = k.stG.batch(gp, lambda e, ss: [
                        dma(e, ss, out=out_d[rr:rr + 128, :], in_=k.tw0[:, :],
                            accum_op=op)])

                sec(k, "gpsimd", pl2)
            k.m_glu_wdused = k.sP.n
            if which == "shared" and getattr(k, "gidx_jobs", None):
                emit_gidx(k, k.gidx_jobs[:4])
                del k.gidx_jobs[:4]
        k.m_glu_xused = k.sP.n
        if which == "expert":
            ch = grp
            m_st = k.m_glu_last_st

            def plag(gp, ch=ch, m_st=m_st):
                gp.wait_ge(k.sG.h, m_st)
                def em(e, ss):
                    for r0_, w_ in widths(g.ECH, 512):
                        rr_ = ch * g.ECH + r0_
                        dma(e, ss, out=k.eout_b_d[rr_:rr_ + w_, :],
                            in_=k.eout_f_d[rr_:rr_ + w_, :])
                k.stG.batch(gp, em)
                cc = gp.collective_compute(
                    "AllGather", ALU.bypass,
                    replica_groups=[list(range(g.NC))],
                    ins=[k.eout_b_d.ap()[ch * g.ECH:(ch + 1) * g.ECH, :].opt()],
                    outs=[k.eall_d.ap()[ch * g.NC * g.ECH:
                                        (ch + 1) * g.NC * g.ECH, :].opt()])
                k.ccEo.inc(cc)

            sec(k, "gpsimd", plag)
    k.m_glu_done = marks(k)
    if which == "shared":
        k.m_shared_done = marks(k)
    else:
        k.m_expert_done = marks(k)


def emit_expert_slab(k, grp, r0, GS, slab_sl, m_slab_free):
    nc, g = k.nc, k.g
    FC = g.FC
    ntile = GS // 128
    gths = [k.gth0, k.gth1, k.gth2, k.gth3]

    for tt in range(ntile):
        j = r0 // 128 + tt

        def pl(gp, tt=tt, j=j, grp=grp):
            if grp == 0 and tt == 0:
                gp.wait_ge(k.ccXg.h, 3)
                wait_marks(k, gp, dict(sV=k.m_routing["sV"],
                                       sP=k.m_seq_end["sP"],
                                       sY=k.m_seq_end["sY"]))
            prev = k.m_exp_tr.get(j - 4) if hasattr(k, "m_exp_tr") else None
            if prev:
                gp.wait_ge(k.sP.h, prev)
            dst = gths[j % 4]
            slot = tt % 2
            k.stG.batch(gp, lambda e, ss: [
                idma(e, ss,
                     out=k.tidv[:, slot:slot + 1],
                     out_offset=None,
                     in_=k.tidall_d[:, :],
                     in_offset=bass.IndirectOffsetOnAxis(ap=k.icmp[:, j:j + 1], axis=0),
                     bounds_check=k.rXgB,
                     oob_is_err=False)])
            k.sG.inc(gp.tensor_copy(k.tidvi[:, slot:slot + 1],
                                    k.tidv[:, slot:slot + 1]))
            gp.wait_ge(k.sG.h, k.sG.n)
            k.m_exp_g = k.stG.batch(gp, lambda e, ss: [
                idma(e, ss,
                     out=dst[:, :],
                     out_offset=None,
                     in_=k.xma_d[:, :],
                     in_offset=bass.IndirectOffsetOnAxis(ap=k.tidvi[:, slot:slot + 1],
                                                         axis=0),
                     bounds_check=k.rXmB,
                     oob_is_err=False)])

        sec(k, "gpsimd", pl)
        p0 = k.sP.n
        v0 = k.sV.n

        def pe(t_, tt=tt, j=j, p0=p0, v0=v0):
            t_.wait_ge(k.sG.h, k.m_exp_g)
            src = gths[j % 4]
            for c in range(FC):
                ps = k.psSM1 if c % 2 == 0 else k.psSM2
                if c >= 2:
                    t_.wait_ge(k.sV.h, v0 + c - 1)
                mm = t_.transpose(ps[:, :], src[:, c * 128:(c + 1) * 128], k.ident_bf[:, :])
                k.sP.inc(mm)
            k.m_exp_tr = getattr(k, "m_exp_tr", {})
            k.m_exp_tr[j] = k.sP.n

        def dv(v, tt=tt, j=j, p0=p0, GS=GS, grp=grp):
            if tt == 0:
                v.wait_ge(k.sP.h, m_slab_free)
            for c in range(FC):
                ps = k.psSM1 if c % 2 == 0 else k.psSM2
                v.wait_ge(k.sP.h, p0 + c + 1)
                k.sV.inc(v.tensor_copy(
                    slab_sl(grp, c * GS + tt * 128, 128), ps[:, :]))

        sec(k, "tensor", pe)
        sec(k, "vector", dv)



def emit_phase_c(k):
    nc, g = k.nc, k.g
    D = g.D
    NT = g.NT
    with contextlib.ExitStack() as es_c:
        csb = lambda name, shape, dt: es_c.enter_context(nc.sbuf_tensor(name, shape, dt))
        cshr = [csb("cshr%d" % j, [128, D], F32) for j in range(3)]
        cxm = [csb("cxm%d" % j, [128, D], F32) for j in range(3)]
        ce1 = [csb("ce1%d" % j, [128, D], BF16) for j in range(3)]
        ce2 = [csb("ce2%d" % j, [128, D], BF16) for j in range(3)]
        cacc = [csb("cacc%d" % j, [128, D], F32) for j in range(4)]
        cdum = [csb("cdum%d" % j, [128, D], F32) for j in range(2)]
        cst = csb("cst", [128, 16], F32)
        ml, mg, mcb, msq, msv, msx, mnm, mst = {}, {}, {}, {}, {}, {}, {}, {}

        for i in range(NT + 3):
            tl, tc, ts_, tn = i, i - 1, i - 2, i - 3
            if tl < NT:
                j3 = tl % 3

                def syl(s, t=tl, j3=j3):
                    if t == 0:
                        s.wait_ge(k.sG.h, k.m_expert_done["sG"])
                        s.wait_ge(k.sY.h, k.m_seq_end["sY"])
                    if t >= 3:
                        s.wait_ge(k.sV.h, mcb[t - 3])
                    ml[t] = k.stS.batch(s, lambda e, ss: [
                        dma(e, ss, out=cshr[j3][:, :],
                            in_=k.shr_d[t * 128:(t + 1) * 128, :]),
                        dma(e, ss, out=cxm[j3][:, :],
                            in_=k.xmid_f_d[t * 128:(t + 1) * 128, :])])

                sec(k, "sync", syl)

                def pl(gp, t=tl, j3=j3):
                    if t == 0:
                        gp.wait_ge(k.ccEo.h, k.g.NECH)
                        gp.wait_ge(k.sV.h, k.m_routing["sV"])
                    if t >= 3:
                        gp.wait_ge(k.sV.h, mcb[t - 3])

                    def em(e, ss):
                        for gx, dst in [(k.gidx1, ce1[j3]), (k.gidx2, ce2[j3])]:
                            idma(e, ss, out=dst[:, :], out_offset=None,
                                 in_=k.eall_d[:, :],
                                 in_offset=bass.IndirectOffsetOnAxis(
                                     ap=gx[:, t:t + 1], axis=0),
                                 bounds_check=k.rEoB, oob_is_err=False)
                    mg[t] = k.stG.batch(gp, em)

                sec(k, "gpsimd", pl)

            if 0 <= tc < NT:
                j3, j4 = tc % 3, tc % 4

                def dvc(v, t=tc, j3=j3, j4=j4):
                    def step(inst):
                        k.sV.inc(inst)
                        v.wait_ge(k.sV.h, k.sV.n)
                    v.wait_ge(k.sY.h, ml[t])
                    v.wait_ge(k.sG.h, mg[t])
                    if t >= 4:
                        v.wait_ge(k.sY.h, mst[t - 4])
                    step(v.tensor_add(cacc[j4][:, :], cshr[j3][:, :], cxm[j3][:, :]))
                    step(v.scalar_tensor_tensor(cacc[j4][:, :], ce1[j3][:, :],
                                                k.rwv1[:, t:t + 1], cacc[j4][:, :],
                                                ALU.mult, ALU.add))
                    step(v.scalar_tensor_tensor(cacc[j4][:, :], ce2[j3][:, :],
                                                k.rwv2[:, t:t + 1], cacc[j4][:, :],
                                                ALU.mult, ALU.add))
                    step(v.reduce_sum(cst[:, j4:j4 + 1], cacc[j4][:, :],
                                      mybir.AxisListType.X))
                    mcb[t] = k.sV.n

                sec(k, "vector", dvc)

                def acq(s, t=tc, j4=j4):
                    s.wait_ge(k.sV.h, mcb[t])
                    k.sA.inc(s.activation(cdum[t % 2][:, :], cacc[j4][:, :], AF.Square,
                                          accum_out=cst[:, 4 + j4:5 + j4]))
                    msq[t] = k.sA.n

                sec(k, "scalar", acq)

            if 0 <= ts_ < NT:
                j4 = ts_ % 4

                def dvs(v, t=ts_, j4=j4):
                    def step(inst):
                        k.sV.inc(inst)
                        v.wait_ge(k.sV.h, k.sV.n)
                    v.wait_ge(k.sA.h, msq[t])
                    step(v.tensor_scalar(cst[:, j4:j4 + 1], cst[:, j4:j4 + 1],
                                         1.0 / D, None, ALU.mult))
                    step(v.tensor_scalar(cst[:, 4 + j4:5 + j4], cst[:, 4 + j4:5 + j4],
                                         1.0 / D, None, ALU.mult))
                    step(v.tensor_mul(cst[:, 8 + j4:9 + j4], cst[:, j4:j4 + 1],
                                      cst[:, j4:j4 + 1]))
                    step(v.tensor_sub(cst[:, 4 + j4:5 + j4], cst[:, 4 + j4:5 + j4],
                                      cst[:, 8 + j4:9 + j4]))
                    msv[t] = k.sV.n

                sec(k, "vector", dvs)

                def acr(s, t=ts_, j4=j4):
                    s.wait_ge(k.sV.h, msv[t])
                    k.sA.inc(s.activation(cst[:, 8 + j4:9 + j4], cst[:, 4 + j4:5 + j4],
                                          AF.Sqrt, bias=k.epsln_c[:, :]))
                    msx[t] = k.sA.n

                sec(k, "scalar", acr)

            if 0 <= tn < NT:
                j4 = tn % 4

                def dvn(v, t=tn, j4=j4):
                    def step(inst):
                        k.sV.inc(inst)
                        v.wait_ge(k.sV.h, k.sV.n)
                    v.wait_ge(k.sA.h, msx[t])
                    step(v.reciprocal(cst[:, 8 + j4:9 + j4], cst[:, 8 + j4:9 + j4]))
                    step(v.tensor_scalar(cst[:, 12 + j4:13 + j4], cst[:, j4:j4 + 1],
                                         -1.0, None, ALU.mult))
                    step(v.tensor_scalar(cacc[j4][:, :], cacc[j4][:, :],
                                         cst[:, 12 + j4:13 + j4],
                                         cst[:, 8 + j4:9 + j4],
                                         ALU.add, ALU.mult))
                    mnm[t] = k.sV.n

                sec(k, "vector", dvn)

                def sy2(s, t=tn, j4=j4):
                    s.wait_ge(k.sV.h, mnm[t])
                    mst[t] = k.stS.batch(s, lambda e, ss: [
                        dma(e, ss, out=k.out_d[t * 128:(t + 1) * 128, :],
                            in_=cacc[j4][:, :])])

                sec(k, "sync", sy2)
    k.m_out_done = k.sY.n


def emit_final(k):
    def sy(s):
        s.wait_ge(k.sY.h, k.m_out_done)
        s.wait_ge(k.sG.h, k.sG.n)

    sec(k, "sync", sy)



def _permw(w):
    K, N = w.shape
    return np.ascontiguousarray(
        w.reshape(K // 128, 128, N // 128, 128).transpose(2, 1, 0, 3)
        .reshape(N, K))


def _shard_inputs(g, inputs):
    x = np.ascontiguousarray(inputs["L0_attn_x"], dtype=np.float32)
    D = g.D
    base = {
        "wq": _permw(np.asarray(inputs["L0_attn_Wq"], dtype=np.float32)),
        "wk": _permw(np.asarray(inputs["L0_attn_Wk"], dtype=np.float32)),
        "wv": _permw(np.asarray(inputs["L0_attn_Wv"], dtype=np.float32)),
        "wo": np.ascontiguousarray(inputs["L0_attn_Wo"], dtype=np.float32),
        "g1": np.ascontiguousarray(inputs["L0_attn_G1"], dtype=np.float32),
        "g2": np.ascontiguousarray(inputs["L0_attn_G2"], dtype=np.float32),
        "gamma": np.ascontiguousarray(inputs["L0_attn_gamma"], dtype=np.float32).reshape(1, D),
        "beta": np.ascontiguousarray(inputs["L0_attn_beta"], dtype=np.float32).reshape(1, D),
        "theta": np.ascontiguousarray(inputs["L0_attn_theta"], dtype=np.float32).reshape(g.H * g.HD, 1),
        "rw": np.ascontiguousarray(inputs["L0_ffn_router_w"], dtype=np.float32),
        "sg": _permw(np.asarray(inputs["L0_ffn_sg"], dtype=np.float32)),
        "su": _permw(np.asarray(inputs["L0_ffn_su"], dtype=np.float32)),
        "sd": np.ascontiguousarray(inputs["L0_ffn_sd"], dtype=np.float32),
    }
    rg, ru, rd = inputs["L0_ffn_rg"], inputs["L0_ffn_ru"], inputs["L0_ffn_rd"]
    in_maps = []
    for c in range(g.NC):
        m = dict(base)
        m["x"] = np.ascontiguousarray(x[c * g.BL:(c + 1) * g.BL].reshape(g.TL, D))
        m["rg"] = _permw(np.asarray(rg[c], dtype=np.float32))
        m["ru"] = _permw(np.asarray(ru[c], dtype=np.float32))
        m["rd"] = np.ascontiguousarray(rd[c], dtype=np.float32)
        m["ltcol"] = (np.arange(g.NC) < c).astype(np.float32).reshape(g.NC, 1)
        m["myrow"] = (np.arange(g.E) == c).astype(np.float32).reshape(1, g.E)
        m["ltrow"] = (np.arange(g.E) < c).astype(np.float32).reshape(1, g.E)
        m["cb"] = (c * g.S + np.arange(128)).astype(np.float32).reshape(128, 1)
        in_maps.append(m)
    return in_maps


def kernel(**inputs):
    g = FULL
    nc = build(g)
    in_maps = _shard_inputs(g, inputs)
    res = run_bass_kernel_spmd(nc, in_maps, core_ids=list(range(g.NC)))
    out = np.concatenate([res.results[c]["out"] for c in range(g.NC)], axis=0)
    return out.reshape(g.B, g.S, g.D).astype(np.float32)
